# version 01
# baseline (speedup 1.0000x reference)
import contextlib
import numpy as np
import ml_dtypes
import concourse.bass as bass
import concourse.mybir as mybir
from concourse.bass_utils import run_bass_kernel_spmd

F32 = mybir.dt.float32
BF16 = mybir.dt.bfloat16
AF = mybir.ActivationFunctionType
ALU = mybir.AluOpType
AX = mybir.AxisListType
BF = ml_dtypes.bfloat16

NCORE = 8
T = 2048
TT = T // 128
DM = 4096
DCH = DM // 128
DIN = 8192
CONVD = 10240
PROJ = 18560
NH = 128
HD = 64
G = 8
NST = 128
CS = 128
NC = T // CS
EPS_LN = 1e-5
EPS_GN = 1e-5
NEG_BIG = -1e30



class Prog:

    _ENG = {"ve": "vector", "se": "scalar", "te": "tensor", "gp": "gpsimd", "sy": "sync"}

    def __init__(self):
        self.nc = bass.Bass("TRN2", target_bir_lowering=False)
        self.stack = contextlib.ExitStack()
        self.sems = {}
        self.cnt = {}
        self.blk = None

    def sem(self, name):
        if name not in self.sems:
            self.sems[name] = self.stack.enter_context(self.nc.semaphore(name))
            self.cnt[name] = 0
        return self.sems[name]

    def sb(self, name, shape, dtype):
        return self.stack.enter_context(self.nc.sbuf_tensor(name, shape, dtype))

    @contextlib.contextmanager
    def scoped(self):
        with contextlib.ExitStack() as es:
            class S:
                def sb(_s, name, shape, dtype):
                    return es.enter_context(self.nc.sbuf_tensor(name, shape, dtype))
                def ps(_s, name, shape, dtype=F32):
                    return es.enter_context(self.nc.psum_tensor(name, shape, dtype))
            yield S()

    def ps(self, name, shape, dtype=F32):
        return self.stack.enter_context(self.nc.psum_tensor(name, shape, dtype))

    def din(self, name, shape, dtype=F32):
        return self.nc.dram_tensor(name, list(shape), dtype, kind="ExternalInput").ap()

    def dout(self, name, shape, dtype=F32):
        return self.nc.dram_tensor(name, list(shape), dtype, kind="ExternalOutput").ap()

    def _section(self, key, fn):
        if key[0] == "v":
            eng = "vector"
        elif key.startswith("se"):
            eng = "scalar"
        elif key.startswith("te"):
            eng = "tensor"
        elif key.startswith("gp"):
            eng = "gpsimd"
        else:
            eng = "sync"
        getattr(self.blk, eng)(fn)

    def _waits(self, e, waits):
        for (s, v) in waits:
            e.wait_ge(self.sem(s) if isinstance(s, str) else s, v)

    def run(self, key, fn, waits=(), self_sync=True, inc=1):
        sem = self.sem(key)
        cur = self.cnt[key]

        def body(e):
            self._waits(e, waits)
            if self_sync and cur:
                e.wait_ge(sem, cur)
            fn(e).then_inc(sem, inc)

        self._section(key, body)
        self.cnt[key] += inc
        return (key, self.cnt[key])

    def runs(self, key, fns, waits=(), self_sync=True):
        sem = self.sem(key)
        start = self.cnt[key]

        def body(e):
            self._waits(e, waits)
            c = start
            for fn in fns:
                if self_sync and c:
                    e.wait_ge(sem, c)
                fn(e).then_inc(sem, 1)
                c += 1

        self._section(key, body)
        self.cnt[key] += len(fns)
        return (key, self.cnt[key])

    def run_group(self, key, fns, waits=(), inc=1):
        sem = self.sem(key)

        def body(e):
            self._waits(e, waits)
            for fn in fns[:-1]:
                fn(e)
            fns[-1](e).then_inc(sem, inc)

        self._section(key, body)
        self.cnt[key] += inc
        return (key, self.cnt[key])

    def dma(self, qkey, out, in_, waits=(), sem_name=None):
        sname = sem_name or (qkey + "q")
        sem = self.sem(sname)

        def body(e):
            self._waits(e, waits)
            e.dma_start(out=out, in_=in_).then_inc(sem, 16)

        self._section(qkey, body)
        self.cnt[sname] += 16
        return (sname, self.cnt[sname])

    def wait_on(self, key, waits):
        def body(e):
            self._waits(e, waits)
        self._section(key, body)

    def phase_barrier(self):
        snap = [(k, v) for (k, v) in self.cnt.items() if v > 0]
        for key in ("ve", "se", "te", "gp", "sy"):
            self.wait_on(key, snap)

    def finish(self, waits):
        self.wait_on("sy", waits)

    def build(self):
        self.blk = self.stack.enter_context(self.nc.Block())
        return self

    def close(self):
        self.stack.close()



def _bf(x):
    return np.ascontiguousarray(x).astype(BF)


def lhsT_tiles(wT):
    a = np.ascontiguousarray(wT, dtype=np.float32)
    din, dout = a.shape
    a = a.reshape(din // 128, 128, dout // 128, 128)
    return np.ascontiguousarray(a.transpose(2, 0, 1, 3))



def build_ln0():
    p = Prog()
    x = p.din("x", [T, DM], F32)
    identb = p.din("identb", [128, 128], BF16)
    xnfm = p.dout("xnfm", [DCH, 128, T], BF16)
    lvtok = p.dout("lvtok", [128, TT], F32)
    lvpart = p.dout("lvpart", [1, 1], F32)

    xt = [p.sb(f"xt{j}", [128, DM], F32) for j in range(2)]
    xnb = p.sb("xnb", [128, DM], BF16)
    bst = p.sb("bst", [128, 6 * (DM // 512)], F32)
    mcols = p.sb("mcols", [128, TT], F32)
    vcols = p.sb("vcols", [128, TT], F32)
    rcols = p.sb("rcols", [128, TT], F32)
    tmp = p.sb("tmpl", [128, TT], F32)
    tmp2 = p.sb("tmp2", [128, TT], F32)
    lv = p.sb("lv", [128, TT], F32)
    lvs = p.sb("lvs", [128, 1], F32)
    mv = p.sb("mv", [128, 2], F32)
    ident = p.sb("ident", [128, 128], BF16)
    onescol = p.sb("onescol", [128, 1], F32)
    xev = [p.sb(f"xev{j}", [128, 512], BF16) for j in range(2)]
    lvtot = p.sb("lvtot", [1, 1], F32)

    pst = [p.ps(f"pst{j}", [128, 512], BF16) for j in range(2)]
    pss = p.ps("pss", [1, 1], F32)

    p.build()
    NG = DM // 512

    tok_id = p.dma("sy", ident[:], identb, sem_name="kq")
    v_done = {}
    for i in range(TT):
        w = [v_done[i - 2]] if i >= 2 else []
        tok_x = p.dma("sy", xt[i % 2][:], x[i * 128:(i + 1) * 128, :], waits=w, sem_name=f"xq{i % 2}")
        fns = [(lambda e, g=g, i=i: e.bn_stats(out=bst[:, 6 * g:6 * g + 6],
                                               in_=xt[i % 2][:, 512 * g:512 * (g + 1)])) for g in range(NG)]
        fns.append(lambda e: e.bn_aggr(out=mv[:], in_=bst[:]))
        fns.append(lambda e, i=i: e.tensor_copy(out=mcols[:, i:i + 1], in_=mv[:, 0:1]))
        fns.append(lambda e, i=i: e.tensor_copy(out=vcols[:, i:i + 1], in_=mv[:, 1:2]))
        v_done[i] = p.runs("ve", fns, waits=[tok_x])
    p.run("ve", lambda e: e.tensor_scalar_add(tmp[:], vcols[:], EPS_LN))
    t_ln = p.run("se", lambda e: e.activation(out=tmp2[:], in_=tmp[:], func=AF.Ln),
                 waits=[("ve", p.cnt["ve"])])
    t_rs = p.run("se", lambda e: e.activation(out=rcols[:], in_=tmp2[:], func=AF.Exp, scale=-0.5))
    t_rc = p.run("ve", lambda e: e.reciprocal(tmp2[:], tmp[:]), waits=[t_rs])
    t_vr = p.run("ve", lambda e: e.tensor_mul(tmp[:], vcols[:], tmp2[:]))
    t_lv = p.run("se", lambda e: e.activation(out=lv[:], in_=tmp[:], func=AF.Ln, bias=1.0), waits=[t_vr])
    t_sum = p.run("ve", lambda e: e.reduce_sum(lvs[:], lv[:], axis=AX.X), waits=[t_lv])
    t_one = p.run("ve", lambda e: e.memset(onescol[:], 1.0))
    tok_lv = p.dma("sy", lvtok, lv[:], waits=[t_lv], sem_name="lq")
    t_mm = p.run("te", lambda e: e.matmul(pss[:, :], onescol[:], lvs[:], start=True, stop=True),
                 waits=[t_sum, t_one], self_sync=False)
    t_ev = p.run("ve", lambda e: e.tensor_copy(out=lvtot[:], in_=pss[:, :]), waits=[t_mm])
    tok_lp = p.dma("sy", lvpart, lvtot[:], waits=[t_ev], sem_name="lq")

    xn_done = {}
    tp_done = {}
    ngrp = 0
    for i in range(TT):
        b = i % 2
        w = [xn_done[i - 2]] if i >= 2 else [v_done[14 + i]]
        tok_x = p.dma("sy", xt[b][:], x[i * 128:(i + 1) * 128, :], waits=w, sem_name=f"xq{i % 2}")
        w2 = [tok_x, t_rs]
        if i >= 1:
            w2.append(tp_done[i - 1])
        xn_done[i] = p.run("ve", lambda e, i=i, b=b: e.tensor_scalar(
            xnb[:], xt[b][:], mcols[:, i:i + 1], rcols[:, i:i + 1],
            op0=ALU.subtract, op1=ALU.mult), waits=w2)
        for dcg in range(DCH // 4):
            q = ngrp % 2
            wv = [xn_done[i], tok_id]
            if ngrp >= 2:
                wv.append(("ve2", ngrp - 1))
            fns = [(lambda e, dc=dcg * 4 + j, q=q: e.transpose(
                pst[q][:, (dc % 4) * 128:(dc % 4 + 1) * 128],
                xnb[:, dc * 128:(dc + 1) * 128], ident[:])) for j in range(4)]
            tp_done[i] = p.runs("te", fns, waits=wv, self_sync=False)
            we = [tp_done[i]]
            if ngrp >= 2:
                we.append((f"gq{q}", 16 * (ngrp // 2)))
            t_e = p.run("ve2", lambda e, q=q: e.tensor_copy(out=xev[q][:], in_=pst[q][:, :]), waits=we)
            dst = xnfm[dcg * 4:(dcg + 1) * 4, :, i * 128:(i + 1) * 128].rearrange("a b c -> b a c")
            p.dma("gp", dst, xev[q][:], waits=[t_e], sem_name=f"gq{q}")
            ngrp += 1

    p.finish([("lq", p.cnt["lq"]), ("gq0", p.cnt["gq0"]), ("gq1", p.cnt["gq1"])])
    return p



def emit_stream_mm(p, name, wdram, och, dch, xnsb, xntok, evac, psets, tblk=4, alloc=None):
    al = alloc or p
    wsl = [al.sb(f"{name}_w{j}", [128, dch * 128], BF16) for j in range(2)]
    wtok, mmtok, evtok = {}, {}, {}

    def wdma(o):
        w = [mmtok[o - 2]] if o >= 2 else []
        wtok[o] = p.dma("gp", wsl[o % 2][:], wdram[o].rearrange("p d q -> p (d q)"),
                        waits=w, sem_name=f"{name}w{o % 2}")

    wdma(0)
    for o in range(och):
        if o + 1 < och:
            wdma(o + 1)
        pset = psets[o % 2]
        w = [wtok[o], xntok]
        if o >= 2:
            w.append(evtok[o - 2])
        fns = []
        for dc in range(dch):
            for tb in range(tblk):
                fns.append(lambda e, dc=dc, tb=tb, o=o: e.matmul(
                    pset[tb][:, :], wsl[o % 2][:, dc * 128:(dc + 1) * 128],
                    xnsb[:, dc * T + tb * 512:dc * T + (tb + 1) * 512],
                    start=(dc == 0), stop=(dc == dch - 1)))
        mmtok[o] = p.run_group("te", fns, waits=w)
        evtok[o] = evac(o, pset, mmtok[o])
    return mmtok, evtok


def load_xnsb(p, xnsb, xnfm_ap, dch, nsplit=4):
    tok = None
    step = dch // nsplit
    for s in range(nsplit):
        src = xnfm_ap[s * step:(s + 1) * step].rearrange("d p t -> p d t")
        dst = xnsb[:, s * step * T:(s + 1) * step * T].rearrange("p (d t) -> p d t", d=step)
        tok = p.dma("sy", dst, src, sem_name="xnq")
    return ("xnq", p.cnt["xnq"])



def build_inproj():
    p = Prog()
    xnfm = p.din("xnfm", [DCH, 128, T], BF16)
    wproj = p.din("wproj", [PROJ // 128, 128, DCH, 128], F32)
    dtb = p.din("dtb", [128, 1], F32)
    gsilu = p.dout("gsilu", [DIN // 128, 128, T], BF16)
    xbc = p.dout("xbc", [CONVD // 128, 128, T], BF16)
    dtfm = p.dout("dtfm", [128, T], F32)

    xnsb = p.sb("xnsb", [128, DCH * T], BF16)
    outb = [p.sb(f"outb{j}", [128, T], BF16) for j in range(2)]
    dtsb = p.sb("dtsb", [128, T], F32)
    dsc1 = p.sb("dsc1", [128, 512], F32)
    dsc2 = p.sb("dsc2", [128, 512], F32)
    dtbc = p.sb("dtbc", [128, 1], F32)
    psA = [p.ps(f"ppA{i}", [128, 512]) for i in range(4)]
    psB = [p.ps(f"ppB{i}", [128, 512]) for i in range(4)]

    p.build()
    xntok = load_xnsb(p, xnsb, xnfm, DCH)
    tok_b = p.dma("sy", dtbc[:], dtb, sem_name="kq")
    OCH = PROJ // 128
    odma = {}

    def evac(o, pset, tok_mm):
        if o < 64:
            key, buf = "se", outb[o % 2]
            fns = [(lambda e, tb=tb: e.activation(out=buf[:, tb * 512:(tb + 1) * 512],
                                                  in_=pset[tb][:, :], func=AF.Silu))
                   for tb in range(4)]
        elif o < 144:
            key, buf = "ve", outb[o % 2]
            fns = [(lambda e, tb=tb: e.tensor_copy(out=buf[:, tb * 512:(tb + 1) * 512],
                                                   in_=pset[tb][:, :])) for tb in range(4)]
        else:
            last = None
            for tb in range(4):
                sl = slice(tb * 512, (tb + 1) * 512)
                p.run("se", lambda e, tb=tb: e.activation(out=dsc1[:], in_=pset[tb][:, :],
                                                          func=AF.Abs, bias=dtbc[:]),
                      waits=[tok_mm, tok_b])
                p.run("se", lambda e: e.activation(out=dsc2[:], in_=dsc1[:], func=AF.Exp, scale=-1.0))
                p.run("se", lambda e: e.activation(out=dsc1[:], in_=dsc2[:], func=AF.Ln, bias=1.0))
                t4 = p.run("se", lambda e, tb=tb: e.activation(out=dsc2[:], in_=pset[tb][:, :],
                                                               func=AF.Relu, bias=dtbc[:]))
                last = p.run("ve", lambda e, sl=sl: e.tensor_add(dtsb[:, sl], dsc1[:], dsc2[:]),
                             waits=[t4])
            odma[o] = p.dma("sy", dtfm, dtsb[:], waits=[last], sem_name="oq0")
            return last
        w = [tok_mm]
        if o >= 2:
            w.append(odma[o - 2])
        t_e = p.runs(key, fns, waits=w)
        dst = gsilu[o] if o < 64 else xbc[o - 64]
        odma[o] = p.dma("sy", dst, buf[:], waits=[t_e], sem_name=f"oq{o % 2}")
        return t_e

    emit_stream_mm(p, "ip", wproj, OCH, DCH, xnsb, xntok, evac, (psA, psB))
    p.finish([("oq0", p.cnt["oq0"]), ("oq1", p.cnt["oq1"])])
    return p


def host_inproj_inputs(inputs, xnfm_list):
    wT = np.ascontiguousarray(inputs["L0_attn_in_proj_w"].T)
    slab = np.ascontiguousarray(
        wT.reshape(DCH, 128, PROJ // 128, 128).transpose(2, 1, 0, 3)).astype(np.float32)
    dtb = np.ascontiguousarray(inputs["L0_attn_dt_bias"].reshape(128, 1)).astype(np.float32)
    return [{"xnfm": xnfm_list[c], "wproj": slab, "dtb": dtb} for c in range(len(xnfm_list))]



def build_convssm():
    p = Prog()
    NCT = CONVD // 128
    xbc = p.din("xbc", [NCT, 128, T], BF16)
    xbcb = p.din("xbcb", [NCT, 128, 3], BF16)
    cw = p.din("cw", [NCT, 128, 4], F32)
    cb = p.din("cb", [NCT, 128], F32)
    dtfm_i = p.din("dtfm", [128, T], F32)
    alog = p.din("alog", [128, 1], F32)
    identb = p.din("identb", [128, 128], BF16)
    identf = p.din("identf", [128, 128], F32)

    xch_o = p.dout("xch", [NC, 128, DIN], BF16)
    xs_o = p.dout("xs", [NC, 128, DIN], BF16)
    bfm_o = p.dout("bfm", [G, 128, T], BF16)
    cfm_o = p.dout("cfm", [G, 128, T], BF16)
    cst_o = p.dout("cst", [NC, 128, 128], F32)
    ncst_o = p.dout("ncst", [NC, 128, 128], F32)
    ecst_o = p.dout("ecst", [NC, 128, 128], F32)
    pfac_o = p.dout("pfac", [128, NC], F32)
    dA_o = p.dout("dA", [128, NC], F32)
    rawst_o = p.dout("rawst", [NC, NH, NST, HD], F32)
    owncum_o = p.dout("owncum", [NC, NH, NST * HD], F32)
    shand_o = p.dout("shand", [NH, NST * HD], F32)

    bfm = p.sb("bfm_sb", [128, G * T], BF16)
    dtt = p.sb("dtt", [128, T], F32)
    dect = p.sb("dect", [128, T], F32)
    dAsb = p.sb("dAsb", [128, NC], F32)
    pf = p.sb("pf", [128, NC], F32)
    identS = p.sb("identS", [128, 128], BF16)
    identF = p.sb("identF", [128, 128], F32)
    onesT = p.sb("onesT", [128, 128], F32)

    pbt = [p.ps(f"pbt{j}", [128, 512], BF16) for j in range(2)]
    pft = [p.ps(f"pft{j}", [128, 128], F32) for j in range(2)]
    psm = [p.ps(f"psm{j}", [128, 512]) for j in range(4)]

    p.build()

    with p.scoped() as s:
        dtsb = s.sb("dtsb", [128, T], F32)
        acol = s.sb("acol", [128, 1], F32)
        adt = s.sb("adt", [128, T], F32)
        cssb = s.sb("cssb", [128, T], F32)
        cstA = s.sb("cstA", [128, T], F32)
        ncstA = s.sb("ncstA", [128, T], F32)
        ecstA = s.sb("ecstA", [128, T], F32)
        decfA = s.sb("decfA", [128, T], F32)

        p.dma("sy", identS[:], identb, sem_name="kq")
        p.dma("sy", identF[:], identf, sem_name="kq")
        p.dma("sy", dtsb[:], dtfm_i, sem_name="kq")
        p.dma("sy", acol[:], alog, sem_name="kq")
        tok_k = ("kq", 64)
        t_one = p.run("ve", lambda e: e.memset(onesT[:], 1.0))
        t_a = p.run("se", lambda e: e.activation(out=acol[:], in_=acol[:], func=AF.Exp), waits=[tok_k])
        p.run("ve", lambda e: e.tensor_scalar_mul(acol[:], acol[:], -1.0), waits=[t_a])
        t_adt = p.run("ve", lambda e: e.tensor_scalar_mul(adt[:], dtsb[:], acol[:]))
        fns = [(lambda e, c=c: e.tensor_tensor_scan(out=cssb[:, c * CS:(c + 1) * CS],
                                                    data0=onesT[:], data1=adt[:, c * CS:(c + 1) * CS],
                                                    initial=0.0, op0=ALU.mult, op1=ALU.add))
               for c in range(NC)]
        t_cs = p.runs("ve", fns, waits=[t_adt, t_one])
        fns = [(lambda e, c=c: e.activation(out=dAsb[:, c:c + 1], in_=cssb[:, c * CS + CS - 1:c * CS + CS],
                                            func=AF.Exp)) for c in range(NC)]
        t_dA = p.runs("se", fns, waits=[t_cs])
        p.dma("sy", dA_o, dAsb[:], waits=[t_dA], sem_name="obq")
        t_pf0 = p.run("ve", lambda e: e.memset(pf[:, 0:1], 1.0))
        t_pf = p.run("ve", lambda e: e.tensor_tensor_scan(out=pf[:, 1:NC], data0=dAsb[:, 0:NC - 1],
                                                          data1=onesT[:, 0:NC - 1], initial=1.0,
                                                          op0=ALU.mult, op1=ALU.mult), waits=[t_dA])
        p.dma("sy", pfac_o, pf[:], waits=[t_pf0, t_pf], sem_name="obq")
        fns = [(lambda e, c=c: e.tensor_scalar(decfA[:, c * CS:(c + 1) * CS], cssb[:, c * CS:(c + 1) * CS],
                                               cssb[:, c * CS + CS - 1:c * CS + CS], None,
                                               op0=ALU.subtract)) for c in range(NC)]
        t_dec1 = p.runs("ve", fns, waits=[t_cs])
        t_dec2 = p.run("se", lambda e: e.activation(out=decfA[:], in_=decfA[:], func=AF.Exp, scale=-1.0), waits=[t_dec1])
        pft_free = {}
        for c in range(NC):
            for j, (srcbuf, t_src, dst) in enumerate([(cssb, t_cs, cstA), (decfA, t_dec2, dect),
                                                      (dtsb, tok_k, dtt)]):
                idx = c * 3 + j
                q = idx % 2
                wv = [t_src, tok_k]
                if idx >= 2:
                    wv.append(pft_free[idx - 2])
                t_tp = p.run("te", lambda e, srcbuf=srcbuf, c=c, q=q: e.transpose(
                    pft[q][:, :], srcbuf[:, c * CS:(c + 1) * CS], identF[:]), waits=wv, self_sync=False)
                pft_free[idx] = p.run("vf", lambda e, dst=dst, c=c, q=q: e.tensor_copy(
                    out=dst[:, c * CS:(c + 1) * CS], in_=pft[q][:, :]), waits=[t_tp])
        t_vf_all = ("vf", p.cnt["vf"])
        t_n = p.run("ve", lambda e: e.tensor_scalar_mul(ncstA[:], cstA[:], -1.0), waits=[t_vf_all])
        t_ec = p.run("se", lambda e: e.activation(out=ecstA[:], in_=cstA[:], func=AF.Exp), waits=[t_vf_all])
        p.dma("sy", cst_o.rearrange("c p k -> p c k"), cstA[:].rearrange("p (c k) -> p c k", c=NC),
              waits=[t_vf_all], sem_name="obq")
        p.dma("sy", ncst_o.rearrange("c p k -> p c k"), ncstA[:].rearrange("p (c k) -> p c k", c=NC),
              waits=[t_n], sem_name="obq")
        p.dma("sy", ecst_o.rearrange("c p k -> p c k"), ecstA[:].rearrange("p (c k) -> p c k", c=NC),
              waits=[t_ec], sem_name="obq")

    p.phase_barrier()
    NCT = CONVD // 128
    with p.scoped() as s:
        padded = [s.sb(f"pad{j}", [128, T + 3], BF16) for j in range(2)]
        acc = s.sb("acc", [128, T], F32)
        convo = [s.sb(f"convo{j}", [128, T], BF16) for j in range(2)]
        cwc = [s.sb(f"cwc{j}", [128, 4], F32) for j in range(2)]
        cbc = [s.sb(f"cbc{j}", [128, 1], F32) for j in range(2)]
        xev = [s.sb(f"cxev{j}", [128, 512], BF16) for j in range(2)]
        conv_done = {}
        use_done = {}
        sil_done = {}
        ngrp = 0
        for ct in range(NCT):
            b = ct % 2
            w = [conv_done[ct - 2], sil_done[ct - 2]] if ct >= 2 else []
            p.dma("sy", padded[b][:, 0:3], xbcb[ct], waits=w, sem_name=f"caq{b}")
            p.dma("sy", padded[b][:, 3:], xbc[ct], sem_name=f"caq{b}")
            p.dma("sy", cwc[b][:], cw[ct], sem_name=f"caq{b}")
            p.dma("sy", cbc[b][:], cb[ct].unsqueeze(1), sem_name=f"caq{b}")
            tok_grp = (f"caq{b}", 64 * (ct // 2 + 1))
            wmac = [tok_grp]
            if ct >= 1:
                wmac.append(sil_done[ct - 1])
            t = p.run("ve", lambda e, b=b: e.tensor_scalar_mul(acc[:], padded[b][:, 0:T], cwc[b][:, 0:1]),
                      waits=wmac)
            for k2 in range(1, 4):
                t = p.run("ve", lambda e, k2=k2, b=b: e.scalar_tensor_tensor(
                    acc[:], padded[b][:, k2:k2 + T], cwc[b][:, k2:k2 + 1], acc[:],
                    op0=ALU.mult, op1=ALU.add))
            conv_done[ct] = t
            wsil = [t]
            if ct >= 2:
                wsil.append(use_done[ct - 2])
            t_sl = p.run("se", lambda e, b=b: e.activation(out=convo[b][:], in_=acc[:],
                                                           func=AF.Silu, bias=cbc[b][:]), waits=wsil)
            sil_done[ct] = t_sl
            if ct < 64:
                last_tp = None
                for cg in range(NC // 4):
                    q = ngrp % 2
                    wv = [t_sl]
                    if ngrp >= 2:
                        wv.append(("vx", ngrp - 1))
                    fns = [(lambda e, c=cg * 4 + j, q=q, b=b: e.transpose(
                        pbt[q][:, (c % 4) * 128:(c % 4 + 1) * 128],
                        convo[b][:, c * CS:(c + 1) * CS], identS[:])) for j in range(4)]
                    last_tp = p.runs("te", fns, waits=wv, self_sync=False)
                    we = [last_tp]
                    if ngrp >= 2:
                        we.append((f"xqa{q}", 16 * (ngrp // 2)))
                    t_e = p.run("vx", lambda e, q=q: e.tensor_copy(out=xev[q][:], in_=pbt[q][:, :]), waits=we)
                    dst = xch_o[cg * 4:(cg + 1) * 4, :, ct * 128:(ct + 1) * 128].rearrange("a b c -> b a c")
                    p.dma("gp", dst, xev[q][:], waits=[t_e], sem_name=f"xqa{q}")
                    ngrp += 1
                use_done[ct] = last_tp
            elif ct < 72:
                g = ct - 64
                t_c = p.run("ve", lambda e, g=g, b=b: e.tensor_copy(out=bfm[:, g * T:(g + 1) * T], in_=convo[b][:]),
                            waits=[t_sl])
                p.dma("gp", bfm_o[g], bfm[:, g * T:(g + 1) * T], waits=[t_c], sem_name="bfq")
                use_done[ct] = t_c
            else:
                g = ct - 72
                use_done[ct] = p.dma("gp", cfm_o[g], convo[b][:], waits=[t_sl], sem_name=f"bq{b}")

    p.phase_barrier()
    with p.scoped() as s:
        xchb = [s.sb(f"xchb{j}", [128, DIN], BF16) for j in range(2)]
        xssb = [s.sb(f"xssb{j}", [128, DIN], BF16) for j in range(2)]
        dxssb = s.sb("dxssb", [128, DIN], BF16)
        btok = s.sb("btok", [128, 128], BF16)
        rawsb = [s.sb(f"rawsb{j}", [128, 1024], F32) for j in range(2)]
        xs_read = {}
        mm_done = {}
        mm_last = None
        for c in range(NC):
            b = c % 2
            w = [xs_read[c - 2]] if c >= 2 else []
            tok_xc = p.dma("sy", xchb[b][:], xch_o[c], waits=w, sem_name=f"ccq{b}")
            wxs = [tok_xc]
            if c >= 2:
                wxs.append((f"xsq{b}", 16 * (c // 2)))
            if c >= 1:
                wxs.append(mm_done[c - 1])
            t_xs = p.run("ve", lambda e, c=c, b=b: e.tensor_tensor(
                xssb[b][:].rearrange("p (h d) -> p h d", h=128),
                xchb[b][:].rearrange("p (h d) -> p h d", h=128),
                dtt[:, c * CS:(c + 1) * CS].unsqueeze(2).to_broadcast([128, 128, HD]),
                op=ALU.mult), waits=wxs)
            xs_read[c] = t_xs
            p.dma("gp", xs_o[c], xssb[b][:], waits=[t_xs], sem_name=f"xsq{b}")
            t_dxs = p.run("ve", lambda e, c=c, b=b: e.tensor_tensor(
                dxssb[:].rearrange("p (h d) -> p h d", h=128),
                xssb[b][:].rearrange("p (h d) -> p h d", h=128),
                dect[:, c * CS:(c + 1) * CS].unsqueeze(2).to_broadcast([128, 128, HD]),
                op=ALU.mult))
            for g in range(G):
                n = c * G + g
                q = n % 2
                wv = []
                if n >= 2:
                    wv.append(("vb", n - 1))
                t_tp = p.run("te", lambda e, g=g, c=c, q=q: e.transpose(
                    pbt[q][:, 0:128], bfm[:, g * T + c * CS:g * T + (c + 1) * CS], identS[:]),
                    waits=wv, self_sync=False)
                wbt = [t_tp] + ([mm_last] if n >= 1 else [])
                t_bt = p.run("vb", lambda e, q=q: e.tensor_copy(out=btok[:], in_=pbt[q][:, 0:128]),
                             waits=wbt)
                wmm = [t_bt, t_dxs]
                if n >= 2:
                    wmm.append(("vs", 2 * (n - 1)))
                fns = [(lambda e, g=g, tb=tb, q=q: e.matmul(
                    psm[2 * q + tb][:, :], btok[:],
                    dxssb[:, g * 1024 + tb * 512:g * 1024 + (tb + 1) * 512],
                    start=True, stop=True)) for tb in range(2)]
                mm_last = p.run_group("te", fns, waits=wmm)
                if g == G - 1:
                    mm_done[c] = mm_last
                we = [mm_last]
                if n >= 2:
                    we.append((f"rsq{q}", 16 * (n // 2)))
                t_ev = p.runs("vs", [(lambda e, q=q, tb=tb: e.tensor_copy(
                    out=rawsb[q][:, tb * 512:(tb + 1) * 512], in_=psm[2 * q + tb][:, :]))
                    for tb in range(2)], waits=we)
                dst = rawst_o[c, 16 * g:16 * (g + 1)].rearrange("h n d -> n h d")
                p.dma("gp", dst, rawsb[q][:].rearrange("n (h d) -> n h d", h=16),
                      waits=[t_ev], sem_name=f"rsq{q}")

    p.phase_barrier()
    with p.scoped() as s:
        statesb = s.sb("statesb", [128, NST * HD], F32)
        rawrd = [s.sb(f"rawrd{j}", [128, NST * HD], F32) for j in range(2)]
        t_z = p.run("ve", lambda e: e.memset(statesb[:], 0.0))
        up_tok = {}
        t_up = t_z
        for c in range(NC):
            b = c % 2
            wr = [up_tok[c - 2]] if c >= 2 else []
            tok_r = p.dma("sy", rawrd[b][:], rawst_o[c].rearrange("h n d -> h (n d)"),
                          waits=wr, sem_name=f"rrq{b}")
            tok_st = p.dma("sy", owncum_o[c], statesb[:], waits=[t_up], sem_name="odq")
            t_up = p.run("ve", lambda e, c=c, b=b: e.scalar_tensor_tensor(
                statesb[:], statesb[:], dAsb[:, c:c + 1], rawrd[b][:],
                op0=ALU.mult, op1=ALU.add), waits=[tok_r, tok_st])
            up_tok[c] = t_up
        p.dma("sy", shand_o, statesb[:], waits=[t_up], sem_name="odq")

    allq = [q for q in ("obq", "xqa0", "xqa1", "bfq", "bq0", "bq1", "xsq0", "xsq1",
                        "rsq0", "rsq1", "odq") if q in p.cnt]
    p.finish([(q, p.cnt[q]) for q in allq])
    return p


def host_convssm_inputs(inputs, xbc_list, dt_list):
    cw = np.ascontiguousarray(inputs["L0_attn_conv_w"].reshape(CONVD, 4).reshape(CONVD // 128, 128, 4)).astype(np.float32)
    cb = np.ascontiguousarray(inputs["L0_attn_conv_b"].reshape(CONVD // 128, 128)).astype(np.float32)
    alog = np.ascontiguousarray(inputs["L0_attn_A_log"].reshape(128, 1)).astype(np.float32)
    identB = np.eye(128, dtype=np.float32).astype(BF)
    identF = np.eye(128, dtype=np.float32)
    n = len(xbc_list)
    maps = []
    for c in range(n):
        if c % 2 == 1:
            bdry = np.ascontiguousarray(xbc_list[c - 1][:, :, T - 3:T])
        else:
            bdry = np.zeros((CONVD // 128, 128, 3), dtype=BF)
        maps.append({"xbc": xbc_list[c], "xbcb": bdry, "cw": cw, "cb": cb, "alog": alog,
                     "dtfm": dt_list[c], "identb": identB, "identf": identF})
    return maps



def build_ssm2():
    p = Prog()
    xs_i = p.din("xs", [NC, 128, DIN], BF16)
    xch_i = p.din("xch", [NC, 128, DIN], BF16)
    bfm_i = p.din("bfm", [G, 128, T], BF16)
    cfm_i = p.din("cfm", [G, 128, T], BF16)
    cst_i = p.din("cst", [NC, 128, 128], F32)
    ncst_i = p.din("ncst", [NC, 128, 128], F32)
    ecst_i = p.din("ecst", [NC, 128, 128], F32)
    pfac_i = p.din("pfac", [128, NC], F32)
    owncum_i = p.din("owncum", [NC, NH, NST * HD], F32)
    sinit_i = p.din("sinit", [NH, NST * HD], F32)
    gsilu_i = p.din("gsilu", [DIN // 128, 128, T], BF16)
    normw_i = p.din("normw", [128, DIN // 128], F32)
    dvec_i = p.din("dvec", [1, 128], F32)
    identb = p.din("identb", [128, 128], BF16)
    identf = p.din("identf", [128, 128], F32)
    masklow_i = p.din("masklow_in", [128, 128], F32)
    opw = p.din("opw", [DCH, 128, DIN // 128, 128], F32)

    ynfm_o = p.dout("ynfm", [DIN // 128, 128, T], BF16)
    s0fm_o = p.dout("s0fm", [DCH, 128, T], BF16)

    identS = p.sb("identS", [128, 128], BF16)
    identF = p.sb("identF", [128, 128], F32)
    masklow = p.sb("masklow", [128, 128], F32)
    pfsb = p.sb("pfsb", [128, NC], F32)
    drep = p.sb("drep", [128, 128], F32)
    nwsb = p.sb("nwsb", [128, DIN // 128], F32)
    onescol = p.sb("onescol", [128, 1], F32)
    cstsb = p.sb("cstsb", [128, NC * 128], F32)
    ncstsb = p.sb("ncstsb", [128, NC * 128], F32)
    ecstsb = p.sb("ecstsb", [128, NC * 128], F32)

    p.build()

    nk = 0
    for (dst, srcap) in [(identS[:], identb), (identF[:], identf), (masklow[:], masklow_i),
                         (pfsb[:], pfac_i), (drep[:], dvec_i.to_broadcast([128, 128])),
                         (nwsb[:], normw_i),
                         (cstsb[:].rearrange("p (c k) -> p c k", c=NC), cst_i.rearrange("c p k -> p c k")),
                         (ncstsb[:].rearrange("p (c k) -> p c k", c=NC), ncst_i.rearrange("c p k -> p c k")),
                         (ecstsb[:].rearrange("p (c k) -> p c k", c=NC), ecst_i.rearrange("c p k -> p c k"))]:
        p.dma("sy", dst, srcap, sem_name="kq")
        nk += 16
    tok_k = ("kq", nk)
    t_one = p.run("ve", lambda e: e.memset(onescol[:], 1.0))

    pi_scr = p.nc.dram_tensor("piscr", [NC, NST, NH, HD], F32, kind="Internal").ap()
    rscr = p.nc.dram_tensor("rscr", [1, 1024], F32, kind="Internal").ap()

    with p.scoped() as s:
        sinit = s.sb("sinit_sb", [128, NST * HD], F32)
        pint = [s.sb(f"pint{j}", [128, NST * HD], F32) for j in range(2)]
        tok_si = p.dma("sy", sinit[:], sinit_i, sem_name="kq2")
        for c in range(NC):
            b = c % 2
            w = [tok_si, tok_k] + ([(f"piq{b}", 16 * (c // 2))] if c >= 2 else [])
            t_pi = p.run("ve", lambda e, c=c, b=b: e.tensor_scalar_mul(
                pint[b][:], sinit[:], pfsb[:, c:c + 1]), waits=w)
            p.dma("sy", pi_scr[c].rearrange("n h d -> h n d"),
                  pint[b][:].rearrange("h (n d) -> h n d", n=NST), waits=[t_pi], sem_name=f"piq{b}")

    p.phase_barrier()
    with p.scoped() as s:
        bfmc = [s.sb(f"bfmc{j}", [128, G * 128], BF16) for j in range(2)]
        cfmc = [s.sb(f"cfmc{j}", [128, G * 128], BF16) for j in range(2)]
        xsc = s.sb("xsc", [128, DIN], BF16)
        xchc = s.sb("xchc", [128, DIN], BF16)
        gslc = s.sb("gslc", [128, DIN], BF16)
        ownc = [s.sb(f"ownc{j}", [128, 1024], BF16) for j in range(2)]
        pinc = [s.sb(f"pinc{j}", [128, 1024], BF16) for j in range(2)]
        gmtc = s.sb("gmtc", [128, 128], BF16)
        egsb = [s.sb(f"egsb{j}", [128, 16 * 128], BF16) for j in range(2)]
        ytok = s.sb("ytok", [128, DIN], BF16)
        ytmp = s.sb("ytmp", [128, 1024], F32)
        ygfm = s.sb("ygfm", [128, DIN], BF16)
        ynfmc = s.sb("ynfmc", [128, DIN], BF16)
        sqb = s.sb("sqb", [128, 1024], F32)
        gsrow = s.sb("gsrow", [1, 1024], F32)
        rstdrep = s.sb("rstdrep", [128, 1024], F32)

        psg = s.ps("psg", [128, 128])
        pse = [s.ps(f"pse{j}", [128, 512]) for j in range(2)]
        psyo = s.ps("psyo", [128, 1024])
        psyd = s.ps("psyd", [128, 1024])
        psyt = s.ps("psyt", [128, 512], BF16)

        prev = {}
        for c in range(NC):
            b = c % 2
            wld = [prev[("ld", c - 2)]] if c >= 2 else []
            p.dma("gp", bfmc[b][:].rearrange("p (g k) -> p g k", g=G),
                  bfm_i[:, :, c * CS:(c + 1) * CS].rearrange("g p k -> p g k"),
                  waits=wld, sem_name=f"ldq{b}")
            p.dma("gp", cfmc[b][:].rearrange("p (g k) -> p g k", g=G),
                  cfm_i[:, :, c * CS:(c + 1) * CS].rearrange("g p k -> p g k"), sem_name=f"ldq{b}")
            tok_ld = (f"ldq{b}", p.cnt[f"ldq{b}"])
            wx = [prev[("xcon", c - 1)]] if c >= 1 else []
            tok_xs = p.dma("sy", xsc[:], xs_i[c], waits=wx, sem_name="xq")
            tok_xh = p.dma("sy", xchc[:], xch_i[c], sem_name="xq")
            tok_gs = p.dma("sy", gslc[:].rearrange("p (f k) -> p f k", f=64),
                           gsilu_i[:, :, c * CS:(c + 1) * CS].rearrange("f p k -> p f k"), sem_name="xq")
            tok_x = ("xq", p.cnt["xq"])

            for g in range(G):
                eb = g % 2
                wv = [tok_ld, tok_k] + ([prev[("gmtev",)]] if ("gmtev",) in prev else [])
                t_mm = p.run("te", lambda e, g=g, b=b: e.matmul(
                    psg[:, :], bfmc[b][:, g * 128:(g + 1) * 128], cfmc[b][:, g * 128:(g + 1) * 128],
                    start=True, stop=True), waits=wv, self_sync=False)
                t_gmt = p.run("ve", lambda e: e.tensor_copy(out=gmtc[:], in_=psg[:, :]), waits=[t_mm])
                prev[("gmtev",)] = t_gmt
                for qq in range(4):
                    qb = qq % 2
                    wv = [tok_k] + ([prev[("exp", qb)]] if ("exp", qb) in prev else [])
                    fns = [lambda e, qb=qb: e.matmul(
                        pse[qb][:, :].rearrange("p (a k) -> p a k", a=4), identF[:],
                        masklow[:].unsqueeze(1).to_broadcast([128, 4, 128]),
                        start=True, stop=False)]
                    for hh in range(4):
                        h = g * 16 + qq * 4 + hh
                        fns.append(lambda e, h=h, hh=hh, qb=qb, c=c: e.matmul(
                            pse[qb][:, hh * 128:(hh + 1) * 128],
                            cstsb[:, c * 128 + h:c * 128 + h + 1].to_broadcast([128, 128]),
                            identF[:], start=False, stop=(hh == 3)))
                    t_emm = p.run_group("te", fns, waits=wv)
                    fns = [(lambda e, h=g * 16 + qq * 4 + hh, hh=hh, qb=qb, c=c, qq=qq, g=g: e.activation(
                        out=egsb[g % 2][:, (qq * 4 + hh) * 128:(qq * 4 + hh + 1) * 128],
                        in_=pse[qb][:, hh * 128:(hh + 1) * 128],
                        func=AF.Exp, bias=ncstsb[:, c * 128 + h:c * 128 + h + 1])) for hh in range(4)]
                    wexp = [t_emm] + ([prev[("egused", g % 2)]] if ("egused", g % 2) in prev and qq == 0 else [])
                    prev[("exp", qb)] = p.runs("se", fns, waits=wexp)
                t_exp_all = prev[("exp", 1)]
                t_eg = p.run("ve", lambda e, g=g: e.tensor_tensor(
                    egsb[g % 2][:].rearrange("p (a k) -> p a k", a=16),
                    egsb[g % 2][:].rearrange("p (a k) -> p a k", a=16),
                    gmtc[:].unsqueeze(1).to_broadcast([128, 16, 128]),
                    op=ALU.mult), waits=[prev[("exp", 0)], prev[("exp", 1)], t_gmt])
                wst = [prev[("ymm",)]] if ("ymm",) in prev else []
                tok_ow = p.dma("gp", ownc[b][:].rearrange("p (h d) -> p h d", h=16),
                               owncum_i[c].rearrange("h (n d) -> n h d", n=NST)[:, 16 * g:16 * (g + 1), :],
                               waits=wst, sem_name=f"owq{b}")
                tok_pi = p.dma("gp", pinc[b][:].rearrange("p (h d) -> p h d", h=16),
                               pi_scr[c, :, 16 * g:16 * (g + 1), :], sem_name=f"owq{b}")
                tok_st = (f"owq{b}", p.cnt[f"owq{b}"])
                wv = [tok_st, tok_x, t_eg] + ([prev[("yev",)]] if ("yev",) in prev else [])
                if ("strd",) in prev:
                    wv.append(prev[("strd",)])
                fns = []
                for hh in range(16):
                    fns.append(lambda e, g=g, hh=hh, b=b: e.matmul(
                        psyo[:, hh * 64:(hh + 1) * 64], cfmc[b][:, g * 128:(g + 1) * 128],
                        ownc[b][:, hh * 64:(hh + 1) * 64], start=True, stop=False))
                    fns.append(lambda e, g=g, hh=hh, b=b: e.matmul(
                        psyo[:, hh * 64:(hh + 1) * 64], cfmc[b][:, g * 128:(g + 1) * 128],
                        pinc[b][:, hh * 64:(hh + 1) * 64], start=False, stop=True))
                for hh in range(16):
                    h = g * 16 + hh
                    fns.append(lambda e, h=h, hh=hh, g=g: e.matmul(
                        psyd[:, hh * 64:(hh + 1) * 64], egsb[g % 2][:, hh * 128:(hh + 1) * 128],
                        xsc[:, h * 64:(h + 1) * 64], start=True, stop=True))
                t_ymm = p.run_group("te", fns, waits=wv)
                prev[("ymm",)] = t_ymm
                prev[("egused", g % 2)] = t_ymm
                we1 = [t_ymm]
                if g == 0 and ("ytp",) in prev:
                    we1.append(prev[("ytp",)])
                t_e1 = p.run("ve", lambda e, g=g, c=c: e.tensor_tensor(
                    ytmp[:].rearrange("p (a d) -> p a d", a=16),
                    psyo[:, :].rearrange("p (a d) -> p a d", a=16),
                    ecstsb[:, c * 128 + g * 16:c * 128 + (g + 1) * 16].unsqueeze(2).to_broadcast([128, 16, HD]),
                    op=ALU.mult), waits=we1)
                prev[("yev",)] = p.run("ve", lambda e, g=g: e.tensor_add(
                    ytok[:, g * 1024:(g + 1) * 1024], ytmp[:], psyd[:, :]))
            for g in range(G):
                t_d1 = p.run("ve", lambda e, g=g: e.tensor_tensor(
                    ytmp[:].rearrange("p (a d) -> p a d", a=16),
                    xchc[:, g * 1024:(g + 1) * 1024].rearrange("p (a d) -> p a d", a=16),
                    drep[:, g * 16:(g + 1) * 16].unsqueeze(2).to_broadcast([128, 16, HD]),
                    op=ALU.mult), waits=[prev[("yev",)], tok_x, tok_k] if g == 0 else [])
                prev[("yev",)] = p.run("ve", lambda e, g=g: e.tensor_add(
                    ytok[:, g * 1024:(g + 1) * 1024], ytok[:, g * 1024:(g + 1) * 1024], ytmp[:]))
            t_y = prev[("yev",)]
            for fg in range(16):
                wv = [t_y]
                if ("ytp",) in prev:
                    wv.append(prev[("ytp",)])
                fns = [(lambda e, f=fg * 4 + j: e.transpose(
                    psyt[:, (f % 4) * 128:(f % 4 + 1) * 128],
                    ytok[:, f * 128:(f + 1) * 128], identS[:])) for j in range(4)]
                t_tp = p.runs("te", fns, waits=wv, self_sync=False)
                prev[("ytp",)] = p.run("ve", lambda e, fg=fg: e.tensor_copy(
                    out=ygfm[:, fg * 512:(fg + 1) * 512], in_=psyt[:, :]), waits=[t_tp])
            t_gate = p.run("ve", lambda e: e.tensor_mul(ygfm[:], ygfm[:], gslc[:]),
                           waits=[prev[("ytp",)]])
            prev[("xcon", c)] = t_gate
            for g in range(G):
                wq = [t_gate] + ([prev[("sqmm",)]] if ("sqmm",) in prev else [])
                t_sq = p.run("se", lambda e, g=g: e.activation(
                    out=sqb[:], in_=ygfm[:, g * 1024:(g + 1) * 1024], func=AF.Square), waits=wq)
                fns = [(lambda e, f=f, g=g: e.matmul(
                    psyo[0:1, g * 128:(g + 1) * 128], onescol[:], sqb[:, f * 128:(f + 1) * 128],
                    start=(f == 0), stop=(f == 7))) for f in range(8)]
                wmm = [t_sq, t_one]
                if g == 0:
                    wmm.append(prev[("yev",)])
                    if ("strd",) in prev:
                        wmm.append(prev[("strd",)])
                prev[("sqmm",)] = p.run_group("te", fns, waits=wmm)
            t_st_mm = prev[("sqmm",)]
            t_r1 = p.run("ve", lambda e: e.tensor_scalar(gsrow[:], psyo[0:1, :], 1.0 / 1024.0, EPS_GN,
                                                         op0=ALU.mult, op1=ALU.add), waits=[t_st_mm])
            prev[("strd",)] = t_r1
            p.run("se", lambda e: e.activation(out=gsrow[:], in_=gsrow[:], func=AF.Ln), waits=[t_r1])
            t_r3 = p.run("se", lambda e: e.activation(out=gsrow[:], in_=gsrow[:], func=AF.Exp, scale=-0.5))
            wrw = [t_r3] + ([prev[("rrd",)]] if ("rrd",) in prev else [])
            tok_rw = p.dma("sy", rscr, gsrow[:], waits=wrw, sem_name="rsw")
            wrr = [tok_rw] + ([prev[("rst",)]] if ("rst",) in prev else [])
            tok_rr = p.dma("sy", rstdrep[:], rscr.to_broadcast([128, 1024]), waits=wrr, sem_name="rsr")
            prev[("rrd",)] = tok_rr
            fns = []
            for f in range(64):
                g = f // 8
                fns.append(lambda e, f=f, g=g: e.scalar_tensor_tensor(
                    ynfmc[:, f * 128:(f + 1) * 128], ygfm[:, f * 128:(f + 1) * 128],
                    nwsb[:, f:f + 1], rstdrep[:, g * 128:(g + 1) * 128],
                    op0=ALU.mult, op1=ALU.mult))
            wap = [tok_rr, tok_k] + ([prev[("ynw",)]] if ("ynw",) in prev else [])
            t_ap = p.runs("ve", fns, waits=wap)
            prev[("rst",)] = t_ap
            dst = ynfm_o[:, :, c * CS:(c + 1) * CS].rearrange("f p k -> p f k")
            prev[("ynw",)] = p.dma("gp", dst, ynfmc[:].rearrange("p (f k) -> p f k", f=64),
                                   waits=[t_ap], sem_name="ynq")
            prev[("ld", c)] = t_ymm

    p.phase_barrier()
    with p.scoped() as s:
        ynsb = s.sb("ynsb", [128, 64 * 1024], BF16)
        wsl = [s.sb(f"opwsl{j}", [128, 64 * 128], BF16) for j in range(2)]
        sout = [s.sb(f"sout{j}", [128, 1024], BF16) for j in range(2)]
        psA = [s.ps(f"oppA{i}", [128, 512]) for i in range(2)]
        psB = [s.ps(f"oppB{i}", [128, 512]) for i in range(2)]
        psets = [psA, psB]
        for half in range(2):
            for sp in range(4):
                p.dma("sy", ynsb[:, sp * 16 * 1024:(sp + 1) * 16 * 1024].rearrange(
                    "p (f k) -> p f k", f=16),
                    ynfm_o[sp * 16:(sp + 1) * 16, :, half * 1024:(half + 1) * 1024].rearrange("f p k -> p f k"),
                    waits=[("opd", p.cnt.get("opd", 0))] if sp == 0 else [], sem_name="ynl")
            tok_y = ("ynl", p.cnt["ynl"])
            wtok, mmtok, evtok = {}, {}, {}
            for o in range(DCH):
                wd = [mmtok[o - 2]] if o >= 2 else ([("opd", p.cnt.get("opd", 0))] if half else [])
                wtok[o] = p.dma("gp", wsl[o % 2][:], opw[o].rearrange("p d q -> p (d q)"),
                                waits=wd, sem_name=f"opw{o % 2}")
                pset = psets[o % 2]
                w = [wtok[o], tok_y]
                if o >= 2:
                    w.append(evtok[o - 2])
                elif half == 1:
                    w.append(ev_last)
                fns = []
                for dc in range(64):
                    for tb in range(2):
                        fns.append(lambda e, dc=dc, tb=tb, o=o: e.matmul(
                            pset[tb][:, :], wsl[o % 2][:, dc * 128:(dc + 1) * 128],
                            ynsb[:, dc * 1024 + tb * 512:dc * 1024 + (tb + 1) * 512],
                            start=(dc == 0), stop=(dc == 63)))
                mmtok[o] = p.run_group("te", fns, waits=w)
                we = [mmtok[o]]
                if o >= 2 or half == 1:
                    we.append((f"opq{o % 2}", p.cnt.get(f"opq{o % 2}", 0)))
                evtok[o] = p.runs("ve", [(lambda e, tb=tb, o=o: e.tensor_copy(
                    out=sout[o % 2][:, tb * 512:(tb + 1) * 512], in_=pset[tb][:, :])) for tb in range(2)],
                    waits=we)
                p.dma("sy", s0fm_o[o, :, half * 1024:(half + 1) * 1024], sout[o % 2][:],
                      waits=[evtok[o]], sem_name=f"opq{o % 2}")
            ev_last = evtok[DCH - 1]
            p.dma("gp", rscr[:, 0:1], sout[(DCH - 1) % 2][0:1, 0:1], waits=[mmtok[DCH - 1], evtok[DCH - 1]],
                  sem_name="opd")

    allq = [q for q in ("ynq", "opq0", "opq1") if q in p.cnt]
    p.finish([(q, p.cnt[q]) for q in allq])
    return p


def host_ssm2_inputs(inputs, l3outs, shand_list):
    wT = np.ascontiguousarray(inputs["L0_attn_out_proj_w"].T)
    opw = np.ascontiguousarray(wT.reshape(64, 128, DCH, 128).transpose(2, 1, 0, 3)).astype(np.float32)
    identB = np.eye(128, dtype=np.float32).astype(BF)
    identF = np.eye(128, dtype=np.float32)
    ml = np.where(np.arange(128)[None, :] < np.arange(128)[:, None], np.float32(NEG_BIG), np.float32(0.0))
    nw = np.ascontiguousarray(inputs["L0_attn_norm_w"].reshape(64, 128).T).astype(np.float32)
    dv = np.ascontiguousarray(inputs["L0_attn_D"].reshape(1, 128)).astype(np.float32)
    maps = []
    for c, o3 in enumerate(l3outs):
        sinit = (np.zeros((NH, NST * HD), dtype=np.float32) if c % 2 == 0
                 else np.ascontiguousarray(shand_list[c - 1]))
        maps.append({"xs": o3["xs"], "xch": o3["xch"], "bfm": o3["bfm"], "cfm": o3["cfm"],
                     "cst": o3["cst"], "ncst": o3["ncst"], "ecst": o3["ecst"], "pfac": o3["pfac"],
                     "owncum": o3["owncum"], "sinit": sinit, "gsilu": o3["gsilu"],
                     "normw": nw, "dvec": dv, "identb": identB, "identf": identF,
                     "masklow_in": ml, "opw": opw})
    return maps



def build_ffn0():
    p = Prog()
    NSLOT = 1024
    xnfm = p.din("xnfm", [DCH, 128, T], BF16)
    lvrow_i = p.din("lvrow", [1, T], F32)
    lvparts_i = p.din("lvparts", [1, 8], F32)
    wqT = p.din("wqT", [128, DCH, 64], F32)
    wrT = p.din("wrT", [64, 48], F32)
    keysT = p.din("keysT", [64, NSLOT], F32)
    logrel = p.din("logrel", [1, NSLOT], F32)
    wu1 = p.din("wu1", [8, 128, DCH, 128], F32)
    bu1 = p.din("bu1", [128, 8], F32)
    wu2T = p.din("wu2T", [128, 8], F32)
    bu2 = p.din("bu2", [1, 1], F32)
    gw1 = p.din("gw1", [1, 1], F32)
    gb = p.din("gb", [1, 1], F32)
    vals = p.din("vals", [DCH, 128, 8, 128], F32)
    wvdT = p.din("wvdT", [2, 128, DCH, 128], F32)
    wvuT = p.din("wvuT", [DCH, 128, 2, 128], F32)
    identb = p.din("identb", [128, 128], BF16)
    g0fm_o = p.dout("g0fm", [DCH, 128, T], BF16)

    wtsb = p.sb("wtsb", [128, 8 * T], BF16)
    identS = p.sb("identS", [128, 128], BF16)
    minv = p.sb("minv", [1, 1], F32)
    gwsb = p.sb("gwsb", [1, 1], F32)
    gbsb = p.sb("gbsb", [1, 1], F32)
    bu2sb = p.sb("bu2sb", [1, 1], F32)
    bu1sb = p.sb("bu1sb", [128, 8], F32)

    gscr = p.nc.dram_tensor("gscr", [1, T], F32, kind="Internal").ap()

    p.build()
    for (dst, srcap) in [(identS[:], identb), (gwsb[:], gw1), (gbsb[:], gb),
                         (bu2sb[:], bu2), (bu1sb[:], bu1)]:
        p.dma("sy", dst, srcap, sem_name="kq")
    tok_k = [("kq", p.cnt["kq"])]

    outer = p.stack.enter_context(p.scoped.__wrapped__(p)) if False else None
    from contextlib import ExitStack
    es_outer = p.stack.enter_context(ExitStack())
    def osb(name, shape, dtype):
        return es_outer.enter_context(p.nc.sbuf_tensor(name, shape, dtype))
    queryfm = osb("queryfm", [64, T], BF16)
    wrsb = osb("wrsb", [64, 48], BF16)
    keysb = osb("keysb", [64, NSLOT], BF16)
    rkfm = osb("rkfm", [48, NSLOT], BF16)
    rqfm = osb("rqfm", [48, T], BF16)
    h1sb = osb("h1sb", [128, 8 * T], BF16)
    for (dst, srcap) in [(wrsb[:], wrT), (keysb[:], keysT)]:
        p.dma("gp", dst, srcap, sem_name="kqg")
    tok_kg = ("kqg", p.cnt["kqg"])

    with p.scoped() as s:
        xnsb = s.sb("xnsb", [128, DCH * 1024], BF16)
        wqsb = s.sb("wqsb", [128, DCH * 64], BF16)
        gsc = [s.sb(f"gsc{j}", [128, 512], F32) for j in range(3)]
        wusl = [s.sb(f"wusl{j}", [128, DCH * 128], BF16) for j in range(2)]
        psq = [s.ps(f"psq{i}", [64, 512]) for i in range(2)]
        psA = [s.ps(f"ffpA{i}", [128, 512]) for i in range(2)]
        psB = [s.ps(f"ffpB{i}", [128, 512]) for i in range(2)]
        HT = 1024
        tok_wq = p.dma("gp", wqsb[:].rearrange("p (d q) -> p d q", d=DCH), wqT, sem_name="wqq")
        ev_q = {}
        mm_half = {}
        for half in range(2):
            xntok = None
            for sp in range(4):
                wld = [mm_half[half - 1]] if (half >= 1 and sp == 0) else []
                src_ = xnfm[sp * 8:(sp + 1) * 8, :, half * HT:(half + 1) * HT].rearrange("d p t -> p d t")
                dst_ = xnsb[:, sp * 8 * HT:(sp + 1) * 8 * HT].rearrange("p (d t) -> p d t", d=8)
                p.dma("sy", dst_, src_, waits=wld, sem_name="xnq")
            xntok = ("xnq", p.cnt["xnq"])
            for tb in range(2):
                fns = [(lambda e, dc=dc, tb=tb: e.matmul(
                    psq[tb][:, :], wqsb[:, dc * 64:(dc + 1) * 64],
                    xnsb[:, dc * HT + tb * 512:dc * HT + (tb + 1) * 512],
                    start=(dc == 0), stop=(dc == DCH - 1))) for dc in range(DCH)]
                w = [xntok, tok_wq]
                if (half, tb) != (0, 0):
                    w.append(ev_q.get((half, tb - 1), ev_q.get((half - 1, 1))))
                t_mm = p.run_group("te", fns, waits=w)
                ev_q[(half, tb)] = p.run("ve", lambda e, half=half, tb=tb: e.tensor_copy(
                    out=queryfm[:, half * HT + tb * 512:half * HT + (tb + 1) * 512],
                    in_=psq[tb][:, :]), waits=[t_mm])
            wtok, mmtok, evtok = {}, {}, {}
            for o in range(8):
                wd = [mmtok[o - 2]] if o >= 2 else ([mm_half[half - 1]] if half >= 1 else [])
                wtok[o] = p.dma("gp", wusl[o % 2][:], wu1[o].rearrange("p d q -> p (d q)"),
                                waits=wd, sem_name=f"wuq{o % 2}")
                pset = (psA, psB)[o % 2]
                w = [wtok[o], xntok]
                if o >= 2:
                    w.append(evtok[o - 2])
                elif half == 1:
                    w.append(ev_prev)
                fns = []
                for dc in range(DCH):
                    for tb in range(2):
                        fns.append(lambda e, dc=dc, tb=tb, o=o: e.matmul(
                            pset[tb][:, :], wusl[o % 2][:, dc * 128:(dc + 1) * 128],
                            xnsb[:, dc * HT + tb * 512:dc * HT + (tb + 1) * 512],
                            start=(dc == 0), stop=(dc == DCH - 1)))
                mmtok[o] = p.run_group("te", fns, waits=w)
                last = None
                for tb in range(2):
                    pm = pset[tb]
                    off = o * T + half * HT + tb * 512
                    t0 = p.run("ve", lambda e, pm=pm, o=o: e.tensor_scalar_add(
                        gsc[0][:], pm[:, :], bu1sb[:, o:o + 1]), waits=[mmtok[o]] + tok_k)
                    t1 = p.run("se", lambda e: e.activation(out=gsc[1][:], in_=gsc[0][:], func=AF.Square),
                               waits=[t0])
                    t2 = p.run("ve", lambda e: e.tensor_scalar(gsc[1][:], gsc[1][:], 0.044715, 1.0,
                                                               op0=ALU.mult, op1=ALU.add), waits=[t1])
                    t3 = p.run("ve", lambda e: e.tensor_mul(gsc[1][:], gsc[1][:], gsc[0][:]))
                    t4 = p.run("se", lambda e: e.activation(out=gsc[2][:], in_=gsc[1][:], func=AF.Tanh,
                                                            scale=0.7978845608028654), waits=[t3])
                    t5 = p.run("ve", lambda e: e.tensor_scalar(gsc[2][:], gsc[2][:], 0.5, 0.5,
                                                               op0=ALU.mult, op1=ALU.add), waits=[t4])
                    last = p.run("ve", lambda e, off=off: e.tensor_mul(
                        h1sb[:, off:off + 512], gsc[2][:], gsc[0][:]))
                evtok[o] = last
            ev_prev = evtok[7]
            mm_half[half] = mmtok[7]
        t_rk = {}
        for nb in range(2):
            t_mm = p.run("te", lambda e, nb=nb: e.matmul(
                psq[nb][0:48, :], wrsb[:], keysb[:, nb * 512:(nb + 1) * 512],
                start=True, stop=True), waits=[tok_kg, ev_q[(1, nb)], mm_half[1]], self_sync=False)
            t_rk[nb] = p.run("ve", lambda e, nb=nb: e.tensor_copy(
                out=rkfm[:, nb * 512:(nb + 1) * 512], in_=psq[nb][0:48, :]), waits=[t_mm])
        t_q_all0 = ("ve", p.cnt["ve"])
        for tb in range(4):
            t_mm = p.run("te", lambda e, tb=tb: e.matmul(
                psq[tb % 2][0:48, :], wrsb[:], queryfm[:, tb * 512:(tb + 1) * 512],
                start=True, stop=True), waits=[t_rk.get(tb, ("ve", p.cnt["ve"]))], self_sync=False)
            p.run("ve", lambda e, tb=tb: e.tensor_copy(
                out=rqfm[:, tb * 512:(tb + 1) * 512], in_=psq[tb % 2][0:48, :]), waits=[t_mm])
        t_q_all = ("ve", p.cnt["ve"])

    p.phase_barrier()
    with p.scoped() as s:
        lrrep = s.sb("lrrep", [128, NSLOT], F32)
        lrow = s.sb("lrow", [1, T], F32)
        urow = s.sb("urow", [1, T], F32)
        trow = s.sb("trow", [1, T], F32)
        lvrow = s.sb("lvrow_sb", [1, T], F32)
        lvp = s.sb("lvp", [1, 8], F32)
        for (dst, srcap) in [(lrrep[:], logrel.to_broadcast([128, NSLOT])),
                             (lvrow[:], lvrow_i), (lvp[:], lvparts_i)]:
            p.dma("sy", dst, srcap, sem_name="kq2")
        tok_k2 = [("kq2", p.cnt["kq2"])]
        scr = s.sb("scr", [128, NSLOT], F32)
        top8 = s.sb("top8", [128, 8], F32)
        mask8 = s.sb("mask8", [128, NSLOT], F32)
        sc2 = s.sb("sc2", [128, NSLOT], F32)
        mcol = s.sb("mcol", [128, 1], F32)
        zcol = s.sb("zcol", [128, 1], F32)
        esb = s.sb("esb", [128, NSLOT], F32)
        wsb = s.sb("wsb", [128, NSLOT], BF16)
        psr = [s.ps(f"psr{i}", [128, 512]) for i in range(2)]
        ps2 = [s.ps(f"ps2{i}", [128, 512]) for i in range(2)]
        pst = s.ps("pstw", [128, 512], BF16)
        SQ48 = float(1.0 / np.sqrt(48.0))
        prev = {}
        for i in range(TT):
            wv = [("vscr", p.cnt.get("vscr", 0))]
            fns = [(lambda e, nb=nb, i=i: e.matmul(
                psr[nb][:, :], rqfm[:, i * 128:(i + 1) * 128], rkfm[:, nb * 512:(nb + 1) * 512],
                start=True, stop=True)) for nb in range(2)]
            t_smm = p.run_group("te", fns, waits=wv)
            t_scr = p.runs("vscr", [(lambda e, nb=nb: e.scalar_tensor_tensor(
                scr[:, nb * 512:(nb + 1) * 512], psr[nb][:, :], SQ48, lrrep[:, nb * 512:(nb + 1) * 512],
                op0=ALU.mult, op1=ALU.add)) for nb in range(2)], waits=[t_smm] + tok_k2)
            t_t8 = p.run("vscr", lambda e: e.max(out=top8[:], in_=scr[:]))
            t_msk = p.run("vscr", lambda e: e.tensor_single_scalar(mask8[:], scr[:], top8[:, 7:8], op=ALU.is_ge))
            wv = [("vsc2", p.cnt.get("vsc2", 0))]
            fns = [(lambda e, nb=nb, i=i: e.matmul(
                ps2[nb][:, :], queryfm[:, i * 128:(i + 1) * 128], keysb[:, nb * 512:(nb + 1) * 512],
                start=True, stop=True)) for nb in range(2)]
            t_qmm = p.run_group("te", fns, waits=wv)
            t_sc2 = p.runs("vsc2", [(lambda e, nb=nb: e.scalar_tensor_tensor(
                sc2[:, nb * 512:(nb + 1) * 512], ps2[nb][:, :], 0.125, lrrep[:, nb * 512:(nb + 1) * 512],
                op0=ALU.mult, op1=ALU.add)) for nb in range(2)], waits=[t_qmm] + tok_k2)
            fns = [lambda e: e.tensor_scalar_add(sc2[:], sc2[:], 1e30),
                   lambda e: e.tensor_mul(sc2[:], sc2[:], mask8[:]),
                   lambda e: e.tensor_scalar_add(sc2[:], sc2[:], -1e30),
                   lambda e: e.reduce_max(mcol[:], sc2[:], axis=AX.X),
                   lambda e: e.tensor_scalar_mul(mcol[:], mcol[:], -1.0)]
            t_pre = p.runs("vsc2", fns, waits=[t_msk])
            t_ex = p.run("se", lambda e: e.activation(out=esb[:], in_=sc2[:], func=AF.Exp,
                                                      bias=mcol[:], accum_out=zcol[:]), waits=[t_pre])
            t_rc = p.run("vsc2", lambda e: e.reciprocal(zcol[:], zcol[:]), waits=[t_ex])
            t_w = p.run("vsc2", lambda e: e.tensor_scalar_mul(wsb[:], esb[:], zcol[:]))
            for nb2 in range(2):
                wv = [t_w, tok_k[0]]
                if ("wtp",) in prev:
                    wv.append(prev[("wtp",)])
                fns = [(lambda e, nc4=nb2 * 4 + j: e.transpose(
                    pst[:, (nc4 % 4) * 128:(nc4 % 4 + 1) * 128],
                    wsb[:, nc4 * 128:(nc4 + 1) * 128], identS[:])) for j in range(4)]
                t_tp = p.runs("te", fns, waits=wv, self_sync=False)
                prev[("wtp",)] = p.run("vsc2", lambda e, nb2=nb2, i=i: e.tensor_copy(
                    out=wtsb[:, :].rearrange("p (n t) -> p n t", n=8)[:, nb2 * 4:(nb2 + 1) * 4,
                                                                     i * 128:(i + 1) * 128],
                    in_=pst[:, :].rearrange("p (a k) -> p a k", a=4)), waits=[t_tp])

        psu = [psr[0], psr[1], ps2[0], ps2[1]]
        wu2sb = s.sb("wu2sb", [128, 8], BF16)
        p.dma("gp", wu2sb[:], wu2T, sem_name="kqg")
        tok_w2 = ("kqg", p.cnt["kqg"])
        wfree = [("vscr", p.cnt["vscr"]), ("vsc2", p.cnt["vsc2"])]
        for tb in range(4):
            fns = [(lambda e, ch=ch, tb=tb: e.matmul(
                psu[tb][0:1, :], wu2sb[:, ch:ch + 1],
                h1sb[:, ch * T + tb * 512:ch * T + (tb + 1) * 512],
                start=(ch == 0), stop=(ch == 7))) for ch in range(8)]
            t_mm = p.run_group("te", fns, waits=[tok_w2] + wfree)
            p.run("vu", lambda e, tb=tb: e.tensor_copy(out=lrow[:, tb * 512:(tb + 1) * 512],
                                                        in_=psu[tb][0:1, :]), waits=[t_mm])
        t_l = p.run("vu", lambda e: e.tensor_scalar_add(lrow[:], lrow[:], bu2sb[:, :]), waits=tok_k)
        t_sg = p.run("se", lambda e: e.activation(out=lrow[:], in_=lrow[:], func=AF.Sigmoid), waits=[t_l])
        t_s1 = p.run("vu", lambda e: e.reduce_sum(minv[:], lvp[:], axis=AX.X), waits=tok_k2)
        t_s2 = p.run("vu", lambda e: e.tensor_scalar(minv[:], minv[:], 1.0 / 16384.0, 1e-6,
                                                      op0=ALU.mult, op1=ALU.add))
        t_s3 = p.run("vu", lambda e: e.reciprocal(minv[:], minv[:]))
        t_nv = p.run("vu", lambda e: e.tensor_scalar_mul(trow[:], lvrow[:], minv[:, :]))
        t_u1 = p.run("vu", lambda e: e.tensor_scalar_mul(trow[:], trow[:], 0.5))
        t_u2 = p.run("vu", lambda e: e.scalar_tensor_tensor(urow[:], lrow[:], 2.5, trow[:],
                                                             op0=ALU.mult, op1=ALU.add), waits=[t_sg])
        t_u3 = p.run("vu", lambda e: e.tensor_scalar(urow[:], urow[:], 5.0, 0.0, op0=ALU.min, op1=ALU.max))
        t_u4 = p.run("vu", lambda e: e.tensor_scalar(urow[:], urow[:], 0.5, 1.0 / 1.5,
                                                      op0=ALU.subtract, op1=ALU.mult))
        t_u5 = p.run("vu", lambda e: e.tensor_scalar(urow[:], urow[:], 1.0, 0.0, op0=ALU.min, op1=ALU.max))
        t_g1 = p.run("se", lambda e: e.activation(out=urow[:], in_=urow[:], func=AF.Sigmoid,
                                                  scale=gwsb[:, :], bias=gbsb[:, :]), waits=[t_u5] + tok_k)
        t_g2 = p.run("vu", lambda e: e.tensor_single_scalar(trow[:], urow[:], 0.05, op=ALU.is_ge),
                     waits=[t_g1])
        t_g3 = p.run("vu", lambda e: e.tensor_mul(urow[:], urow[:], trow[:]))
        tok_gw = p.dma("sy", gscr, urow[:], waits=[t_g3], sem_name="gwq")

    es_outer.close()
    p.phase_barrier()
    with p.scoped() as s:
        grep = s.sb("grep", [128, T], F32)
        tok_gr = p.dma("sy", grep[:], gscr.to_broadcast([128, T]), waits=[("gwq", 16)], sem_name="grq")
        auxsb = s.sb("auxsb", [128, DCH * T], BF16)
        psA = [s.ps(f"fxpA{i}", [128, 512]) for i in range(4)]
        psB = [s.ps(f"fxpB{i}", [128, 512]) for i in range(4)]

        def evac_aux(o, pset, tok_mm):
            return p.runs("ve", [(lambda e, tb=tb, o=o: e.tensor_copy(
                out=auxsb[:, o * T + tb * 512:o * T + (tb + 1) * 512], in_=pset[tb][:, :]))
                for tb in range(4)], waits=[tok_mm])

        emit_stream_mm(p, "aux", vals, DCH, 8, wtsb, ("vsc2", p.cnt["vsc2"]), evac_aux, (psA, psB), alloc=s)

        avdsb = s.sb("avdsb", [128, 2 * T], BF16)

        def evac_avd(o, pset, tok_mm):
            return p.runs("ve", [(lambda e, tb=tb, o=o: e.tensor_copy(
                out=avdsb[:, o * T + tb * 512:o * T + (tb + 1) * 512], in_=pset[tb][:, :]))
                for tb in range(4)], waits=[tok_mm])

        emit_stream_mm(p, "avd", wvdT, 2, DCH, auxsb, ("ve", p.cnt["ve"]), evac_avd, (psA, psB), alloc=s)

        xnt = [s.sb(f"xnt{j}", [128, T], BF16) for j in range(2)]
        g1f = s.sb("g1f", [128, 512], F32)
        xl_tok = {}

        sil_prev = [None]

        def evac_g(o, pset, tok_mm):
            b = o % 2
            w = [xl_tok[o]] if o in xl_tok else []
            last = None
            for tb in range(4):
                wv = ([tok_mm, tok_gr] + w if tb == 0 else [tok_mm])
                if sil_prev[0] is not None:
                    wv = wv + [sil_prev[0]]
                t1 = p.run("ve", lambda e, tb=tb, b=b: e.tensor_mul(
                    g1f[:], pset[tb][:, :], grep[:, tb * 512:(tb + 1) * 512]),
                    waits=wv)
                t2 = p.run("ve", lambda e, tb=tb, b=b: e.tensor_add(
                    g1f[:], g1f[:], xnt[b][:, tb * 512:(tb + 1) * 512]))
                last = p.run("se", lambda e, tb=tb, o=o: e.activation(
                    out=auxsb[:, o * T + tb * 512:o * T + (tb + 1) * 512],
                    in_=g1f[:], func=AF.Silu), waits=[t2])
                sil_prev[0] = last
            tok_o = p.dma("sy", g0fm_o[o], auxsb[:, o * T:(o + 1) * T], waits=[last], sem_name=f"goq{o % 2}")
            return last

        def pre_mm_g(o):
            b = o % 2
            w = [("goq" + str(b), p.cnt.get("goq" + str(b), 0))]
            xl_tok[o] = p.dma("sy", xnt[b][:], xnfm[o], waits=w, sem_name=f"xlt{b}")

        wvsl = [s.sb(f"wvsl{j}", [128, 2 * 128], BF16) for j in range(2)]
        wtok, mmtok, evtok = {}, {}, {}
        for o in range(DCH):
            wd = [mmtok[o - 2]] if o >= 2 else []
            wtok[o] = p.dma("gp", wvsl[o % 2][:], wvuT[o].rearrange("p d q -> p (d q)"),
                            waits=wd, sem_name=f"wvq{o % 2}")
            pre_mm_g(o)
            pset = (psA, psB)[o % 2]
            w = [wtok[o], ("ve", p.cnt["ve"])] if o == 0 else [wtok[o]]
            if o >= 2:
                w.append(evtok[o - 2])
            fns = []
            for dc in range(2):
                for tb in range(4):
                    fns.append(lambda e, dc=dc, tb=tb, o=o: e.matmul(
                        pset[tb][:, :], wvsl[o % 2][:, dc * 128:(dc + 1) * 128],
                        avdsb[:, dc * T + tb * 512:dc * T + (tb + 1) * 512],
                        start=(dc == 0), stop=(dc == 1)))
            mmtok[o] = p.run_group("te", fns, waits=w)
            evtok[o] = evac_g(o, pset, mmtok[o])

    p.finish([("goq0", p.cnt["goq0"]), ("goq1", p.cnt["goq1"])])
    return p


def host_ffn0_inputs(inputs, xnfm_list, lvrow_list, lvparts):
    wqT = np.ascontiguousarray(inputs["L0_ffn_Wq"].T.reshape(DCH, 128, 64).transpose(1, 0, 2)).astype(np.float32)
    wrT = np.ascontiguousarray(inputs["L0_ffn_Wr"].T).astype(np.float32)
    keysT = np.ascontiguousarray(inputs["L0_ffn_aux_keys"].T).astype(np.float32)
    logrel = np.ascontiguousarray(inputs["L0_ffn_log_rel"].reshape(1, 1024)).astype(np.float32)
    wu1 = np.ascontiguousarray(inputs["L0_ffn_Wu1"].T.reshape(DCH, 128, 8, 128).transpose(2, 1, 0, 3)).astype(np.float32)
    bu1 = np.ascontiguousarray(inputs["L0_ffn_bu1"].reshape(8, 128).T).astype(np.float32)
    wu2T = np.ascontiguousarray(inputs["L0_ffn_Wu2"].reshape(8, 128).T).astype(np.float32)
    bu2 = np.asarray(inputs["L0_ffn_bu2"]).reshape(1, 1).astype(np.float32)
    gw1 = np.asarray(inputs["L0_ffn_gate_w1"]).reshape(1, 1).astype(np.float32)
    gb = np.asarray(inputs["L0_ffn_gate_bias"]).reshape(1, 1).astype(np.float32)
    vals = np.ascontiguousarray(np.asarray(inputs["L0_ffn_aux_values"]).reshape(8, 128, DCH, 128)
                                .transpose(2, 1, 0, 3)).astype(np.float32)
    wvdT = np.ascontiguousarray(inputs["L0_ffn_Wvd"].T.reshape(DCH, 128, 2, 128).transpose(2, 1, 0, 3)).astype(np.float32)
    wvuT = np.ascontiguousarray(inputs["L0_ffn_Wvu"].T.reshape(2, 128, DCH, 128).transpose(2, 1, 0, 3)).astype(np.float32)
    identB = np.eye(128, dtype=np.float32).astype(BF)
    maps = []
    for c, xnfm in enumerate(xnfm_list):
        maps.append({"xnfm": xnfm, "lvrow": lvrow_list[c], "lvparts": lvparts,
                     "wqT": wqT, "wrT": wrT, "keysT": keysT, "logrel": logrel, "wu1": wu1,
                     "bu1": bu1, "wu2T": wu2T, "bu2": bu2, "gw1": gw1, "gb": gb, "vals": vals,
                     "wvdT": wvdT, "wvuT": wvuT, "identb": identB})
    return maps



def build_res0():
    p = Prog()
    x = p.din("x", [T, DM], F32)
    s0fm_i = p.din("s0fm", [DCH, 128, T], BF16)
    g0fm_i = p.din("g0fm", [DCH, 128, T], BF16)
    wqkv = p.din("wqkv", [34, 128, DCH, 128], F32)
    routT = p.din("routT", [128, DCH, 4], F32)
    acat = p.din("acat", [128, DCH, 64], F32)
    ball = p.din("ball", [DCH, 64, 128], F32)
    identb = p.din("identb", [128, 128], BF16)
    x1_o = p.dout("x1", [T, DM], F32)
    xn1fm_o = p.dout("xn1fm", [DCH, 128, T], BF16)
    xn1lofm_o = p.dout("xn1lofm", [DCH, 128, T], BF16)
    g1fm_o = p.dout("g1fm", [DCH, 128, T], BF16)
    qkvfm_o = p.dout("qkvfm", [34, 128, T], BF16)

    identS = p.sb("identS", [128, 128], BF16)
    gscr2 = p.nc.dram_tensor("gscr2", [4, T], F32, kind="Internal").ap()

    p.build()
    p.dma("sy", identS[:], identb, sem_name="kq")
    tok_k = ("kq", 16)

    with p.scoped() as s:
        ssl = [s.sb(f"ssl{j}", [128, DM], BF16) for j in range(2)]
        gsl = [s.sb(f"gsl{j}", [128, DM], BF16) for j in range(2)]
        xt = s.sb("xt", [128, DM], F32)
        sgt = s.sb("sgt", [128, 512], BF16)
        sgtok = s.sb("sgtok", [128, DM], F32)
        xnb = s.sb("xnb", [128, DM], BF16)
        xnf = s.sb("xnf", [128, DM], F32)
        xlob = s.sb("xlob", [128, DM], BF16)
        bst = s.sb("bst", [128, 6 * 8], F32)
        mv = s.sb("mv", [128, 2], F32)
        t1c = s.sb("t1c", [128, 1], F32)
        rstd = s.sb("rstd", [128, 1], F32)
        xev = [s.sb(f"xev{j}", [128, 512], BF16) for j in range(2)]
        pstA = s.ps("pstA", [128, 512], BF16)
        pstB = s.ps("pstB", [128, 512], BF16)

        prev = {}
        for i in range(TT):
            b = i % 2
            wl = [prev[("sgp2", b)]] if ("sgp2", b) in prev else []
            p.dma("gp", ssl[b][:].rearrange("p (d t) -> p d t", d=DCH),
                  s0fm_i[:, :, i * 128:(i + 1) * 128].rearrange("d p t -> p d t"),
                  waits=wl, sem_name=f"ldq{b}")
            p.dma("gp", gsl[b][:].rearrange("p (d t) -> p d t", d=DCH),
                  g0fm_i[:, :, i * 128:(i + 1) * 128].rearrange("d p t -> p d t"), sem_name=f"ldq{b}")
            tok_ld = (f"ldq{b}", p.cnt[f"ldq{b}"])
            wx = [prev[("xn", i - 1)], ("x1q", 16 * i)] if i >= 1 else []
            tok_x = p.dma("sy", xt[:], x[i * 128:(i + 1) * 128, :], waits=wx, sem_name="xq")
            for dcg in range(8):
                wv = [tok_ld] + ([prev[("sgp",)]] if ("sgp",) in prev else [])
                t_sg = p.run("ve", lambda e, dcg=dcg, b=b: e.tensor_tensor(
                    sgt[:].rearrange("p (a k) -> p a k", a=4),
                    ssl[b][:].rearrange("p (d t) -> p d t", d=DCH)[:, dcg * 4:(dcg + 1) * 4, :],
                    gsl[b][:].rearrange("p (d t) -> p d t", d=DCH)[:, dcg * 4:(dcg + 1) * 4, :],
                    op=ALU.mult), waits=wv)
                q = dcg % 2
                ps = (pstA, pstB)[q]
                wv = [t_sg, tok_k]
                if ("sgev", q) in prev:
                    wv.append(prev[("sgev", q)])
                fns = [(lambda e, j=j, ps=ps: e.transpose(
                    ps[:, j * 128:(j + 1) * 128], sgt[:, j * 128:(j + 1) * 128], identS[:]))
                    for j in range(4)]
                t_tp = p.runs("te", fns, waits=wv, self_sync=False)
                prev[("sgp",)] = t_tp
                prev[("sgev", q)] = p.run("ve", lambda e, dcg=dcg, ps=ps: e.tensor_copy(
                    out=sgtok[:, dcg * 512:(dcg + 1) * 512], in_=ps[:, :]), waits=[t_tp])
            prev[("sgp2", b)] = prev[("sgp",)]
            t_x1 = p.run("ve", lambda e: e.tensor_add(xt[:], xt[:], sgtok[:]),
                         waits=[tok_x, prev[("sgev", 0)], prev[("sgev", 1)]])
            tok_x1 = p.dma("sy", x1_o[i * 128:(i + 1) * 128, :], xt[:], waits=[t_x1], sem_name="x1q")
            fns = [(lambda e, g=g: e.bn_stats(out=bst[:, 6 * g:6 * g + 6],
                                              in_=xt[:, 512 * g:512 * (g + 1)])) for g in range(8)]
            fns.append(lambda e: e.bn_aggr(out=mv[:], in_=bst[:]))
            fns.append(lambda e: e.tensor_scalar_add(t1c[:], mv[:, 1:2], EPS_LN))
            t_v = p.runs("ve", fns, waits=[t_x1])
            t_l1 = p.run("se", lambda e: e.activation(out=rstd[:], in_=t1c[:], func=AF.Ln), waits=[t_v])
            t_l2 = p.run("se", lambda e: e.activation(out=rstd[:], in_=rstd[:], func=AF.Exp, scale=-0.5))
            t_xnf = p.run("ve", lambda e: e.tensor_scalar(
                xnf[:], xt[:], mv[:, 0:1], rstd[:], op0=ALU.subtract, op1=ALU.mult), waits=[t_l2])
            t_xn = p.run("ve", lambda e: e.tensor_copy(out=xnb[:], in_=xnf[:]))
            t_lo = p.run("ve", lambda e: e.tensor_sub(xlob[:], xnf[:], xnb[:]))
            prev[("xn", i)] = t_xnf
            for dcg in range(8):
                q = dcg % 2
                ps = (pstA, pstB)[q]
                wv = [t_xn, tok_k]
                if ("sgev", q) in prev:
                    wv.append(prev[("sgev", q)])
                if ("xnev", q) in prev:
                    wv.append(prev[("xnev", q)])
                fns = [(lambda e, dc=dcg * 4 + j, ps=ps: e.transpose(
                    ps[:, (dc % 4) * 128:(dc % 4 + 1) * 128], xnb[:, dc * 128:(dc + 1) * 128], identS[:]))
                    for j in range(4)]
                t_tp = p.runs("te", fns, waits=wv, self_sync=False)
                prev[("sgp",)] = t_tp
                wv = [t_tp]
                if ("xnw", q) in prev:
                    wv.append(prev[("xnw", q)])
                t_e = p.run("ve2", lambda e, ps=ps, q=q: e.tensor_copy(out=xev[q][:], in_=ps[:, :]), waits=wv)
                dst = xn1fm_o[dcg * 4:(dcg + 1) * 4, :, i * 128:(i + 1) * 128].rearrange("a b c -> b a c")
                prev[("xnw", q)] = p.dma("gp", dst, xev[q][:].rearrange("p (a k) -> p a k", a=4),
                                         waits=[t_e], sem_name=f"xnw{q}")
                prev[("xnev", q)] = t_e
            for dcg in range(8):
                q = dcg % 2
                ps = (pstA, pstB)[q]
                wv = [t_lo, tok_k, prev[("xnev", q)]]
                fns = [(lambda e, dc=dcg * 4 + j, ps=ps: e.transpose(
                    ps[:, (dc % 4) * 128:(dc % 4 + 1) * 128], xlob[:, dc * 128:(dc + 1) * 128], identS[:]))
                    for j in range(4)]
                t_tp = p.runs("te", fns, waits=wv, self_sync=False)
                prev[("sgp",)] = t_tp
                wv = [t_tp]
                if ("xnw", q) in prev:
                    wv.append(prev[("xnw", q)])
                t_e = p.run("ve2", lambda e, ps=ps, q=q: e.tensor_copy(out=xev[q][:], in_=ps[:, :]), waits=wv)
                dst = xn1lofm_o[dcg * 4:(dcg + 1) * 4, :, i * 128:(i + 1) * 128].rearrange("a b c -> b a c")
                prev[("xnw", q)] = p.dma("gp", dst, xev[q][:].rearrange("p (a k) -> p a k", a=4),
                                         waits=[t_e], sem_name=f"xnw{q}")
                prev[("xnev", q)] = t_e
        tok_xnw = [("xnw0", p.cnt["xnw0"]), ("xnw1", p.cnt["xnw1"]), ("x1q", p.cnt["x1q"])]

    p.phase_barrier()
    with p.scoped() as so:
        xnsb = so.sb("xnsb2", [128, DCH * T], BF16)
        grep64 = so.sb("grep64", [64, T], BF16)
        midsb = so.sb("midsb", [64, T], BF16)
        xntok = load_xnsb(p, xnsb, xn1fm_o, DCH)

        with p.scoped() as s:
            rsb = s.sb("rsb", [128, DCH * 4], BF16)
            losl = [s.sb("losl0", [128, 8 * T], BF16)] * 2
            lgt = s.sb("lgt", [128, 8], F32)
            top2 = s.sb("top2", [128, 8], F32)
            nv1 = s.sb("nv1", [128, 1], F32)
            egt = s.sb("egt", [128, 4], F32)
            msk = s.sb("msk", [128, 4], F32)
            zc = s.sb("zc", [128, 1], F32)
            gts = s.sb("gts", [128, 4], BF16)
            gtr = s.sb("gtr", [4, T], F32)
            psl = s.ps("psl", [128, 64])
            pslb = s.ps("pslb", [4, 128], BF16)
            tok_r = p.dma("gp", rsb[:].rearrange("p (d q) -> p d q", d=DCH), routT, sem_name="rq")
            lotoks = {}
            mmtoks = {}
            for grp in range(4):
                b = grp % 2
                wl = [mmtoks[grp - 1]] if grp >= 1 else []
                src_ = xn1lofm_o[grp * 8:(grp + 1) * 8].rearrange("d p t -> p d t")
                dst_ = losl[b][:].rearrange("p (d t) -> p d t", d=8)
                lotoks[grp] = p.dma("sy", dst_, src_, waits=wl, sem_name=f"loq{b}")
                fns = []
                for dcg in range(8):
                    dc = grp * 8 + dcg
                    for i in range(TT):
                        fns.append(lambda e, dc=dc, i=i: e.matmul(
                            psl[:, i * 4:(i + 1) * 4], xnsb[:, dc * T + i * 128:dc * T + (i + 1) * 128],
                            rsb[:, dc * 4:(dc + 1) * 4], start=(dc == 0 and i == 0), stop=False))
                    for i in range(TT):
                        fns.append(lambda e, dc=dc, dcg=dcg, grp=grp, b=b, i=i: e.matmul(
                            psl[:, i * 4:(i + 1) * 4], losl[b][:, dcg * T + i * 128:dcg * T + (i + 1) * 128],
                            rsb[:, dc * 4:(dc + 1) * 4], start=False,
                            stop=(dc == DCH - 1 and i == TT - 1)))
                mmtoks[grp] = p.run_group("te", fns, waits=[xntok, lotoks[grp], tok_r])
            t_lmm = mmtoks[3]
            prev = {}
            for i in range(TT):
                fns = [lambda e: e.memset(lgt[:], NEG_BIG),
                       (lambda e, i=i: e.tensor_copy(out=lgt[:, 0:4], in_=psl[:, i * 4:(i + 1) * 4])),
                       lambda e: e.max(out=top2[:], in_=lgt[:]),
                       lambda e: e.tensor_scalar_mul(nv1[:], top2[:, 0:1], -1.0)]
                t_l = p.runs("ve", fns, waits=[t_lmm])
                t_e = p.run("se", lambda e: e.activation(out=egt[:], in_=lgt[:, 0:4], func=AF.Exp,
                                                         bias=nv1[:]), waits=[t_l])
                fns = [lambda e: e.tensor_single_scalar(msk[:], lgt[:, 0:4], top2[:, 1:2], op=ALU.is_ge),
                       lambda e: e.tensor_mul(egt[:], egt[:], msk[:]),
                       lambda e: e.reduce_sum(zc[:], egt[:], axis=AX.X),
                       lambda e: e.reciprocal(zc[:], zc[:]),
                       lambda e: e.tensor_scalar(gts[:], egt[:], zc[:], 4.0, op0=ALU.mult, op1=ALU.mult)]
                t_g = p.runs("ve", fns, waits=[t_e])
                wv = [t_g, tok_k] + ([prev[("gtp",)]] if ("gtp",) in prev else [])
                t_tp = p.run("te", lambda e: e.transpose(pslb[:, :], gts[:], identS[:]), waits=wv,
                             self_sync=False)
                prev[("gtp",)] = p.run("ve", lambda e, i=i: e.tensor_copy(
                    out=gtr[:, i * 128:(i + 1) * 128], in_=pslb[:, :]), waits=[t_tp])
            tok_gw = p.dma("sy", gscr2, gtr[:], waits=[prev[("gtp",)]], sem_name="gwq")
            for ee in range(4):
                p.dma("gp", grep64[16 * ee:16 * (ee + 1), :],
                      gscr2[ee:ee + 1, :].to_broadcast([16, T]), waits=[tok_gw] if ee == 0 else [],
                      sem_name="grq")
        tok_gr = ("grq", 64)

        p.phase_barrier()
        with p.scoped() as s:
            asb = s.sb("asb", [128, DCH * 64], BF16)
            bsl = [s.sb(f"bsl{j}", [64, 128], BF16) for j in range(2)]
            g1sc = [s.sb(f"g1sc{j}", [128, T], BF16) for j in range(2)]
            psm4 = [s.ps(f"psm4{i}", [64, 512]) for i in range(4)]
            psD = [s.ps(f"psD{i}", [128, 512]) for i in range(4)]
            tok_a = p.dma("gp", asb[:].rearrange("p (d q) -> p d q", d=DCH), acat, sem_name="rq")
            for tb in range(4):
                fns = [(lambda e, dc=dc, tb=tb: e.matmul(
                    psm4[tb][:, :], asb[:, dc * 64:(dc + 1) * 64],
                    xnsb[:, dc * T + tb * 512:dc * T + (tb + 1) * 512],
                    start=(dc == 0), stop=(dc == DCH - 1))) for dc in range(DCH)]
                t_mm = p.run_group("te", fns, waits=[xntok, tok_a])
                p.run("vm", lambda e, tb=tb: e.tensor_copy(out=midsb[:, tb * 512:(tb + 1) * 512],
                                                           in_=psm4[tb][:, :]), waits=[t_mm])
            t_ms = p.run("vm", lambda e: e.tensor_mul(midsb[:], midsb[:], grep64[:]), waits=[tok_gr])
            prev = {}
            for o in range(DCH):
                b = o % 2
                tok_b = p.dma("gp", bsl[b][:], ball[o],
                              waits=[prev[("dmm", b)]] if ("dmm", b) in prev else [], sem_name=f"bwq{b}")
                wv = [tok_b, t_ms] + ([prev[("dev",)]] if ("dev",) in prev else [])
                fns = [(lambda e, tb=tb, b=b: e.matmul(
                    psD[tb][:, :], bsl[b][:], midsb[:, tb * 512:(tb + 1) * 512],
                    start=True, stop=True)) for tb in range(4)]
                t_mm = p.run_group("te", fns, waits=wv)
                prev[("dmm", b)] = t_mm
                wv = [t_mm] + ([prev[("g1w", b)]] if ("g1w", b) in prev else [])
                t_ev = p.runs("se", [(lambda e, tb=tb, b=b: e.activation(
                    out=g1sc[b][:, tb * 512:(tb + 1) * 512], in_=psD[tb][:, :], func=AF.Silu))
                    for tb in range(4)], waits=wv)
                prev[("dev",)] = t_ev
                prev[("g1w", b)] = p.dma("sy", g1fm_o[o], g1sc[b][:], waits=[t_ev], sem_name=f"g1q{b}")
            tok_g1 = [("g1q0", p.cnt["g1q0"]), ("g1q1", p.cnt["g1q1"])]

        p.phase_barrier()
        with p.scoped() as s:
            psA = [s.ps(f"qvA{i}", [128, 512]) for i in range(4)]
            psB = [s.ps(f"qvB{i}", [128, 512]) for i in range(4)]
            qsl = [s.sb(f"qsl{j}", [128, DCH * 128], BF16) for j in range(2)]
            qout = [s.sb(f"qout{j}", [128, T], BF16) for j in range(2)]

            def evac_qkv(o, pset, tok_mm):
                b = o % 2
                wv = [tok_mm] + ([otok[o - 2]] if o >= 2 else [])
                t_ev = p.runs("ve", [(lambda e, tb=tb, b=b: e.tensor_copy(
                    out=qout[b][:, tb * 512:(tb + 1) * 512], in_=pset[tb][:, :])) for tb in range(4)],
                    waits=wv)
                otok[o] = p.dma("sy", qkvfm_o[o], qout[b][:], waits=[t_ev], sem_name=f"qoq{b}")
                return t_ev

            otok = {}
            emit_stream_mm(p, "qkv", wqkv, 34, DCH, xnsb, xntok, evac_qkv, (psA, psB), alloc=s)
            tok_qo = [("qoq0", p.cnt["qoq0"]), ("qoq1", p.cnt["qoq1"])]

    p.finish(tok_qo + tok_g1 + tok_xnw)
    return p


def host_res0_inputs(inputs, x_full, s0_list, g0_list):
    wqkv = np.ascontiguousarray(np.asarray(inputs["L1_attn_Wqkv"]).reshape(DCH, 128, 34, 128)
                                .transpose(2, 1, 0, 3)).astype(np.float32)
    routT = np.ascontiguousarray(np.asarray(inputs["L1_ffn_router_w"]).T.reshape(DCH, 128, 4)
                                 .transpose(1, 0, 2)).astype(np.float32)
    A = np.asarray(inputs["L1_ffn_A"])
    acat = np.ascontiguousarray(A.transpose(2, 0, 1).reshape(4096, 64).reshape(DCH, 128, 64)
                                .transpose(1, 0, 2)).astype(np.float32)
    B = np.asarray(inputs["L1_ffn_B"])
    ball = np.ascontiguousarray(B.transpose(0, 2, 1).reshape(64, 4096).reshape(64, DCH, 128)
                                .transpose(1, 0, 2)).astype(np.float32)
    identB = np.eye(128, dtype=np.float32).astype(BF)
    maps = []
    for c in range(len(s0_list)):
        maps.append({"x": np.ascontiguousarray(x_full[c * T:(c + 1) * T]).astype(np.float32),
                     "s0fm": s0_list[c], "g0fm": g0_list[c], "wqkv": wqkv, "routT": routT,
                     "acat": acat, "ball": ball, "identb": identB})
    return maps



ATT_W = [64, 256, 512, 512]
ATT_WIN = [536, 2164, 999999, 999999]

def build_attn():
    p = Prog()
    S = 4096
    qsl_i = p.din("qsl", [4, 128, 4 * S], BF16)
    ksl_i = p.din("ksl", [128, 4 * S], BF16)
    vsl_i = p.din("vsl", [128, 4 * S], BF16)
    acol_i = p.din("acol", [128, 4, 32], F32)
    mconst_i = p.din("mconst", [4, 64], F32)
    mb64_i = p.din("mb64", [2, 128, 64], F32)
    mb256_i = p.din("mb256", [2, 128, 256], F32)
    mb512_i = p.din("mb512", [4, 128, 512], F32)
    identb = p.din("identb", [128, 128], BF16)
    attnfm_o = p.dout("attnfm", [4, 128, 4 * S], BF16)

    SCALE = float(128 ** -0.5)
    identS = p.sb("identS", [128, 128], BF16)
    acol = p.sb("acol_sb", [128, 4 * 32], F32)
    mconst = p.sb("mconst_sb", [1, 4 * 64], F32)
    mb64 = p.sb("mb64_sb", [128, 2 * 64], F32)
    mb256 = p.sb("mb256_sb", [128, 2 * 256], F32)
    mb512 = p.sb("mb512_sb", [128, 4 * 512], F32)
    onesb = p.sb("onesb", [128, 1], BF16)
    onesf = p.sb("onesf", [128, 1], F32)

    mscr = p.nc.dram_tensor("mscr", [1, 64], F32, kind="Internal").ap()
    zscr = p.nc.dram_tensor("zscr", [1, 512], F32, kind="Internal").ap()

    p.build()
    p.dma("sy", identS[:], identb, sem_name="kq")
    p.dma("sy", acol[:].rearrange("p (s t) -> p s t", s=4), acol_i, sem_name="kq")
    p.dma("sy", mconst[:].rearrange("a (s q) -> a s q", s=4), mconst_i.unsqueeze(0), sem_name="kq")
    p.dma("sy", mb64[:].rearrange("p (a w) -> p a w", a=2), mb64_i.rearrange("a p w -> p a w"), sem_name="kq")
    p.dma("sy", mb256[:].rearrange("p (a w) -> p a w", a=2), mb256_i.rearrange("a p w -> p a w"), sem_name="kq")
    p.dma("sy", mb512[:].rearrange("p (a w) -> p a w", a=4), mb512_i.rearrange("a p w -> p a w"), sem_name="kq")
    tok_k = ("kq", p.cnt["kq"])
    t_ones = p.run("ve", lambda e: e.memset(onesb[:], 1.0))
    t_onesf = p.run("ve", lambda e: e.memset(onesf[:], 1.0))

    with p.scoped() as s:
        qb_sb = s.sb("qb_sb", [128, 4 * S], BF16)
        kb_sb = s.sb("kb_sb", [128, S], BF16)
        vb_sb = s.sb("vb_sb", [128, S], BF16)
        vtok = s.sb("vtok", [128, 32 * 128], BF16)
        attnb = s.sb("attnb", [128, 4 * S], BF16)
        esb = [s.sb(f"esb{j}", [128, 512], BF16) for j in range(2)]
        ssb = s.sb("ssb", [128, 512], F32)
        qsq = s.sb("qsq", [128, S], F32)
        qn2 = s.sb("qn2", [1, S], F32)
        kn2 = s.sb("kn2", [1, 1], F32)
        mrow = s.sb("mrow", [1, 64], F32)
        mrep = s.sb("mrep", [128, 64], F32)
        zrow = s.sb("zrow", [1, 512], F32)
        zrep = [s.sb(f"zrep{j}", [128, 512], F32) for j in range(2)]
        bcol = [s.sb(f"bcol{j}", [128, 1], F32) for j in range(2)]

        pssc = [s.ps(f"pssc{j}", [128, 512]) for j in range(2)]
        psz = [s.ps(f"psz{j}", [1, 512]) for j in range(2)]
        pso = [s.ps(f"pso{j}", [128, 512]) for j in range(2)]
        psq2 = s.ps("psq2", [1, 512])
        pst = s.ps("pstv", [128, 512], BF16)

        prev = {}
        for b in range(4):
            bs = slice(b * S, (b + 1) * S)
            wb = [prev[("bdone",)]] if ("bdone",) in prev else []
            tok_q = p.dma("sy", qb_sb[:].rearrange("p (s t) -> p s t", s=4),
                          qsl_i[:, :, bs].rearrange("s p t -> p s t"), waits=wb, sem_name="qld")
            tok_kv = p.dma("gp", kb_sb[:], ksl_i[:, bs], waits=wb, sem_name="kld")
            p.dma("gp", vb_sb[:], vsl_i[:, bs], sem_name="kld")
            tok_kv = ("kld", p.cnt["kld"])
            for g4 in range(8):
                wv = [tok_kv, tok_k] + ([prev[("vtp",)]] if ("vtp",) in prev else [])
                fns = [(lambda e, kt=g4 * 4 + j: e.transpose(
                    pst[:, (kt % 4) * 128:(kt % 4 + 1) * 128], vb_sb[:, kt * 128:(kt + 1) * 128],
                    identS[:])) for j in range(4)]
                t_tp = p.runs("te", fns, waits=wv, self_sync=False)
                prev[("vtp",)] = p.run("ve", lambda e, g4=g4: e.tensor_copy(
                    out=vtok[:, g4 * 512:(g4 + 1) * 512], in_=pst[:, :]), waits=[t_tp])
            t_vtok = prev[("vtp",)]
            t_ks = p.run("se", lambda e: e.activation(out=qsq[:], in_=kb_sb[:], func=AF.Square),
                         waits=[tok_kv])
            wv = [t_ks, t_onesf] + ([prev[("kn2mm",)]] if ("kn2mm",) in prev else [])
            fns = [(lambda e, tb=tb: e.matmul(psq2[:, 0:512], onesf[:].bitcast(F32) if False else onesf[:],
                                              qsq[:, tb * 512:(tb + 1) * 512],
                                              start=(tb == 0), stop=False)) for tb in range(8)]
            fns = [(lambda e, tb=tb: e.matmul(psq2[:, 0:512], onesf[:],
                                              qsq[:, tb * 512:(tb + 1) * 512],
                                              start=True, stop=True)) for tb in range(8)]
            t_kn = None
            for tb in range(8):
                t_mm = p.run("te", fns[tb], waits=wv if tb == 0 else [prev[("knev",)]], self_sync=False)
                prev[("knev",)] = p.run("vn", lambda e, tb=tb: e.tensor_copy(out=zrow[:], in_=psq2[:, 0:512]),
                                        waits=[t_mm])
                if tb == 0:
                    t_kn = p.run("vn", lambda e: e.reduce_max(kn2[:], zrow[:], axis=AX.X))
                else:
                    t_kn = p.run("vn", lambda e: e.reduce_max(zrow[:, 0:1], zrow[:], axis=AX.X))
                    t_kn = p.run("vn", lambda e: e.tensor_max(kn2[:], kn2[:], zrow[:, 0:1]))
            prev[("kn2mm",)] = t_kn
            t_kn2 = p.run("se", lambda e: e.activation(out=kn2[:], in_=kn2[:], func=AF.Sqrt), waits=[t_kn])

            for slot in range(4):
                W = ATT_W[slot]
                NQB = S // W
                qoff = slot * S
                t_qs = p.run("se", lambda e, slot=slot: e.activation(
                    out=qsq[:], in_=qb_sb[:, slot * S:(slot + 1) * S], func=AF.Square),
                    waits=[tok_q, prev[("qn2used",)]] if ("qn2used",) in prev else [tok_q])
                for tb in range(8):
                    wv = [t_qs, t_onesf] + ([prev[("qnev",)]] if ("qnev",) in prev else [])
                    t_mm = p.run("te", lambda e, tb=tb: e.matmul(
                        psq2[:, 0:512], onesf[:], qsq[:, tb * 512:(tb + 1) * 512],
                        start=True, stop=True), waits=wv, self_sync=False)
                    prev[("qnev",)] = p.run("vn", lambda e, tb=tb: e.tensor_copy(
                        out=qn2[:, tb * 512:(tb + 1) * 512], in_=psq2[:, 0:512]), waits=[t_mm])
                t_qn = p.run("vn", lambda e, W=W, NQB=NQB: e.reduce_max(
                    mrow[:, 0:NQB], qn2[:].rearrange("a (q w) -> a q w", w=W), axis=AX.X),
                    waits=[prev[("qnev",)]])
                t_m1 = p.run("se", lambda e, NQB=NQB: e.activation(out=mrow[:, 0:NQB], in_=mrow[:, 0:NQB],
                                                                   func=AF.Sqrt), waits=[t_qn])
                t_m2 = p.run("vn", lambda e, NQB=NQB: e.tensor_scalar_mul(mrow[:, 0:NQB], mrow[:, 0:NQB],
                                                                          kn2[:, :]), waits=[t_m1, t_kn2])
                t_m3 = p.run("vn", lambda e, NQB=NQB, slot=slot: e.scalar_tensor_tensor(
                    mrow[:, 0:NQB], mrow[:, 0:NQB], SCALE, mconst[:, slot * 64:slot * 64 + NQB],
                    op0=ALU.mult, op1=ALU.add), waits=[tok_k])
                wv = [t_m3] + ([prev[("mrd",)]] if ("mrd",) in prev else [])
                tok_mw = p.dma("sy", mscr[:, 0:NQB], mrow[:, 0:NQB], waits=wv, sem_name="mwq")
                wv = [tok_mw] + ([prev[("mused",)]] if ("mused",) in prev else [])
                tok_mr = p.dma("sy", mrep[:, 0:NQB], mscr[:, 0:NQB].to_broadcast([128, NQB]),
                               waits=wv, sem_name="mrq")
                prev[("mrd",)] = tok_mr
                prev[("qn2used",)] = t_qn

                alpha_min = 0.84 ** (slot * 8 + 8)
                for qb in range(NQB):
                    q_lo, q_end = qb * W, qb * W + W - 1
                    kt_hi = q_end // 128
                    kt_lo = 0
                    while (q_lo - (kt_lo * 128 + 127)) * alpha_min > 133:
                        kt_lo += 1
                    kts = list(range(kt_lo, kt_hi + 1))
                    pq = qb % 2
                    first = True
                    for kt in kts:
                        wv = [tok_mr, tok_k] + ([prev[("bused", pq)]] if ("bused", pq) in prev else [])
                        t_b = p.run("vb", lambda e, kt=kt, qb=qb, slot=slot, pq=pq: e.tensor_scalar(
                            bcol[pq][:], acol[:, slot * 32 + kt:slot * 32 + kt + 1],
                            mrep[:, qb:qb + 1], None, op0=ALU.subtract), waits=wv)
                        wv = [tok_q, tok_kv]
                        if ("spq", pq) in prev:
                            wv.append(prev[("spq", pq)])
                        t_sc = p.run("te", lambda e, kt=kt, qb=qb, slot=slot, pq=pq, W=W: e.matmul(
                            pssc[pq][:, 0:W], kb_sb[:, kt * 128:(kt + 1) * 128],
                            qb_sb[:, slot * S + qb * W:slot * S + (qb + 1) * W],
                            start=True, stop=True), waits=wv, self_sync=False)
                        partial = (kt * 128 + 127 > q_end - W + 0) and (kt * 128 + 127 > q_end) or (kt * 128 + 127 > q_lo + 0 and kt * 128 + 127 > q_end)
                        partial = kt * 128 + 127 > q_end
                        partial = (kt * 128 + 127) > q_lo and (kt == kt_hi or (kt * 128 + 127) > q_lo + W - 1) and ((kt * 128 + 127) > q_end or kt == kt_hi)
                        partial = (kt * 128 + 127) > q_lo
                        if partial:
                            off = qb * W - kt * 128
                            if W == 64:
                                mv_ap = mb64[:, (0 if off == 0 else 1) * 64:(0 if off == 0 else 1) * 64 + 64]
                                assert off in (0, 64), (W, off)
                                mv_ap = mb64[:, (off // 64) * 64:(off // 64) * 64 + 64]
                            elif W == 256:
                                assert off in (0, -128), (W, off)
                                mv_ap = mb256[:, (0 if off == 0 else 1) * 256:(0 if off == 0 else 1) * 256 + 256]
                            else:
                                assert off in (0, -128, -256, -384), (W, off)
                                mv_ap = mb512[:, (-off // 128) * 512:(-off // 128) * 512 + 512]
                            t_msk = p.run("vb", lambda e, pq=pq, W=W, mv_ap=mv_ap: e.tensor_add(
                                ssb[:, 0:W], pssc[pq][:, 0:W], mv_ap), waits=[t_sc])
                            exp_in = ssb[:, 0:W]
                            exp_wait = [t_msk, t_b]
                            prev[("spq", pq)] = t_msk
                        else:
                            exp_in = pssc[pq][:, 0:W]
                            exp_wait = [t_sc, t_b]
                        wv = exp_wait + ([prev[("eused", pq)]] if ("eused", pq) in prev else [])
                        t_exp = p.run("se", lambda e, pq=pq, W=W, exp_in=exp_in: e.activation(
                            out=esb[pq][:, 0:W], in_=exp_in, func=AF.Exp, scale=SCALE,
                            bias=bcol[pq][:]), waits=wv)
                        if not partial:
                            prev[("spq", pq)] = t_exp
                        prev[("bused", pq)] = t_exp
                        wz = [t_exp, t_ones]
                        if first and ("zev", pq) in prev:
                            wz.append(prev[("zev", pq)])
                        lastkt = (kt == kts[-1])
                        t_z = p.run("te", lambda e, pq=pq, W=W, first=first, lastkt=lastkt: e.matmul(
                            psz[pq][:, 0:W], onesb[:], esb[pq][:, 0:W],
                            start=first, stop=lastkt), waits=wz, self_sync=False)
                        wo = [t_exp, t_vtok]
                        if first and ("oev", pq) in prev:
                            wo.append(prev[("oev", pq)])
                        t_o = p.run("te", lambda e, pq=pq, W=W, kt=kt, first=first, lastkt=lastkt: e.matmul(
                            pso[pq][:, 0:W], vtok[:, kt * 128:(kt + 1) * 128], esb[pq][:, 0:W],
                            start=first, stop=lastkt), waits=wo, self_sync=False)
                        prev[("eused", pq)] = t_o
                        first = False
                    t_zev = p.run("vz", lambda e, pq=pq, W=W: e.tensor_copy(out=zrow[:, 0:W],
                                                                            in_=psz[pq][:, 0:W]),
                                  waits=[t_z] + ([prev[("zrow",)]] if ("zrow",) in prev else []))
                    prev[("zev", pq)] = t_zev
                    t_zc = p.run("vz", lambda e, W=W: e.tensor_scalar_max(zrow[:, 0:W], zrow[:, 0:W], 1e-35))
                    t_zr = p.run("vz", lambda e, W=W: e.reciprocal(zrow[:, 0:W], zrow[:, 0:W]))
                    prev[("zrow",)] = t_zr
                    wv = [t_zr] + ([prev[("zrd", pq)]] if ("zrd", pq) in prev else [])
                    tok_zw = p.dma("sy", zscr[:, 0:W], zrow[:, 0:W], waits=wv, sem_name="zwq")
                    wv = [tok_zw] + ([prev[("zused", pq)]] if ("zused", pq) in prev else [])
                    tok_zr = p.dma("sy", zrep[pq][:, 0:W], zscr[:, 0:W].to_broadcast([128, W]),
                                   waits=wv, sem_name=f"zrq{pq}")
                    prev[("zrd", pq)] = tok_zr
                    woev = [t_o, tok_zr]
                    if qb == 0 and ("adma", slot) in prev:
                        woev.append(prev[("adma", slot)])
                    t_oev = p.run("vz", lambda e, pq=pq, W=W, slot=slot, qb=qb: e.tensor_mul(
                        attnb[:, slot * S + qb * W:slot * S + (qb + 1) * W],
                        pso[pq][:, 0:W], zrep[pq][:, 0:W]), waits=woev)
                    prev[("oev", pq)] = t_oev
                    prev[("zused", pq)] = t_oev
                wv = [prev[("oev", 0)]]
                if NQB > 1:
                    wv.append(prev[("oev", 1)])
                prev[("adma", slot)] = p.dma("gp", attnfm_o[slot, :, bs],
                                             attnb[:, slot * S:(slot + 1) * S], waits=wv,
                                             sem_name=f"adq{slot}")
            prev[("bdone",)] = prev[("adma", 3)]
        p.finish([(f"adq{s2}", p.cnt[f"adq{s2}"]) for s2 in range(4)])
    return p


def host_attn_inputs(qkvfm_all):
    import math
    S = 4096
    def _alibi_slopes(n):
        cp2 = 2 ** int(math.floor(math.log2(n)))
        base = 2.0 ** (-2.0 ** (-(math.log2(cp2) - 3)))
        slopes = base ** np.arange(1, cp2 + 1, dtype=np.float64)
        if cp2 != n:
            extra_base = 2.0 ** (-2.0 ** (-(math.log2(2 * cp2) - 3)))
            rem = min(cp2, n - cp2)
            extra = extra_base ** np.arange(1, 1 + 2 * rem, 2, dtype=np.float64)
            slopes = np.concatenate([slopes, extra])
        return slopes
    slopes = _alibi_slopes(32)
    ksl = np.ascontiguousarray(qkvfm_all[32])
    vsl = np.ascontiguousarray(qkvfm_all[33])
    identB = np.eye(128, dtype=np.float32).astype(BF)
    kabs = np.arange(S).reshape(32, 128)
    maps = []
    for c in range(NCORE):
        heads = [c, c + 8, c + 16, c + 24]
        qsl = np.ascontiguousarray(np.stack([qkvfm_all[h] for h in heads]))
        acol = np.ascontiguousarray(np.stack([slopes[h] * kabs for h in heads]).transpose(2, 0, 1)).astype(np.float32)
        mconst = np.zeros((4, 64), dtype=np.float32)
        for s2, h in enumerate(heads):
            W = ATT_W[s2]
            nqb = S // W
            mconst[s2, :nqb] = slopes[h] * (np.arange(nqb) * W + W - 1) - 40.0
        mb64 = np.stack([_maskbig(64, off) for off in (0, 64)]).astype(np.float32)
        mb256 = np.stack([_maskbig(256, off) for off in (0, -128)]).astype(np.float32)
        mb512 = np.stack([_maskbig(512, off) for off in (0, -128, -256, -384)]).astype(np.float32)
        maps.append({"qsl": qsl, "ksl": ksl, "vsl": vsl, "acol": acol, "mconst": mconst,
                     "mb64": mb64, "mb256": mb256, "mb512": mb512, "identb": identB})
    return maps


def _maskbig(W, off):
    i = np.arange(128)[:, None]
    j = np.arange(W)[None, :]
    return np.where(i <= off + j, 0.0, NEG_BIG).astype(np.float32)



def build_wo():
    p = Prog()
    attn_i = p.din("attn", [DCH, 128, T], BF16)
    wo = p.din("wo", [DCH, 128, DCH, 128], F32)
    g1fm_i = p.din("g1fm", [DCH, 128, T], BF16)
    x1_i = p.din("x1", [T, DM], F32)
    identb = p.din("identb", [128, 128], BF16)
    out_o = p.dout("out", [T, DM], F32)

    identS = p.sb("identS", [128, 128], BF16)
    sgscr = p.nc.dram_tensor("sgscr", [DCH, 128, T], BF16, kind="Internal").ap()

    p.build()
    p.dma("sy", identS[:], identb, sem_name="kq")
    tok_k = ("kq", 16)

    with p.scoped() as s:
        attnsb = s.sb("attnsb", [128, DCH * T], BF16)
        g1t = [s.sb(f"g1t{j}", [128, T], BF16) for j in range(2)]
        sgo = [s.sb(f"sgo{j}", [128, T], BF16) for j in range(2)]
        psA = [s.ps(f"woA{i}", [128, 512]) for i in range(4)]
        psB = [s.ps(f"woB{i}", [128, 512]) for i in range(4)]
        atok = load_xnsb(p, attnsb, attn_i, DCH)
        gl_tok = {}

        def evac_wo(o, pset, tok_mm):
            b = o % 2
            wv = [tok_mm, gl_tok[o]] + ([otok[o - 2]] if o >= 2 else [])
            t_ev = p.runs("ve", [(lambda e, tb=tb, b=b: e.tensor_mul(
                sgo[b][:, tb * 512:(tb + 1) * 512], pset[tb][:, :],
                g1t[b][:, tb * 512:(tb + 1) * 512])) for tb in range(4)], waits=wv)
            otok[o] = p.dma("sy", sgscr[o], sgo[b][:], waits=[t_ev], sem_name=f"soq{b}")
            return t_ev

        otok = {}
        wtok, mmtok, evtok = {}, {}, {}
        wsl = [s.sb(f"wosl{j}", [128, DCH * 128], BF16) for j in range(2)]
        for o in range(DCH):
            b = o % 2
            wd = [mmtok[o - 2]] if o >= 2 else []
            wtok[o] = p.dma("gp", wsl[b][:], wo[o].rearrange("p d q -> p (d q)"),
                            waits=wd, sem_name=f"wwq{b}")
            gl_tok[o] = p.dma("gp", g1t[b][:], g1fm_i[o],
                              waits=[otok[o - 2]] if o >= 2 else [], sem_name=f"glq{b}")
            pset = (psA, psB)[b]
            w = [wtok[o], atok]
            if o >= 2:
                w.append(evtok[o - 2])
            fns = []
            for dc in range(DCH):
                for tb in range(4):
                    fns.append(lambda e, dc=dc, tb=tb, o=o, b=b: e.matmul(
                        pset[tb][:, :], wsl[b][:, dc * 128:(dc + 1) * 128],
                        attnsb[:, dc * T + tb * 512:dc * T + (tb + 1) * 512],
                        start=(dc == 0), stop=(dc == DCH - 1)))
            mmtok[o] = p.run_group("te", fns, waits=w)
            evtok[o] = evac_wo(o, pset, mmtok[o])
        tok_sg = [("soq0", p.cnt["soq0"]), ("soq1", p.cnt["soq1"])]

    p.phase_barrier()
    with p.scoped() as s:
        sgl = [s.sb(f"sgl{j}", [128, DM], BF16) for j in range(2)]
        xt = s.sb("xt", [128, DM], F32)
        sgt = s.sb("sgtk", [128, DM], F32)
        pstA = s.ps("pstA", [128, 512], BF16)
        pstB = s.ps("pstB", [128, 512], BF16)
        prev = {}
        for i in range(TT):
            b = i % 2
            wl = [prev[("tp2", b)]] if ("tp2", b) in prev else []
            tok_sgl = p.dma("gp", sgl[b][:].rearrange("p (d t) -> p d t", d=DCH),
                            sgscr[:, :, i * 128:(i + 1) * 128].rearrange("d p t -> p d t"),
                            waits=wl, sem_name=f"sgq{b}")
            wx = ([prev[("add",)], ("owq", 16 * i)] if i >= 1 else [])
            tok_x = p.dma("sy", xt[:], x1_i[i * 128:(i + 1) * 128, :], waits=wx, sem_name="xq")
            for dcg in range(8):
                q = dcg % 2
                ps = (pstA, pstB)[q]
                wv = [tok_sgl, tok_k]
                if ("ev", q) in prev:
                    wv.append(prev[("ev", q)])
                fns = [(lambda e, dc=dcg * 4 + j, ps=ps, b=b: e.transpose(
                    ps[:, (dc % 4) * 128:(dc % 4 + 1) * 128], sgl[b][:, dc * 128:(dc + 1) * 128],
                    identS[:])) for j in range(4)]
                t_tp = p.runs("te", fns, waits=wv, self_sync=False)
                prev[("tp2", b)] = t_tp
                prev[("ev", q)] = p.run("ve", lambda e, dcg=dcg, ps=ps: e.tensor_copy(
                    out=sgt[:, dcg * 512:(dcg + 1) * 512], in_=ps[:, :]), waits=[t_tp])
            t_add = p.run("ve", lambda e: e.tensor_add(xt[:], xt[:], sgt[:]),
                          waits=[tok_x, prev[("ev", 0)], prev[("ev", 1)]])
            prev[("add",)] = t_add
            p.dma("sy", out_o[i * 128:(i + 1) * 128, :], xt[:], waits=[t_add], sem_name="owq")

    p.finish([("owq", p.cnt["owq"])])
    return p


def host_wo_inputs(inputs, attn_list, g1_list, x1_list):
    woT = np.ascontiguousarray(np.asarray(inputs["L1_attn_Wo"]).reshape(DCH, 128, DCH, 128)
                               .transpose(2, 1, 0, 3)).astype(np.float32)
    identB = np.eye(128, dtype=np.float32).astype(BF)
    maps = []
    for c in range(len(attn_list)):
        maps.append({"attn": attn_list[c], "wo": woT, "g1fm": g1_list[c], "x1": x1_list[c],
                     "identb": identB})
    return maps



def host_ln0_inputs(x_full):
    ident = np.eye(128, dtype=np.float32).astype(BF)
    maps = []
    for c in range(NCORE):
        maps.append({"x": np.ascontiguousarray(x_full[c * T:(c + 1) * T]),
                     "identb": ident})
    return maps


def _run8(p, maps):
    res = run_bass_kernel_spmd(p.nc, maps, core_ids=list(range(len(maps))))
    p.close()
    return res.results


def kernel(**inputs):
    inputs = {k: np.asarray(v) for k, v in inputs.items()}
    x_full = np.ascontiguousarray(inputs["L0_attn_hidden_states"].astype(np.float32, copy=False)
                                  .reshape(16384, DM))
    r1 = _run8(build_ln0(), host_ln0_inputs(x_full))
    xnfm_list = [r["xnfm"] for r in r1]
    lvrow_list = [np.ascontiguousarray(r["lvtok"].T).reshape(1, T).astype(np.float32) for r in r1]
    lvparts = np.ascontiguousarray(np.concatenate([r["lvpart"] for r in r1], axis=1)).astype(np.float32)
    r2 = _run8(build_inproj(), host_inproj_inputs(inputs, xnfm_list))
    r3 = _run8(build_convssm(), host_convssm_inputs(inputs, [r["xbc"] for r in r2],
                                                    [r["dtfm"] for r in r2]))
    l3o = [{**r3[c], "gsilu": r2[c]["gsilu"]} for c in range(NCORE)]
    r4 = _run8(build_ssm2(), host_ssm2_inputs(inputs, l3o, [r["shand"] for r in r3]))
    r5 = _run8(build_ffn0(), host_ffn0_inputs(inputs, xnfm_list, lvrow_list, lvparts))
    r6 = _run8(build_res0(), host_res0_inputs(inputs, x_full,
                                              [r4[c]["s0fm"] for c in range(NCORE)],
                                              [r5[c]["g0fm"] for c in range(NCORE)]))
    qkv_all = np.ascontiguousarray(np.concatenate([r6[c]["qkvfm"] for c in range(NCORE)], axis=2))
    r7 = _run8(build_attn(), host_attn_inputs(qkv_all))
    attn_full = np.stack([r7[h % 8]["attnfm"][h // 8] for h in range(32)])
    attn_list = [np.ascontiguousarray(attn_full[:, :, c * T:(c + 1) * T]) for c in range(NCORE)]
    r8 = _run8(build_wo(), host_wo_inputs(inputs, attn_list,
                                          [r6[c]["g1fm"] for c in range(NCORE)],
                                          [r6[c]["x1"] for c in range(NCORE)]))
    out = np.concatenate([np.asarray(r8[c]["out"]) for c in range(NCORE)], axis=0)
    return np.ascontiguousarray(out).reshape(4, 4096, DM).astype(np.float32, copy=False)


# version 04 (final)
# speedup vs baseline: 1.0239x; 1.0239x over previous
import contextlib
import numpy as np
import ml_dtypes
import concourse.bass as bass
import concourse.mybir as mybir
from concourse.bass_utils import run_bass_kernel_spmd
from concourse.bass_types import AP

F32 = mybir.dt.float32
BF16 = mybir.dt.bfloat16
AF = mybir.ActivationFunctionType
ALU = mybir.AluOpType
AX = mybir.AxisListType
BF = ml_dtypes.bfloat16

NCORE = 8
T = 2048
TT = T // 128
DM = 4096
DCH = DM // 128
DIN = 8192
CONVD = 10240
PROJ = 18560
NH = 128
HD = 64
G = 8
NST = 128
CS = 128
NC = T // CS
EPS_LN = 1e-5
EPS_GN = 1e-5
NEG_BIG = -1e30



class Prog:

    _ENG = {"ve": "vector", "se": "scalar", "te": "tensor", "gp": "gpsimd", "sy": "sync"}

    def __init__(self):
        self.nc = bass.Bass("TRN2", target_bir_lowering=False)
        self.stack = contextlib.ExitStack()
        self.sems = {}
        self.cnt = {}
        self.blk = None
        self.ns = ""
        self._alias = {}
        self._ns_next = {}
        self._ded = set()

    def phase_ns(self, tag):
        self.ns = tag

    def _phys(self, name):
        if not self.ns:
            return name
        key = (self.ns, name)
        if key in self._ded:
            return f"{self.ns}_{name}"
        if key not in self._alias:
            idx = self._ns_next.get(self.ns, 0)
            self._ns_next[self.ns] = idx + 1
            self._alias[key] = f"S{idx}"
        return self._alias[key]

    def _mark_ded(self, name):
        key = (self.ns, name)
        if key not in self._ded:
            assert key not in self._alias, f"sem {key} pooled before gp use"
            self._ded.add(key)

    def c(self, name):
        return self.cnt[self._phys(name)]

    def cg(self, name):
        return self.cnt.get(self._phys(name), 0)

    def _tname(self, name):
        return (self.ns + "_" + name) if self.ns else name

    def dint(self, name, shape, dtype=F32):
        return self.nc.dram_tensor(self._tname(name), list(shape), dtype, kind="Internal").ap()

    def sem(self, name):
        name = self._phys(name)
        if name not in self.sems:
            self.sems[name] = self.stack.enter_context(self.nc.semaphore(name))
            self.cnt[name] = 0
        return self.sems[name]

    def sb(self, name, shape, dtype):
        return self.stack.enter_context(self.nc.sbuf_tensor(self._tname(name), shape, dtype))

    @contextlib.contextmanager
    def scoped(self):
        with contextlib.ExitStack() as es:
            class S:
                def sb(_s, name, shape, dtype):
                    return es.enter_context(self.nc.sbuf_tensor(self._tname(name), shape, dtype))
                def ps(_s, name, shape, dtype=F32):
                    return es.enter_context(self.nc.psum_tensor(self._tname(name), shape, dtype))
            yield S()

    def ps(self, name, shape, dtype=F32):
        return self.stack.enter_context(self.nc.psum_tensor(self._tname(name), shape, dtype))

    def din(self, name, shape, dtype=F32):
        return self.nc.dram_tensor(name, list(shape), dtype, kind="ExternalInput").ap()

    def dout(self, name, shape, dtype=F32):
        return self.nc.dram_tensor(name, list(shape), dtype, kind="ExternalOutput").ap()

    def _section(self, key, fn):
        if key[0] == "v":
            eng = "vector"
        elif key.startswith("se"):
            eng = "scalar"
        elif key.startswith("te"):
            eng = "tensor"
        elif key.startswith("gp"):
            eng = "gpsimd"
        else:
            eng = "sync"
        getattr(self.blk, eng)(fn)

    def _waits(self, e, waits):
        for (s, v) in waits:
            if isinstance(s, str):
                s = self.sems[s] if s in self.sems else self.sem(s)
            e.wait_ge(s, v)

    def run(self, key, fn, waits=(), self_sync=True, inc=1):
        if key.startswith("gp"):
            self._mark_ded(key)
        sem = self.sem(key)
        pk = self._phys(key)
        cur = self.cnt[pk]

        def body(e):
            self._waits(e, waits)
            if self_sync and cur:
                e.wait_ge(sem, cur)
            fn(e).then_inc(sem, inc)

        self._section(key, body)
        self.cnt[pk] += inc
        return (pk, self.cnt[pk])

    def runs(self, key, fns, waits=(), self_sync=True):
        if key.startswith("gp"):
            self._mark_ded(key)
        sem = self.sem(key)
        pk = self._phys(key)
        start = self.cnt[pk]

        def body(e):
            self._waits(e, waits)
            c = start
            for fn in fns:
                if self_sync and c:
                    e.wait_ge(sem, c)
                fn(e).then_inc(sem, 1)
                c += 1

        self._section(key, body)
        self.cnt[pk] += len(fns)
        return (pk, self.cnt[pk])

    def run_group(self, key, fns, waits=(), inc=1):
        sem = self.sem(key)
        pk = self._phys(key)

        def body(e):
            self._waits(e, waits)
            for fn in fns[:-1]:
                fn(e)
            fns[-1](e).then_inc(sem, inc)

        self._section(key, body)
        self.cnt[pk] += inc
        return (pk, self.cnt[pk])

    def dma(self, qkey, out, in_, waits=(), sem_name=None):
        sname = sem_name or (qkey + "q")
        if qkey == "gp":
            self._mark_ded(sname)
        sem = self.sem(sname)
        pk = self._phys(sname)

        def body(e):
            self._waits(e, waits)
            e.dma_start(out=out, in_=in_).then_inc(sem, 16)

        self._section(qkey, body)
        self.cnt[pk] += 16
        return (pk, self.cnt[pk])

    def wait_on(self, key, waits):
        def body(e):
            self._waits(e, waits)
        self._section(key, body)

    def phase_barrier(self):
        snap = [(k, v) for (k, v) in self.cnt.items() if v > 0]
        for key in ("ve", "se", "te", "gp", "sy"):
            self.wait_on(key, snap)

    def finish(self, waits):
        self.wait_on("sy", waits)

    def build(self):
        self.blk = self.stack.enter_context(self.nc.Block())
        return self

    def close(self):
        self.stack.close()



def _bf(x):
    return np.ascontiguousarray(x).astype(BF)


def lhsT_tiles(wT):
    a = np.ascontiguousarray(wT, dtype=np.float32)
    din, dout = a.shape
    a = a.reshape(din // 128, 128, dout // 128, 128)
    return np.ascontiguousarray(a.transpose(2, 0, 1, 3))



def emit_ln0(p, al, io):
    x = io["x"]; identb = io["identb"]; xnfm = io["xnfm"]
    lvflat = io["lvflat"]; lvpart = io["lvpart"]

    xt = [al.sb(f"xt{j}", [128, DM], F32) for j in range(2)]
    xnb = al.sb("xnb", [128, DM], BF16)
    bst = al.sb("bst", [128, 6 * (DM // 512)], F32)
    mcols = al.sb("mcols", [128, TT], F32)
    vcols = al.sb("vcols", [128, TT], F32)
    rcols = al.sb("rcols", [128, TT], F32)
    tmp = al.sb("tmpl", [128, TT], F32)
    tmp2 = al.sb("tmp2", [128, TT], F32)
    lv = al.sb("lv", [128, TT], F32)
    lvs = al.sb("lvs", [128, 1], F32)
    mv = al.sb("mv", [128, 2], F32)
    ident = al.sb("ident", [128, 128], BF16)
    onescol = al.sb("onescol", [128, 1], F32)
    xev = [al.sb(f"xev{j}", [128, 512], BF16) for j in range(2)]
    lvT = al.sb("lvT", [TT, 128], F32)
    lvtot = al.sb("lvtot", [1, 1], F32)

    pst = [al.ps(f"pst{j}", [128, 512], BF16) for j in range(2)]
    psl2 = al.ps("psl2", [TT, 128], F32)
    pss = al.ps("pss", [1, 1], F32)
    NG = DM // 512

    tok_id = p.dma("sy", ident[:], identb, sem_name="kq")
    v_done = {}
    for i in range(TT):
        w = [v_done[i - 2]] if i >= 2 else []
        tok_x = p.dma("sy", xt[i % 2][:], x[i * 128:(i + 1) * 128, :], waits=w, sem_name=f"xq{i % 2}")
        fns = [(lambda e, g=g, i=i: e.bn_stats(out=bst[:, 6 * g:6 * g + 6],
                                               in_=xt[i % 2][:, 512 * g:512 * (g + 1)])) for g in range(NG)]
        fns.append(lambda e: e.bn_aggr(out=mv[:], in_=bst[:]))
        fns.append(lambda e, i=i: e.tensor_copy(out=mcols[:, i:i + 1], in_=mv[:, 0:1]))
        fns.append(lambda e, i=i: e.tensor_copy(out=vcols[:, i:i + 1], in_=mv[:, 1:2]))
        v_done[i] = p.runs("ve", fns, waits=[tok_x])
    p.run("ve", lambda e: e.tensor_scalar_add(tmp[:], vcols[:], EPS_LN))
    t_ln = p.run("se", lambda e: e.activation(out=tmp2[:], in_=tmp[:], func=AF.Ln),
                 waits=[("ve", p.c("ve"))])
    t_rs = p.run("se", lambda e: e.activation(out=rcols[:], in_=tmp2[:], func=AF.Exp, scale=-0.5))
    t_rc = p.run("ve", lambda e: e.reciprocal(tmp2[:], tmp[:]), waits=[t_rs])
    t_vr = p.run("ve", lambda e: e.tensor_mul(tmp[:], vcols[:], tmp2[:]))
    t_lv = p.run("se", lambda e: e.activation(out=lv[:], in_=tmp[:], func=AF.Ln, bias=1.0), waits=[t_vr])
    t_sum = p.run("ve", lambda e: e.reduce_sum(lvs[:], lv[:], axis=AX.X), waits=[t_lv])
    t_one = p.run("ve", lambda e: e.memset(onescol[:], 1.0))
    t_lvt = p.run("te", lambda e: e.transpose(psl2[:, :], lv[:, :], None, ) if False else
                  e.matmul(psl2[:, :], lv[:], identF_dummy[:], start=True, stop=True), waits=[t_lv],
                  self_sync=False) if False else None
    identf = io.get("identf")
    identF = al.sb("identF_ln", [128, 128], F32)
    tok_idf = p.dma("sy", identF[:], identf, sem_name="kqf")
    t_lvt = p.run("te", lambda e: e.transpose(psl2[:, :], lv[:], identF[:]),
                  waits=[t_lv, tok_idf], self_sync=False)
    t_lve = p.run("ve", lambda e: e.tensor_copy(out=lvT[:], in_=psl2[:, :]), waits=[t_lvt])
    tok_lv = p.dma("sy", lvflat.rearrange("a (i p) -> (a i) p", i=TT), lvT[:], waits=[t_lve], sem_name="lq")
    t_mm = p.run("te", lambda e: e.matmul(pss[:, :], onescol[:], lvs[:], start=True, stop=True),
                 waits=[t_sum, t_one], self_sync=False)
    t_ev = p.run("ve", lambda e: e.tensor_copy(out=lvtot[:], in_=pss[:, :]), waits=[t_mm])
    tok_lp = p.dma("sy", lvpart, lvtot[:], waits=[t_ev], sem_name="lq")

    xn_done = {}
    tp_done = {}
    ngrp = 0
    for i in range(TT):
        b = i % 2
        w = [xn_done[i - 2]] if i >= 2 else [v_done[14 + i]]
        tok_x = p.dma("sy", xt[b][:], x[i * 128:(i + 1) * 128, :], waits=w, sem_name=f"xq{i % 2}")
        w2 = [tok_x, t_rs]
        if i >= 1:
            w2.append(tp_done[i - 1])
        xn_done[i] = p.run("ve", lambda e, i=i, b=b: e.tensor_scalar(
            xnb[:], xt[b][:], mcols[:, i:i + 1], rcols[:, i:i + 1],
            op0=ALU.subtract, op1=ALU.mult), waits=w2)
        for dcg in range(DCH // 4):
            q = ngrp % 2
            wv = [xn_done[i], tok_id]
            if ngrp >= 2:
                wv.append(("ve2", p.c("ve2") - 1))
            fns = [(lambda e, dc=dcg * 4 + j, q=q: e.transpose(
                pst[q][:, (dc % 4) * 128:(dc % 4 + 1) * 128],
                xnb[:, dc * 128:(dc + 1) * 128], ident[:])) for j in range(4)]
            tp_done[i] = p.runs("te", fns, waits=wv, self_sync=False)
            we = [tp_done[i]]
            if ngrp >= 2:
                we.append((f"gq{q}", 16 * (ngrp // 2)))
            t_e = p.run("ve2", lambda e, q=q: e.tensor_copy(out=xev[q][:], in_=pst[q][:, :]), waits=we)
            dst = xnfm[dcg * 4:(dcg + 1) * 4, :, i * 128:(i + 1) * 128].rearrange("a b c -> b a c")
            p.dma("gp", dst, xev[q][:], waits=[t_e], sem_name=f"gq{q}")
            ngrp += 1
    return [tok_lv, tok_lp, (p._phys("gq0"), p.c("gq0")), (p._phys("gq1"), p.c("gq1"))]


def build_ln0():
    p = Prog()
    io = {"x": p.din("x", [T, DM], F32),
          "identb": p.din("identb", [128, 128], BF16),
          "identf": p.din("identf", [128, 128], F32),
          "xnfm": p.dout("xnfm", [DCH, 128, T], BF16),
          "lvflat": p.dout("lvflat", [1, T], F32),
          "lvpart": p.dout("lvpart", [1, 1], F32)}
    p.build()
    toks = emit_ln0(p, p, io)
    p.finish(toks)
    return p



def emit_stream_mm(p, name, wdram, och, dch, xnsb, xntok, evac, psets, tblk=4, alloc=None):
    al = alloc or p
    wsl = [al.sb(f"{name}_w{j}", [128, dch * 128], BF16) for j in range(2)]
    wtok, mmtok, evtok = {}, {}, {}

    def wdma(o):
        w = [mmtok[o - 2]] if o >= 2 else []
        wtok[o] = p.dma("gp", wsl[o % 2][:], wdram[o].rearrange("p d q -> p (d q)"),
                        waits=w, sem_name=f"{name}w{o % 2}")

    wdma(0)
    for o in range(och):
        if o + 1 < och:
            wdma(o + 1)
        pset = psets[o % 2]
        w = [wtok[o], xntok]
        if o >= 2:
            w.append(evtok[o - 2])
        fns = []
        for dc in range(dch):
            for tb in range(tblk):
                fns.append(lambda e, dc=dc, tb=tb, o=o: e.matmul(
                    pset[tb][:, :], wsl[o % 2][:, dc * 128:(dc + 1) * 128],
                    xnsb[:, dc * T + tb * 512:dc * T + (tb + 1) * 512],
                    start=(dc == 0), stop=(dc == dch - 1)))
        mmtok[o] = p.run_group("te", fns, waits=w)
        evtok[o] = evac(o, pset, mmtok[o])
    return mmtok, evtok


def load_xnsb(p, xnsb, xnfm_ap, dch, nsplit=4):
    tok = None
    step = dch // nsplit
    for s in range(nsplit):
        src = xnfm_ap[s * step:(s + 1) * step].rearrange("d p t -> p d t")
        dst = xnsb[:, s * step * T:(s + 1) * step * T].rearrange("p (d t) -> p d t", d=step)
        tok = p.dma("sy", dst, src, sem_name="xnq")
    return tok



def emit_inproj(p, al, io):
    xnfm = io["xnfm"]; wproj = io["wproj"]; dtb = io["dtb"]
    gsilu = io["gsilu"]; xbc = io["xbc"]; dtfm = io["dtfm"]; bdry = io.get("bdry")

    xnsb = al.sb("xnsb", [128, DCH * T], BF16)
    outb = [al.sb(f"outb{j}", [128, T], BF16) for j in range(2)]
    dtsb = al.sb("dtsb", [128, T], F32)
    dsc1 = al.sb("dsc1", [128, 512], F32)
    dsc2 = al.sb("dsc2", [128, 512], F32)
    dtbc = al.sb("dtbc", [128, 1], F32)
    psA = [al.ps(f"ppA{i}", [128, 512]) for i in range(4)]
    psB = [al.ps(f"ppB{i}", [128, 512]) for i in range(4)]

    xntok = load_xnsb(p, xnsb, xnfm, DCH)
    tok_b = p.dma("sy", dtbc[:], dtb, sem_name="kq")
    OCH = PROJ // 128
    odma = {}

    def evac(o, pset, tok_mm):
        if o < 64:
            key, buf = "se", outb[o % 2]
            fns = [(lambda e, tb=tb: e.activation(out=buf[:, tb * 512:(tb + 1) * 512],
                                                  in_=pset[tb][:, :], func=AF.Silu))
                   for tb in range(4)]
        elif o < 144:
            key, buf = "ve", outb[o % 2]
            fns = [(lambda e, tb=tb: e.tensor_copy(out=buf[:, tb * 512:(tb + 1) * 512],
                                                   in_=pset[tb][:, :])) for tb in range(4)]
        else:
            last = None
            for tb in range(4):
                sl = slice(tb * 512, (tb + 1) * 512)
                p.run("se", lambda e, tb=tb: e.activation(out=dsc1[:], in_=pset[tb][:, :],
                                                          func=AF.Abs, bias=dtbc[:]),
                      waits=[tok_mm, tok_b] + ([last] if last else []))
                p.run("se", lambda e: e.activation(out=dsc2[:], in_=dsc1[:], func=AF.Exp, scale=-1.0))
                p.run("se", lambda e: e.activation(out=dsc1[:], in_=dsc2[:], func=AF.Ln, bias=1.0))
                t4 = p.run("se", lambda e, tb=tb: e.activation(out=dsc2[:], in_=pset[tb][:, :],
                                                               func=AF.Relu, bias=dtbc[:]))
                last = p.run("ve", lambda e, sl=sl: e.tensor_add(dtsb[:, sl], dsc1[:], dsc2[:]),
                             waits=[t4])
            odma[o] = p.dma("sy", dtfm, dtsb[:], waits=[last], sem_name="oq0")
            return last
        w = [tok_mm]
        if o >= 2:
            w.append(odma[o - 2])
        t_e = p.runs(key, fns, waits=w)
        if o < 64:
            odma[o] = p.dma("sy", gsilu[o], buf[:], waits=[t_e], sem_name=f"oq{o % 2}")
        else:
            if bdry is not None:
                p.dma("sy", bdry[o - 64], buf[:, T - 3:T], waits=[t_e], sem_name=f"oq{o % 2}")
            odma[o] = p.dma("sy", xbc[o - 64], buf[:], waits=[t_e], sem_name=f"oq{o % 2}")
        return t_e

    emit_stream_mm(p, "ip", wproj, OCH, DCH, xnsb, xntok, evac, (psA, psB), alloc=al)
    return [(p._phys("oq0"), p.c("oq0")), (p._phys("oq1"), p.c("oq1"))]


def build_inproj():
    p = Prog()
    io = {"xnfm": p.din("xnfm", [DCH, 128, T], BF16),
          "wproj": p.din("wproj", [PROJ // 128, 128, DCH, 128], F32),
          "dtb": p.din("dtb", [128, 1], F32),
          "gsilu": p.dout("gsilu", [DIN // 128, 128, T], BF16),
          "xbc": p.dout("xbc", [CONVD // 128, 128, T], BF16),
          "dtfm": p.dout("dtfm", [128, T], F32)}
    p.build()
    toks = emit_inproj(p, p, io)
    p.finish(toks)
    return p


def host_inproj_inputs(inputs, xnfm_list):
    wT = np.ascontiguousarray(inputs["L0_attn_in_proj_w"].T)
    slab = np.ascontiguousarray(
        wT.reshape(DCH, 128, PROJ // 128, 128).transpose(2, 1, 0, 3)).astype(np.float32)
    dtb = np.ascontiguousarray(inputs["L0_attn_dt_bias"].reshape(128, 1)).astype(np.float32)
    return [{"xnfm": xnfm_list[c], "wproj": slab, "dtb": dtb} for c in range(len(xnfm_list))]



def emit_convssm(p, al, io):
    NCT = CONVD // 128
    xbc = io["xbc"]; bdry_src = io["bdry_src"]; cw = io["cw"]; cb = io["cb"]
    dtfm_i = io["dtfm"]; alog = io["alog"]; identb = io["identb"]; identf = io["identf"]
    xch_o = io["xch"]; xs_o = io["xs"]; bfm_o = io["bfm"]; cfm_o = io["cfm"]
    cst_o = io["cst"]; ncst_o = io["ncst"]; ecst_o = io["ecst"]; pfac_o = io["pfac"]
    dA_o = io["dA"]; rawst_o = io["rawst"]; owncum_o = io["owncum"]; shand_o = io["shand"]
    rmask = al.sb("rmask", [128, 1], F32)
    tok_rm = p.dma("sy", rmask[:], io["rolemask"], sem_name="rmq")
    bfm = al.sb("bfm_sb", [128, G * T], BF16)
    dtt = al.sb("dtt", [128, T], F32)
    dect = al.sb("dect", [128, T], F32)
    dAsb = al.sb("dAsb", [128, NC], F32)
    pf = al.sb("pf", [128, NC], F32)
    identS = al.sb("identS", [128, 128], BF16)
    identF = al.sb("identF", [128, 128], F32)
    onesT = al.sb("onesT", [128, 128], F32)

    pbt = [al.ps(f"pbt{j}", [128, 512], BF16) for j in range(2)]
    pft = [al.ps(f"pft{j}", [128, 128], F32) for j in range(2)]
    psm = [al.ps(f"psm{j}", [128, 512]) for j in range(4)]

    with p.scoped() as s:
        dtsb = s.sb("dtsb", [128, T], F32)
        acol = s.sb("acol", [128, 1], F32)
        adt = s.sb("adt", [128, T], F32)
        cssb = s.sb("cssb", [128, T], F32)
        cstA = s.sb("cstA", [128, T], F32)
        ncstA = s.sb("ncstA", [128, T], F32)
        ecstA = s.sb("ecstA", [128, T], F32)
        decfA = s.sb("decfA", [128, T], F32)

        p.dma("sy", identS[:], identb, sem_name="kq")
        p.dma("sy", identF[:], identf, sem_name="kq")
        p.dma("sy", dtsb[:], dtfm_i, sem_name="kq")
        p.dma("sy", acol[:], alog, sem_name="kq")
        tok_k = ("kq", p.c("kq"))
        t_one = p.run("ve", lambda e: e.memset(onesT[:], 1.0))
        t_a = p.run("se", lambda e: e.activation(out=acol[:], in_=acol[:], func=AF.Exp), waits=[tok_k])
        p.run("ve", lambda e: e.tensor_scalar_mul(acol[:], acol[:], -1.0), waits=[t_a])
        t_adt = p.run("ve", lambda e: e.tensor_scalar_mul(adt[:], dtsb[:], acol[:]))
        fns = [(lambda e, c=c: e.tensor_tensor_scan(out=cssb[:, c * CS:(c + 1) * CS],
                                                    data0=onesT[:], data1=adt[:, c * CS:(c + 1) * CS],
                                                    initial=0.0, op0=ALU.mult, op1=ALU.add))
               for c in range(NC)]
        t_cs = p.runs("ve", fns, waits=[t_adt, t_one])
        fns = [(lambda e, c=c: e.activation(out=dAsb[:, c:c + 1], in_=cssb[:, c * CS + CS - 1:c * CS + CS],
                                            func=AF.Exp)) for c in range(NC)]
        t_dA = p.runs("se", fns, waits=[t_cs])
        p.dma("sy", dA_o, dAsb[:], waits=[t_dA], sem_name="obq")
        t_pf0 = p.run("ve", lambda e: e.memset(pf[:, 0:1], 1.0))
        t_pf = p.run("ve", lambda e: e.tensor_tensor_scan(out=pf[:, 1:NC], data0=dAsb[:, 0:NC - 1],
                                                          data1=onesT[:, 0:NC - 1], initial=1.0,
                                                          op0=ALU.mult, op1=ALU.mult), waits=[t_dA])
        p.dma("sy", pfac_o, pf[:], waits=[t_pf0, t_pf], sem_name="obq")
        fns = [(lambda e, c=c: e.tensor_scalar(decfA[:, c * CS:(c + 1) * CS], cssb[:, c * CS:(c + 1) * CS],
                                               cssb[:, c * CS + CS - 1:c * CS + CS], None,
                                               op0=ALU.subtract)) for c in range(NC)]
        t_dec1 = p.runs("ve", fns, waits=[t_cs])
        t_dec2 = p.run("se", lambda e: e.activation(out=decfA[:], in_=decfA[:], func=AF.Exp, scale=-1.0), waits=[t_dec1])
        pft_free = {}
        for c in range(NC):
            for j, (srcbuf, t_src, dst) in enumerate([(cssb, t_cs, cstA), (decfA, t_dec2, dect),
                                                      (dtsb, tok_k, dtt)]):
                idx = c * 3 + j
                q = idx % 2
                wv = [t_src, tok_k]
                if idx >= 2:
                    wv.append(pft_free[idx - 2])
                t_tp = p.run("te", lambda e, srcbuf=srcbuf, c=c, q=q: e.transpose(
                    pft[q][:, :], srcbuf[:, c * CS:(c + 1) * CS], identF[:]), waits=wv, self_sync=False)
                pft_free[idx] = p.run("vf", lambda e, dst=dst, c=c, q=q: e.tensor_copy(
                    out=dst[:, c * CS:(c + 1) * CS], in_=pft[q][:, :]), waits=[t_tp])
        t_vf_all = ("vf", p.c("vf"))
        t_n = p.run("ve", lambda e: e.tensor_scalar_mul(ncstA[:], cstA[:], -1.0), waits=[t_vf_all])
        t_ec = p.run("se", lambda e: e.activation(out=ecstA[:], in_=cstA[:], func=AF.Exp), waits=[t_vf_all])
        p.dma("sy", cst_o.rearrange("c p k -> p c k"), cstA[:].rearrange("p (c k) -> p c k", c=NC),
              waits=[t_vf_all], sem_name="obq")
        p.dma("sy", ncst_o.rearrange("c p k -> p c k"), ncstA[:].rearrange("p (c k) -> p c k", c=NC),
              waits=[t_n], sem_name="obq")
        p.dma("sy", ecst_o.rearrange("c p k -> p c k"), ecstA[:].rearrange("p (c k) -> p c k", c=NC),
              waits=[t_ec], sem_name="obq")

    p.phase_barrier()
    NCT = CONVD // 128
    with p.scoped() as s:
        padded = [s.sb(f"pad{j}", [128, T + 3], BF16) for j in range(2)]
        acc = s.sb("acc", [128, T], F32)
        convo = [s.sb(f"convo{j}", [128, T], BF16) for j in range(2)]
        cwc = [s.sb(f"cwc{j}", [128, 4], F32) for j in range(2)]
        cbc = [s.sb(f"cbc{j}", [128, 1], F32) for j in range(2)]
        xev = [s.sb(f"cxev{j}", [128, 512], BF16) for j in range(2)]
        conv_done = {}
        use_done = {}
        sil_done = {}
        ngrp = 0
        cab = {b: p.cg(f"caq{b}") for b in (0, 1)}
        for ct in range(NCT):
            b = ct % 2
            w = [conv_done[ct - 2], sil_done[ct - 2]] if ct >= 2 else []
            p.dma("sy", padded[b][:, 0:3], bdry_src[ct], waits=w, sem_name=f"caq{b}")
            p.dma("sy", padded[b][:, 3:], xbc[ct], sem_name=f"caq{b}")
            p.dma("sy", cwc[b][:], cw[ct], sem_name=f"caq{b}")
            p.dma("sy", cbc[b][:], cb[ct].unsqueeze(1), sem_name=f"caq{b}")
            tok_grp = (f"caq{b}", cab[b] + 64 * (ct // 2 + 1))
            wmac = [tok_grp]
            if ct >= 1:
                wmac.append(sil_done[ct - 1])
            t0 = p.run("ve", lambda e, b=b: e.tensor_scalar_mul(padded[b][:, 0:3], padded[b][:, 0:3],
                                                                rmask[:]), waits=wmac + [tok_rm])
            t = p.run("ve", lambda e, b=b: e.tensor_scalar_mul(acc[:], padded[b][:, 0:T], cwc[b][:, 0:1]))
            for k2 in range(1, 4):
                t = p.run("ve", lambda e, k2=k2, b=b: e.scalar_tensor_tensor(
                    acc[:], padded[b][:, k2:k2 + T], cwc[b][:, k2:k2 + 1], acc[:],
                    op0=ALU.mult, op1=ALU.add))
            conv_done[ct] = t
            wsil = [t]
            if ct >= 2:
                wsil.append(use_done[ct - 2])
            t_sl = p.run("se", lambda e, b=b: e.activation(out=convo[b][:], in_=acc[:],
                                                           func=AF.Silu, bias=cbc[b][:]), waits=wsil)
            sil_done[ct] = t_sl
            if ct < 64:
                last_tp = None
                for cg in range(NC // 4):
                    q = ngrp % 2
                    wv = [t_sl]
                    if ngrp >= 2:
                        wv.append(("vx", ngrp - 1))
                    fns = [(lambda e, c=cg * 4 + j, q=q, b=b: e.transpose(
                        pbt[q][:, (c % 4) * 128:(c % 4 + 1) * 128],
                        convo[b][:, c * CS:(c + 1) * CS], identS[:])) for j in range(4)]
                    last_tp = p.runs("te", fns, waits=wv, self_sync=False)
                    we = [last_tp]
                    if ngrp >= 2:
                        we.append((f"xqa{q}", 16 * (ngrp // 2)))
                    t_e = p.run("vx", lambda e, q=q: e.tensor_copy(out=xev[q][:], in_=pbt[q][:, :]), waits=we)
                    dst = xch_o[cg * 4:(cg + 1) * 4, :, ct * 128:(ct + 1) * 128].rearrange("a b c -> b a c")
                    p.dma("gp", dst, xev[q][:], waits=[t_e], sem_name=f"xqa{q}")
                    ngrp += 1
                use_done[ct] = last_tp
            elif ct < 72:
                g = ct - 64
                t_c = p.run("ve", lambda e, g=g, b=b: e.tensor_copy(out=bfm[:, g * T:(g + 1) * T], in_=convo[b][:]),
                            waits=[t_sl])
                p.dma("gp", bfm_o[g], bfm[:, g * T:(g + 1) * T], waits=[t_c], sem_name="bfq")
                use_done[ct] = t_c
            else:
                g = ct - 72
                use_done[ct] = p.dma("gp", cfm_o[g], convo[b][:], waits=[t_sl], sem_name=f"bq{b}")

    p.phase_barrier()
    with p.scoped() as s:
        xchb = [s.sb(f"xchb{j}", [128, DIN], BF16) for j in range(2)]
        xssb = [s.sb(f"xssb{j}", [128, DIN], BF16) for j in range(2)]
        dxssb = s.sb("dxssb", [128, DIN], BF16)
        btok = s.sb("btok", [128, 128], BF16)
        rawsb = [s.sb(f"rawsb{j}", [128, 1024], F32) for j in range(2)]
        xs_read = {}
        mm_done = {}
        mm_last = None
        for c in range(NC):
            b = c % 2
            w = [xs_read[c - 2]] if c >= 2 else []
            tok_xc = p.dma("sy", xchb[b][:], xch_o[c], waits=w, sem_name=f"ccq{b}")
            wxs = [tok_xc]
            if c >= 2:
                wxs.append((f"xsq{b}", 16 * (c // 2)))
            if c >= 1:
                wxs.append(mm_done[c - 1])
            t_xs = p.run("ve", lambda e, c=c, b=b: e.tensor_tensor(
                xssb[b][:].rearrange("p (h d) -> p h d", h=128),
                xchb[b][:].rearrange("p (h d) -> p h d", h=128),
                dtt[:, c * CS:(c + 1) * CS].unsqueeze(2).to_broadcast([128, 128, HD]),
                op=ALU.mult), waits=wxs)
            xs_read[c] = t_xs
            p.dma("gp", xs_o[c], xssb[b][:], waits=[t_xs], sem_name=f"xsq{b}")
            t_dxs = p.run("ve", lambda e, c=c, b=b: e.tensor_tensor(
                dxssb[:].rearrange("p (h d) -> p h d", h=128),
                xssb[b][:].rearrange("p (h d) -> p h d", h=128),
                dect[:, c * CS:(c + 1) * CS].unsqueeze(2).to_broadcast([128, 128, HD]),
                op=ALU.mult))
            for g in range(G):
                n = c * G + g
                q = n % 2
                wv = []
                if n >= 2:
                    wv.append(("vb", n - 1))
                t_tp = p.run("te", lambda e, g=g, c=c, q=q: e.transpose(
                    pbt[q][:, 0:128], bfm[:, g * T + c * CS:g * T + (c + 1) * CS], identS[:]),
                    waits=wv, self_sync=False)
                wbt = [t_tp] + ([mm_last] if n >= 1 else [])
                t_bt = p.run("vb", lambda e, q=q: e.tensor_copy(out=btok[:], in_=pbt[q][:, 0:128]),
                             waits=wbt)
                wmm = [t_bt, t_dxs]
                if n >= 2:
                    wmm.append(("vs", 2 * (n - 1)))
                fns = [(lambda e, g=g, tb=tb, q=q: e.matmul(
                    psm[2 * q + tb][:, :], btok[:],
                    dxssb[:, g * 1024 + tb * 512:g * 1024 + (tb + 1) * 512],
                    start=True, stop=True)) for tb in range(2)]
                mm_last = p.run_group("te", fns, waits=wmm)
                if g == G - 1:
                    mm_done[c] = mm_last
                we = [mm_last]
                if n >= 2:
                    we.append((f"rsq{q}", 16 * (n // 2)))
                t_ev = p.runs("vs", [(lambda e, q=q, tb=tb: e.tensor_copy(
                    out=rawsb[q][:, tb * 512:(tb + 1) * 512], in_=psm[2 * q + tb][:, :]))
                    for tb in range(2)], waits=we)
                dst = rawst_o[c, 16 * g:16 * (g + 1)].rearrange("h n d -> n h d")
                p.dma("gp", dst, rawsb[q][:].rearrange("n (h d) -> n h d", h=16),
                      waits=[t_ev], sem_name=f"rsq{q}")

    p.phase_barrier()
    with p.scoped() as s:
        statesb = s.sb("statesb", [128, NST * HD], F32)
        rawrd = [s.sb(f"rawrd{j}", [128, NST * HD], F32) for j in range(2)]
        t_z = p.run("ve", lambda e: e.memset(statesb[:], 0.0))
        up_tok = {}
        t_up = t_z
        for c in range(NC):
            b = c % 2
            wr = [up_tok[c - 2]] if c >= 2 else []
            tok_r = p.dma("sy", rawrd[b][:], rawst_o[c].rearrange("h n d -> h (n d)"),
                          waits=wr, sem_name=f"rrq{b}")
            tok_st = p.dma("sy", owncum_o[c], statesb[:], waits=[t_up], sem_name="odq")
            t_up = p.run("ve", lambda e, c=c, b=b: e.scalar_tensor_tensor(
                statesb[:], statesb[:], dAsb[:, c:c + 1], rawrd[b][:],
                op0=ALU.mult, op1=ALU.add), waits=[tok_r, tok_st])
            up_tok[c] = t_up
        p.dma("sy", shand_o, statesb[:], waits=[t_up], sem_name="odq")

    allq = ["obq", "xqa0", "xqa1", "bfq", "bq0", "bq1", "xsq0", "xsq1", "rsq0", "rsq1", "odq"]
    return [(p._phys(q), p.cnt[p._phys(q)]) for q in allq if p._phys(q) in p.cnt]


def build_convssm():
    p = Prog()
    NCT = CONVD // 128
    io = {"xbc": p.din("xbc", [NCT, 128, T], BF16),
          "bdry_src": p.din("xbcb", [NCT, 128, 3], BF16),
          "rolemask": p.din("rolemask", [128, 1], F32),
          "cw": p.din("cw", [NCT, 128, 4], F32),
          "cb": p.din("cb", [NCT, 128], F32),
          "dtfm": p.din("dtfm", [128, T], F32),
          "alog": p.din("alog", [128, 1], F32),
          "identb": p.din("identb", [128, 128], BF16),
          "identf": p.din("identf", [128, 128], F32),
          "xch": p.dout("xch", [NC, 128, DIN], BF16),
          "xs": p.dout("xs", [NC, 128, DIN], BF16),
          "bfm": p.dout("bfm", [G, 128, T], BF16),
          "cfm": p.dout("cfm", [G, 128, T], BF16),
          "cst": p.dout("cst", [NC, 128, 128], F32),
          "ncst": p.dout("ncst", [NC, 128, 128], F32),
          "ecst": p.dout("ecst", [NC, 128, 128], F32),
          "pfac": p.dout("pfac", [128, NC], F32),
          "dA": p.dout("dA", [128, NC], F32),
          "rawst": p.dout("rawst", [NC, NH, NST, HD], F32),
          "owncum": p.dout("owncum", [NC, NH, NST * HD], F32),
          "shand": p.dout("shand", [NH, NST * HD], F32)}
    p.build()
    toks = emit_convssm(p, p, io)
    p.finish(toks)
    return p


def host_convssm_inputs(inputs, xbc_list, dt_list):
    cw = np.ascontiguousarray(inputs["L0_attn_conv_w"].reshape(CONVD, 4).reshape(CONVD // 128, 128, 4)).astype(np.float32)
    cb = np.ascontiguousarray(inputs["L0_attn_conv_b"].reshape(CONVD // 128, 128)).astype(np.float32)
    alog = np.ascontiguousarray(inputs["L0_attn_A_log"].reshape(128, 1)).astype(np.float32)
    identB = np.eye(128, dtype=np.float32).astype(BF)
    identF = np.eye(128, dtype=np.float32)
    n = len(xbc_list)
    maps = []
    for c in range(n):
        if c % 2 == 1:
            bdry = np.ascontiguousarray(xbc_list[c - 1][:, :, T - 3:T])
        else:
            bdry = np.zeros((CONVD // 128, 128, 3), dtype=BF)
        rm = np.ones((128, 1), dtype=np.float32)
        maps.append({"xbc": xbc_list[c], "xbcb": bdry, "rolemask": rm, "cw": cw, "cb": cb,
                     "alog": alog, "dtfm": dt_list[c], "identb": identB, "identf": identF})
    return maps



def emit_ssm2(p, al, io):
    xs_i = io["xs"]; xch_i = io["xch"]; bfm_i = io["bfm"]; cfm_i = io["cfm"]
    cst_i = io["cst"]; ncst_i = io["ncst"]; ecst_i = io["ecst"]; pfac_i = io["pfac"]
    owncum_i = io["owncum"]; sinit_i = io["sinit"]; gsilu_i = io["gsilu"]
    normw_i = io["normw"]; dvec_i = io["dvec"]; identb = io["identb"]; identf = io["identf"]
    masklow_i = io["masklow_in"]; opw = io["opw"]
    ynfm_o = io["ynfm"]; s0fm_o = io["s0fm"]
    identS = al.sb("identS", [128, 128], BF16)
    identF = al.sb("identF", [128, 128], F32)
    masklow = al.sb("masklow", [128, 128], F32)
    pfsb = al.sb("pfsb", [128, NC], F32)
    drep = al.sb("drep", [128, 128], F32)
    nwsb = al.sb("nwsb", [128, DIN // 128], F32)
    onescol = al.sb("onescol", [128, 1], F32)
    cstsb = al.sb("cstsb", [128, NC * 128], F32)
    ncstsb = al.sb("ncstsb", [128, NC * 128], F32)
    ecstsb = al.sb("ecstsb", [128, NC * 128], F32)


    nk = 0
    for (dst, srcap) in [(identS[:], identb), (identF[:], identf), (masklow[:], masklow_i),
                         (pfsb[:], pfac_i), (drep[:], dvec_i.to_broadcast([128, 128])),
                         (nwsb[:], normw_i),
                         (cstsb[:].rearrange("p (c k) -> p c k", c=NC), cst_i.rearrange("c p k -> p c k")),
                         (ncstsb[:].rearrange("p (c k) -> p c k", c=NC), ncst_i.rearrange("c p k -> p c k")),
                         (ecstsb[:].rearrange("p (c k) -> p c k", c=NC), ecst_i.rearrange("c p k -> p c k"))]:
        p.dma("sy", dst, srcap, sem_name="kq")
        nk += 16
    tok_k = ("kq", p.c("kq"))
    t_one = p.run("ve", lambda e: e.memset(onescol[:], 1.0))

    pi_scr = p.dint("piscr", [NC, NST, NH, HD], F32)
    rscr = p.dint("rscr", [1, 1024], F32)

    rmask = al.sb("rmask2", [128, 1], F32)
    tok_rm = p.dma("sy", rmask[:], io["rolemask"], sem_name="rmq2")
    t_pfm = p.run("ve", lambda e: e.tensor_scalar_mul(pfsb[:], pfsb[:], rmask[:]),
                  waits=[tok_rm, tok_k])
    with p.scoped() as s:
        sinit = s.sb("sinit_sb", [128, NST * HD], F32)
        pint = [s.sb(f"pint{j}", [128, NST * HD], F32) for j in range(2)]
        tok_si = p.dma("sy", sinit[:], sinit_i, sem_name="kq2")
        pqb = {bb: p.cg(f"piq{bb}") for bb in (0, 1)}
        for c in range(NC):
            b = c % 2
            w = [tok_si, tok_k] + ([(f"piq{b}", pqb[b] + 16 * (c // 2))] if c >= 2 else [])
            t_pi = p.run("ve", lambda e, c=c, b=b: e.tensor_scalar_mul(
                pint[b][:], sinit[:], pfsb[:, c:c + 1]), waits=w + [t_pfm])
            p.dma("sy", pi_scr[c].rearrange("n h d -> h n d"),
                  pint[b][:].rearrange("h (n d) -> h n d", n=NST), waits=[t_pi], sem_name=f"piq{b}")

    p.phase_barrier()
    with p.scoped() as s:
        bfmc = [s.sb(f"bfmc{j}", [128, G * 128], BF16) for j in range(2)]
        cfmc = [s.sb(f"cfmc{j}", [128, G * 128], BF16) for j in range(2)]
        xsc = s.sb("xsc", [128, DIN], BF16)
        xchc = s.sb("xchc", [128, DIN], BF16)
        gslc = s.sb("gslc", [128, DIN], BF16)
        ownc = [s.sb(f"ownc{j}", [128, 1024], BF16) for j in range(2)]
        pinc = [s.sb(f"pinc{j}", [128, 1024], BF16) for j in range(2)]
        gmtc = s.sb("gmtc", [128, 128], BF16)
        egsb = [s.sb(f"egsb{j}", [128, 16 * 128], BF16) for j in range(2)]
        ytok = s.sb("ytok", [128, DIN], BF16)
        ytmp = s.sb("ytmp", [128, 1024], F32)
        ygfm = s.sb("ygfm", [128, DIN], BF16)
        ynfmc = s.sb("ynfmc", [128, DIN], BF16)
        sqb = s.sb("sqb", [128, 1024], F32)
        gsrow = s.sb("gsrow", [1, 1024], F32)
        rstdrep = s.sb("rstdrep", [128, 1024], F32)

        psg = s.ps("psg", [128, 128])
        pse = [s.ps(f"pse{j}", [128, 512]) for j in range(2)]
        psyo = s.ps("psyo", [128, 1024])
        psyd = s.ps("psyd", [128, 1024])
        psyt = s.ps("psyt", [128, 512], BF16)

        prev = {}
        for c in range(NC):
            b = c % 2
            wld = [prev[("ld", c - 2)]] if c >= 2 else []
            p.dma("gp", bfmc[b][:].rearrange("p (g k) -> p g k", g=G),
                  bfm_i[:, :, c * CS:(c + 1) * CS].rearrange("g p k -> p g k"),
                  waits=wld, sem_name=f"ldq{b}")
            p.dma("gp", cfmc[b][:].rearrange("p (g k) -> p g k", g=G),
                  cfm_i[:, :, c * CS:(c + 1) * CS].rearrange("g p k -> p g k"), sem_name=f"ldq{b}")
            tok_ld = (f"ldq{b}", p.c(f"ldq{b}"))
            wx = [prev[("xcon", c - 1)]] if c >= 1 else []
            tok_xs = p.dma("sy", xsc[:], xs_i[c], waits=wx, sem_name="xq")
            tok_xh = p.dma("sy", xchc[:], xch_i[c], sem_name="xq")
            tok_gs = p.dma("sy", gslc[:].rearrange("p (f k) -> p f k", f=64),
                           gsilu_i[:, :, c * CS:(c + 1) * CS].rearrange("f p k -> p f k"), sem_name="xq")
            tok_x = ("xq", p.c("xq"))

            for g in range(G):
                eb = g % 2
                wv = [tok_ld, tok_k] + ([prev[("gmtev",)]] if ("gmtev",) in prev else [])
                t_mm = p.run("te", lambda e, g=g, b=b: e.matmul(
                    psg[:, :], bfmc[b][:, g * 128:(g + 1) * 128], cfmc[b][:, g * 128:(g + 1) * 128],
                    start=True, stop=True), waits=wv, self_sync=False)
                t_gmt = p.run("ve", lambda e: e.tensor_copy(out=gmtc[:], in_=psg[:, :]), waits=[t_mm])
                prev[("gmtev",)] = t_gmt
                for qq in range(4):
                    qb = qq % 2
                    wv = [tok_k] + ([prev[("exp", qb)]] if ("exp", qb) in prev else [])
                    fns = [lambda e, qb=qb: e.matmul(
                        pse[qb][:, :].rearrange("p (a k) -> p a k", a=4), identF[:],
                        masklow[:].unsqueeze(1).to_broadcast([128, 4, 128]),
                        start=True, stop=False)]
                    for hh in range(4):
                        h = g * 16 + qq * 4 + hh
                        fns.append(lambda e, h=h, hh=hh, qb=qb, c=c: e.matmul(
                            pse[qb][:, hh * 128:(hh + 1) * 128],
                            cstsb[:, c * 128 + h:c * 128 + h + 1].to_broadcast([128, 128]),
                            identF[:], start=False, stop=(hh == 3)))
                    t_emm = p.run_group("te", fns, waits=wv)
                    fns = [(lambda e, h=g * 16 + qq * 4 + hh, hh=hh, qb=qb, c=c, qq=qq, g=g: e.activation(
                        out=egsb[g % 2][:, (qq * 4 + hh) * 128:(qq * 4 + hh + 1) * 128],
                        in_=pse[qb][:, hh * 128:(hh + 1) * 128],
                        func=AF.Exp, bias=ncstsb[:, c * 128 + h:c * 128 + h + 1])) for hh in range(4)]
                    wexp = [t_emm] + ([prev[("egused", g % 2)]] if ("egused", g % 2) in prev and qq == 0 else [])
                    prev[("exp", qb)] = p.runs("se", fns, waits=wexp)
                t_exp_all = prev[("exp", 1)]
                t_eg = p.run("ve", lambda e, g=g: e.tensor_tensor(
                    egsb[g % 2][:].rearrange("p (a k) -> p a k", a=16),
                    egsb[g % 2][:].rearrange("p (a k) -> p a k", a=16),
                    gmtc[:].unsqueeze(1).to_broadcast([128, 16, 128]),
                    op=ALU.mult), waits=[prev[("exp", 0)], prev[("exp", 1)], t_gmt])
                wst = [prev[("ymm",)]] if ("ymm",) in prev else []
                tok_ow = p.dma("gp", ownc[b][:].rearrange("p (h d) -> p h d", h=16),
                               owncum_i[c].rearrange("h (n d) -> n h d", n=NST)[:, 16 * g:16 * (g + 1), :],
                               waits=wst, sem_name=f"owq{b}")
                tok_pi = p.dma("gp", pinc[b][:].rearrange("p (h d) -> p h d", h=16),
                               pi_scr[c, :, 16 * g:16 * (g + 1), :], sem_name=f"owq{b}")
                tok_st = (f"owq{b}", p.c(f"owq{b}"))
                wv = [tok_st, tok_x, t_eg] + ([prev[("yev",)]] if ("yev",) in prev else [])
                if ("strd",) in prev:
                    wv.append(prev[("strd",)])
                fns = []
                for hh in range(16):
                    fns.append(lambda e, g=g, hh=hh, b=b: e.matmul(
                        psyo[:, hh * 64:(hh + 1) * 64], cfmc[b][:, g * 128:(g + 1) * 128],
                        ownc[b][:, hh * 64:(hh + 1) * 64], start=True, stop=False))
                    fns.append(lambda e, g=g, hh=hh, b=b: e.matmul(
                        psyo[:, hh * 64:(hh + 1) * 64], cfmc[b][:, g * 128:(g + 1) * 128],
                        pinc[b][:, hh * 64:(hh + 1) * 64], start=False, stop=True))
                for hh in range(16):
                    h = g * 16 + hh
                    fns.append(lambda e, h=h, hh=hh, g=g: e.matmul(
                        psyd[:, hh * 64:(hh + 1) * 64], egsb[g % 2][:, hh * 128:(hh + 1) * 128],
                        xsc[:, h * 64:(h + 1) * 64], start=True, stop=True))
                t_ymm = p.run_group("te", fns, waits=wv)
                prev[("ymm",)] = t_ymm
                prev[("egused", g % 2)] = t_ymm
                we1 = [t_ymm]
                if g == 0 and ("ytp",) in prev:
                    we1.append(prev[("ytp",)])
                t_e1 = p.run("ve", lambda e, g=g, c=c: e.tensor_tensor(
                    ytmp[:].rearrange("p (a d) -> p a d", a=16),
                    psyo[:, :].rearrange("p (a d) -> p a d", a=16),
                    ecstsb[:, c * 128 + g * 16:c * 128 + (g + 1) * 16].unsqueeze(2).to_broadcast([128, 16, HD]),
                    op=ALU.mult), waits=we1)
                prev[("yev",)] = p.run("ve", lambda e, g=g: e.tensor_add(
                    ytok[:, g * 1024:(g + 1) * 1024], ytmp[:], psyd[:, :]))
            for g in range(G):
                t_d1 = p.run("ve", lambda e, g=g: e.tensor_tensor(
                    ytmp[:].rearrange("p (a d) -> p a d", a=16),
                    xchc[:, g * 1024:(g + 1) * 1024].rearrange("p (a d) -> p a d", a=16),
                    drep[:, g * 16:(g + 1) * 16].unsqueeze(2).to_broadcast([128, 16, HD]),
                    op=ALU.mult), waits=[prev[("yev",)], tok_x, tok_k] if g == 0 else [])
                prev[("yev",)] = p.run("ve", lambda e, g=g: e.tensor_add(
                    ytok[:, g * 1024:(g + 1) * 1024], ytok[:, g * 1024:(g + 1) * 1024], ytmp[:]))
            t_y = prev[("yev",)]
            for fg in range(16):
                wv = [t_y]
                if ("ytp",) in prev:
                    wv.append(prev[("ytp",)])
                fns = [(lambda e, f=fg * 4 + j: e.transpose(
                    psyt[:, (f % 4) * 128:(f % 4 + 1) * 128],
                    ytok[:, f * 128:(f + 1) * 128], identS[:])) for j in range(4)]
                t_tp = p.runs("te", fns, waits=wv, self_sync=False)
                prev[("ytp",)] = p.run("ve", lambda e, fg=fg: e.tensor_copy(
                    out=ygfm[:, fg * 512:(fg + 1) * 512], in_=psyt[:, :]), waits=[t_tp])
            t_gate = p.run("ve", lambda e: e.tensor_mul(ygfm[:], ygfm[:], gslc[:]),
                           waits=[prev[("ytp",)]])
            prev[("xcon", c)] = t_gate
            for g in range(G):
                wq = [t_gate] + ([prev[("sqmm",)]] if ("sqmm",) in prev else [])
                t_sq = p.run("se", lambda e, g=g: e.activation(
                    out=sqb[:], in_=ygfm[:, g * 1024:(g + 1) * 1024], func=AF.Square), waits=wq)
                fns = [(lambda e, f=f, g=g: e.matmul(
                    psyo[0:1, g * 128:(g + 1) * 128], onescol[:], sqb[:, f * 128:(f + 1) * 128],
                    start=(f == 0), stop=(f == 7))) for f in range(8)]
                wmm = [t_sq, t_one]
                if g == 0:
                    wmm.append(prev[("yev",)])
                    if ("strd",) in prev:
                        wmm.append(prev[("strd",)])
                prev[("sqmm",)] = p.run_group("te", fns, waits=wmm)
            t_st_mm = prev[("sqmm",)]
            t_r1 = p.run("ve", lambda e: e.tensor_scalar(gsrow[:], psyo[0:1, :], 1.0 / 1024.0, EPS_GN,
                                                         op0=ALU.mult, op1=ALU.add), waits=[t_st_mm])
            prev[("strd",)] = t_r1
            p.run("se", lambda e: e.activation(out=gsrow[:], in_=gsrow[:], func=AF.Ln), waits=[t_r1])
            t_r3 = p.run("se", lambda e: e.activation(out=gsrow[:], in_=gsrow[:], func=AF.Exp, scale=-0.5))
            wrw = [t_r3] + ([prev[("rrd",)]] if ("rrd",) in prev else [])
            tok_rw = p.dma("sy", rscr, gsrow[:], waits=wrw, sem_name="rsw")
            wrr = [tok_rw] + ([prev[("rst",)]] if ("rst",) in prev else [])
            tok_rr = p.dma("sy", rstdrep[:], rscr.to_broadcast([128, 1024]), waits=wrr, sem_name="rsr")
            prev[("rrd",)] = tok_rr
            fns = []
            for f in range(64):
                g = f // 8
                fns.append(lambda e, f=f, g=g: e.scalar_tensor_tensor(
                    ynfmc[:, f * 128:(f + 1) * 128], ygfm[:, f * 128:(f + 1) * 128],
                    nwsb[:, f:f + 1], rstdrep[:, g * 128:(g + 1) * 128],
                    op0=ALU.mult, op1=ALU.mult))
            wap = [tok_rr, tok_k] + ([prev[("ynw",)]] if ("ynw",) in prev else [])
            t_ap = p.runs("ve", fns, waits=wap)
            prev[("rst",)] = t_ap
            dst = ynfm_o[:, :, c * CS:(c + 1) * CS].rearrange("f p k -> p f k")
            prev[("ynw",)] = p.dma("gp", dst, ynfmc[:].rearrange("p (f k) -> p f k", f=64),
                                   waits=[t_ap], sem_name="ynq")
            prev[("ld", c)] = t_ymm

    p.phase_barrier()
    with p.scoped() as s:
        ynsb = s.sb("ynsb", [128, 64 * 1024], BF16)
        wsl = [s.sb(f"opwsl{j}", [128, 64 * 128], BF16) for j in range(2)]
        sout = [s.sb(f"sout{j}", [128, 1024], BF16) for j in range(2)]
        psA = [s.ps(f"oppA{i}", [128, 512]) for i in range(2)]
        psB = [s.ps(f"oppB{i}", [128, 512]) for i in range(2)]
        psets = [psA, psB]
        p._mark_ded("opd")
        for half in range(2):
            for sp in range(4):
                p.dma("sy", ynsb[:, sp * 16 * 1024:(sp + 1) * 16 * 1024].rearrange(
                    "p (f k) -> p f k", f=16),
                    ynfm_o[sp * 16:(sp + 1) * 16, :, half * 1024:(half + 1) * 1024].rearrange("f p k -> p f k"),
                    waits=[("opd", p.cg("opd"))] if sp == 0 else [], sem_name="ynl")
            tok_y = ("ynl", p.c("ynl"))
            wtok, mmtok, evtok = {}, {}, {}
            for o in range(DCH):
                wd = [mmtok[o - 2]] if o >= 2 else ([("opd", p.cg("opd"))] if half else [])
                wtok[o] = p.dma("gp", wsl[o % 2][:], opw[o].rearrange("p d q -> p (d q)"),
                                waits=wd, sem_name=f"opw{o % 2}")
                pset = psets[o % 2]
                w = [wtok[o], tok_y]
                if o >= 2:
                    w.append(evtok[o - 2])
                elif half == 1:
                    w.append(ev_last)
                fns = []
                for dc in range(64):
                    for tb in range(2):
                        fns.append(lambda e, dc=dc, tb=tb, o=o: e.matmul(
                            pset[tb][:, :], wsl[o % 2][:, dc * 128:(dc + 1) * 128],
                            ynsb[:, dc * 1024 + tb * 512:dc * 1024 + (tb + 1) * 512],
                            start=(dc == 0), stop=(dc == 63)))
                mmtok[o] = p.run_group("te", fns, waits=w)
                we = [mmtok[o]]
                if o >= 2 or half == 1:
                    we.append((f"opq{o % 2}", p.cg(f"opq{o % 2}")))
                evtok[o] = p.runs("ve", [(lambda e, tb=tb, o=o: e.tensor_copy(
                    out=sout[o % 2][:, tb * 512:(tb + 1) * 512], in_=pset[tb][:, :])) for tb in range(2)],
                    waits=we)
                p.dma("sy", s0fm_o[o, :, half * 1024:(half + 1) * 1024], sout[o % 2][:],
                      waits=[evtok[o]], sem_name=f"opq{o % 2}")
            ev_last = evtok[DCH - 1]
            p.dma("gp", rscr[:, 0:1], sout[(DCH - 1) % 2][0:1, 0:1], waits=[mmtok[DCH - 1], evtok[DCH - 1]],
                  sem_name="opd")

    allq = ["ynq", "opq0", "opq1"]
    return [(p._phys(q), p.cnt[p._phys(q)]) for q in allq if p._phys(q) in p.cnt]


def build_ssm2():
    p = Prog()
    io = {"xs": p.din("xs", [NC, 128, DIN], BF16),
          "xch": p.din("xch", [NC, 128, DIN], BF16),
          "bfm": p.din("bfm", [G, 128, T], BF16),
          "cfm": p.din("cfm", [G, 128, T], BF16),
          "cst": p.din("cst", [NC, 128, 128], F32),
          "ncst": p.din("ncst", [NC, 128, 128], F32),
          "ecst": p.din("ecst", [NC, 128, 128], F32),
          "pfac": p.din("pfac", [128, NC], F32),
          "owncum": p.din("owncum", [NC, NH, NST * HD], F32),
          "sinit": p.din("sinit", [NH, NST * HD], F32),
          "rolemask": p.din("rolemask", [128, 1], F32),
          "gsilu": p.din("gsilu", [DIN // 128, 128, T], BF16),
          "normw": p.din("normw", [128, DIN // 128], F32),
          "dvec": p.din("dvec", [1, 128], F32),
          "identb": p.din("identb", [128, 128], BF16),
          "identf": p.din("identf", [128, 128], F32),
          "masklow_in": p.din("masklow_in", [128, 128], F32),
          "opw": p.din("opw", [DCH, 128, DIN // 128, 128], F32),
          "ynfm": p.dout("ynfm", [DIN // 128, 128, T], BF16),
          "s0fm": p.dout("s0fm", [DCH, 128, T], BF16)}
    p.build()
    toks = emit_ssm2(p, p, io)
    p.finish(toks)
    return p


def host_ssm2_inputs(inputs, l3outs, shand_list):
    wT = np.ascontiguousarray(inputs["L0_attn_out_proj_w"].T)
    opw = np.ascontiguousarray(wT.reshape(64, 128, DCH, 128).transpose(2, 1, 0, 3)).astype(np.float32)
    identB = np.eye(128, dtype=np.float32).astype(BF)
    identF = np.eye(128, dtype=np.float32)
    ml = np.where(np.arange(128)[None, :] < np.arange(128)[:, None], np.float32(NEG_BIG), np.float32(0.0))
    nw = np.ascontiguousarray(inputs["L0_attn_norm_w"].reshape(64, 128).T).astype(np.float32)
    dv = np.ascontiguousarray(inputs["L0_attn_D"].reshape(1, 128)).astype(np.float32)
    maps = []
    for c, o3 in enumerate(l3outs):
        sinit = (np.zeros((NH, NST * HD), dtype=np.float32) if c % 2 == 0
                 else np.ascontiguousarray(shand_list[c - 1]))
        maps.append({"xs": o3["xs"], "xch": o3["xch"], "bfm": o3["bfm"], "cfm": o3["cfm"],
                     "cst": o3["cst"], "ncst": o3["ncst"], "ecst": o3["ecst"], "pfac": o3["pfac"],
                     "owncum": o3["owncum"], "sinit": sinit, "gsilu": o3["gsilu"],
                     "rolemask": np.ones((128, 1), dtype=np.float32),
                     "normw": nw, "dvec": dv, "identb": identB, "identf": identF,
                     "masklow_in": ml, "opw": opw})
    return maps



def emit_ffn0(p, al, io, npart=8):
    NSLOT = 1024
    xnfm = io["xnfm"]; lvrow_i = io["lvrow"]; lvparts_i = io["lvparts"]
    wqT = io["wqT"]; wrT = io["wrT"]; keysT = io["keysT"]; logrel = io["logrel"]
    wu1 = io["wu1"]; bu1 = io["bu1"]; wu2T = io["wu2T"]; bu2 = io["bu2"]
    gw1 = io["gw1"]; gb = io["gb"]; vals = io["vals"]; wvdT = io["wvdT"]; wvuT = io["wvuT"]
    identb = io["identb"]; g0fm_o = io["g0fm"]
    wtsb = al.sb("wtsb", [128, 8 * T], BF16)
    identS = al.sb("identS", [128, 128], BF16)
    minv = al.sb("minv", [1, 1], F32)
    gwsb = al.sb("gwsb", [1, 1], F32)
    gbsb = al.sb("gbsb", [1, 1], F32)
    bu2sb = al.sb("bu2sb", [1, 1], F32)
    bu1sb = al.sb("bu1sb", [128, 8], F32)

    gscr = p.dint("gscr", [1, T], F32)

    for (dst, srcap) in [(identS[:], identb), (gwsb[:], gw1), (gbsb[:], gb),
                         (bu2sb[:], bu2), (bu1sb[:], bu1)]:
        p.dma("sy", dst, srcap, sem_name="kq")
    tok_k = [("kq", p.c("kq"))]

    outer = p.stack.enter_context(p.scoped.__wrapped__(p)) if False else None
    from contextlib import ExitStack
    es_outer = p.stack.enter_context(ExitStack())
    def osb(name, shape, dtype):
        return es_outer.enter_context(p.nc.sbuf_tensor(name, shape, dtype))
    queryfm = osb("queryfm", [64, T], BF16)
    wrsb = osb("wrsb", [64, 48], BF16)
    keysb = osb("keysb", [64, NSLOT], BF16)
    rkfm = osb("rkfm", [48, NSLOT], BF16)
    rqfm = osb("rqfm", [48, T], BF16)
    h1sb = osb("h1sb", [128, 8 * T], BF16)
    for (dst, srcap) in [(wrsb[:], wrT), (keysb[:], keysT)]:
        p.dma("gp", dst, srcap, sem_name="kqg")
    tok_kg = ("kqg", p.c("kqg"))

    with p.scoped() as s:
        xnsb = s.sb("xnsb", [128, DCH * 1024], BF16)
        wqsb = s.sb("wqsb", [128, DCH * 64], BF16)
        gsc = [s.sb(f"gsc{j}", [128, 512], F32) for j in range(3)]
        wusl = [s.sb(f"wusl{j}", [128, DCH * 128], BF16) for j in range(2)]
        psq = [s.ps(f"psq{i}", [64, 512]) for i in range(2)]
        psA = [s.ps(f"ffpA{i}", [128, 512]) for i in range(2)]
        psB = [s.ps(f"ffpB{i}", [128, 512]) for i in range(2)]
        HT = 1024
        tok_wq = p.dma("gp", wqsb[:].rearrange("p (d q) -> p d q", d=DCH), wqT, sem_name="wqq")
        ev_q = {}
        mm_half = {}
        for half in range(2):
            xntok = None
            for sp in range(4):
                wld = [mm_half[half - 1]] if (half >= 1 and sp == 0) else []
                src_ = xnfm[sp * 8:(sp + 1) * 8, :, half * HT:(half + 1) * HT].rearrange("d p t -> p d t")
                dst_ = xnsb[:, sp * 8 * HT:(sp + 1) * 8 * HT].rearrange("p (d t) -> p d t", d=8)
                p.dma("sy", dst_, src_, waits=wld, sem_name="xnq")
            xntok = ("xnq", p.c("xnq"))
            for tb in range(2):
                fns = [(lambda e, dc=dc, tb=tb: e.matmul(
                    psq[tb][:, :], wqsb[:, dc * 64:(dc + 1) * 64],
                    xnsb[:, dc * HT + tb * 512:dc * HT + (tb + 1) * 512],
                    start=(dc == 0), stop=(dc == DCH - 1))) for dc in range(DCH)]
                w = [xntok, tok_wq]
                if (half, tb) != (0, 0):
                    w.append(ev_q.get((half, tb - 1), ev_q.get((half - 1, 1))))
                t_mm = p.run_group("te", fns, waits=w)
                ev_q[(half, tb)] = p.run("ve", lambda e, half=half, tb=tb: e.tensor_copy(
                    out=queryfm[:, half * HT + tb * 512:half * HT + (tb + 1) * 512],
                    in_=psq[tb][:, :]), waits=[t_mm])
            wtok, mmtok, evtok = {}, {}, {}
            for o in range(8):
                wd = [mmtok[o - 2]] if o >= 2 else ([mm_half[half - 1]] if half >= 1 else [])
                wtok[o] = p.dma("gp", wusl[o % 2][:], wu1[o].rearrange("p d q -> p (d q)"),
                                waits=wd, sem_name=f"wuq{o % 2}")
                pset = (psA, psB)[o % 2]
                w = [wtok[o], xntok]
                if o >= 2:
                    w.append(evtok[o - 2])
                elif half == 1:
                    w.append(ev_prev)
                fns = []
                for dc in range(DCH):
                    for tb in range(2):
                        fns.append(lambda e, dc=dc, tb=tb, o=o: e.matmul(
                            pset[tb][:, :], wusl[o % 2][:, dc * 128:(dc + 1) * 128],
                            xnsb[:, dc * HT + tb * 512:dc * HT + (tb + 1) * 512],
                            start=(dc == 0), stop=(dc == DCH - 1)))
                mmtok[o] = p.run_group("te", fns, waits=w)
                last = None
                for tb in range(2):
                    pm = pset[tb]
                    off = o * T + half * HT + tb * 512
                    t0 = p.run("ve", lambda e, pm=pm, o=o: e.tensor_scalar_add(
                        gsc[0][:], pm[:, :], bu1sb[:, o:o + 1]), waits=[mmtok[o]] + tok_k)
                    t1 = p.run("se", lambda e: e.activation(out=gsc[1][:], in_=gsc[0][:], func=AF.Square),
                               waits=[t0])
                    t2 = p.run("ve", lambda e: e.tensor_scalar(gsc[1][:], gsc[1][:], 0.044715, 1.0,
                                                               op0=ALU.mult, op1=ALU.add), waits=[t1])
                    t3 = p.run("ve", lambda e: e.tensor_mul(gsc[1][:], gsc[1][:], gsc[0][:]))
                    t4 = p.run("se", lambda e: e.activation(out=gsc[2][:], in_=gsc[1][:], func=AF.Tanh,
                                                            scale=0.7978845608028654), waits=[t3])
                    t5 = p.run("ve", lambda e: e.tensor_scalar(gsc[2][:], gsc[2][:], 0.5, 0.5,
                                                               op0=ALU.mult, op1=ALU.add), waits=[t4])
                    last = p.run("ve", lambda e, off=off: e.tensor_mul(
                        h1sb[:, off:off + 512], gsc[2][:], gsc[0][:]))
                evtok[o] = last
            ev_prev = evtok[7]
            mm_half[half] = mmtok[7]
        t_rk = {}
        for nb in range(2):
            t_mm = p.run("te", lambda e, nb=nb: e.matmul(
                psq[nb][0:48, :], wrsb[:], keysb[:, nb * 512:(nb + 1) * 512],
                start=True, stop=True), waits=[tok_kg, ev_q[(1, nb)], mm_half[1]], self_sync=False)
            t_rk[nb] = p.run("ve", lambda e, nb=nb: e.tensor_copy(
                out=rkfm[:, nb * 512:(nb + 1) * 512], in_=psq[nb][0:48, :]), waits=[t_mm])
        t_q_all0 = ("ve", p.c("ve"))
        for tb in range(4):
            t_mm = p.run("te", lambda e, tb=tb: e.matmul(
                psq[tb % 2][0:48, :], wrsb[:], queryfm[:, tb * 512:(tb + 1) * 512],
                start=True, stop=True), waits=[t_rk.get(tb, ("ve", p.c("ve")))], self_sync=False)
            p.run("ve", lambda e, tb=tb: e.tensor_copy(
                out=rqfm[:, tb * 512:(tb + 1) * 512], in_=psq[tb % 2][0:48, :]), waits=[t_mm])
        t_q_all = ("ve", p.c("ve"))

    p.phase_barrier()
    with p.scoped() as s:
        lrrep = s.sb("lrrep", [128, NSLOT], F32)
        lrow = s.sb("lrow", [1, T], F32)
        urow = s.sb("urow", [1, T], F32)
        trow = s.sb("trow", [1, T], F32)
        lvrow = s.sb("lvrow_sb", [1, T], F32)
        lvp = s.sb("lvp", [1, npart], F32)
        for (dst, srcap) in [(lrrep[:], logrel.to_broadcast([128, NSLOT])),
                             (lvrow[:], lvrow_i), (lvp[:], lvparts_i)]:
            p.dma("sy", dst, srcap, sem_name="kq2")
        tok_k2 = [("kq2", p.c("kq2"))]
        scr = s.sb("scr", [128, NSLOT], F32)
        top8 = s.sb("top8", [128, 8], F32)
        mask8 = s.sb("mask8", [128, NSLOT], F32)
        sc2 = s.sb("sc2", [128, NSLOT], F32)
        mcol = s.sb("mcol", [128, 1], F32)
        zcol = s.sb("zcol", [128, 1], F32)
        esb = s.sb("esb", [128, NSLOT], F32)
        wsb = s.sb("wsb", [128, NSLOT], BF16)
        psr = [s.ps(f"psr{i}", [128, 512]) for i in range(2)]
        ps2 = [s.ps(f"ps2{i}", [128, 512]) for i in range(2)]
        pst = s.ps("pstw", [128, 512], BF16)
        SQ48 = float(1.0 / np.sqrt(48.0))
        prev = {}
        for i in range(TT):
            wv = [("vscr", p.cg("vscr"))]
            fns = [(lambda e, nb=nb, i=i: e.matmul(
                psr[nb][:, :], rqfm[:, i * 128:(i + 1) * 128], rkfm[:, nb * 512:(nb + 1) * 512],
                start=True, stop=True)) for nb in range(2)]
            t_smm = p.run_group("te", fns, waits=wv)
            t_scr = p.runs("vscr", [(lambda e, nb=nb: e.scalar_tensor_tensor(
                scr[:, nb * 512:(nb + 1) * 512], psr[nb][:, :], SQ48, lrrep[:, nb * 512:(nb + 1) * 512],
                op0=ALU.mult, op1=ALU.add)) for nb in range(2)], waits=[t_smm] + tok_k2)
            t_t8 = p.run("vscr", lambda e: e.max(out=top8[:], in_=scr[:]))
            t_msk = p.run("vscr", lambda e: e.tensor_single_scalar(mask8[:], scr[:], top8[:, 7:8], op=ALU.is_ge))
            wv = [("vsc2", p.cg("vsc2"))]
            fns = [(lambda e, nb=nb, i=i: e.matmul(
                ps2[nb][:, :], queryfm[:, i * 128:(i + 1) * 128], keysb[:, nb * 512:(nb + 1) * 512],
                start=True, stop=True)) for nb in range(2)]
            t_qmm = p.run_group("te", fns, waits=wv)
            t_sc2 = p.runs("vsc2", [(lambda e, nb=nb: e.scalar_tensor_tensor(
                sc2[:, nb * 512:(nb + 1) * 512], ps2[nb][:, :], 0.125, lrrep[:, nb * 512:(nb + 1) * 512],
                op0=ALU.mult, op1=ALU.add)) for nb in range(2)], waits=[t_qmm] + tok_k2)
            fns = [lambda e: e.tensor_scalar_add(sc2[:], sc2[:], 1e30),
                   lambda e: e.tensor_mul(sc2[:], sc2[:], mask8[:]),
                   lambda e: e.tensor_scalar_add(sc2[:], sc2[:], -1e30),
                   lambda e: e.reduce_max(mcol[:], sc2[:], axis=AX.X),
                   lambda e: e.tensor_scalar_mul(mcol[:], mcol[:], -1.0)]
            t_pre = p.runs("vsc2", fns, waits=[t_msk])
            t_ex = p.run("se", lambda e: e.activation(out=esb[:], in_=sc2[:], func=AF.Exp,
                                                      bias=mcol[:], accum_out=zcol[:]), waits=[t_pre])
            t_rc = p.run("vsc2", lambda e: e.reciprocal(zcol[:], zcol[:]), waits=[t_ex])
            t_w = p.run("vsc2", lambda e: e.tensor_scalar_mul(wsb[:], esb[:], zcol[:]))
            for nb2 in range(2):
                wv = [t_w, tok_k[0]]
                if ("wtp",) in prev:
                    wv.append(prev[("wtp",)])
                fns = [(lambda e, nc4=nb2 * 4 + j: e.transpose(
                    pst[:, (nc4 % 4) * 128:(nc4 % 4 + 1) * 128],
                    wsb[:, nc4 * 128:(nc4 + 1) * 128], identS[:])) for j in range(4)]
                t_tp = p.runs("te", fns, waits=wv, self_sync=False)
                prev[("wtp",)] = p.run("vsc2", lambda e, nb2=nb2, i=i: e.tensor_copy(
                    out=wtsb[:, :].rearrange("p (n t) -> p n t", n=8)[:, nb2 * 4:(nb2 + 1) * 4,
                                                                     i * 128:(i + 1) * 128],
                    in_=pst[:, :].rearrange("p (a k) -> p a k", a=4)), waits=[t_tp])

        psu = [psr[0], psr[1], ps2[0], ps2[1]]
        wu2sb = s.sb("wu2sb", [128, 8], BF16)
        p.dma("gp", wu2sb[:], wu2T, sem_name="kqg")
        tok_w2 = ("kqg", p.c("kqg"))
        wfree = [("vscr", p.c("vscr")), ("vsc2", p.c("vsc2"))]
        for tb in range(4):
            fns = [(lambda e, ch=ch, tb=tb: e.matmul(
                psu[tb][0:1, :], wu2sb[:, ch:ch + 1],
                h1sb[:, ch * T + tb * 512:ch * T + (tb + 1) * 512],
                start=(ch == 0), stop=(ch == 7))) for ch in range(8)]
            t_mm = p.run_group("te", fns, waits=[tok_w2] + wfree)
            p.run("vu", lambda e, tb=tb: e.tensor_copy(out=lrow[:, tb * 512:(tb + 1) * 512],
                                                        in_=psu[tb][0:1, :]), waits=[t_mm])
        t_l = p.run("vu", lambda e: e.tensor_scalar_add(lrow[:], lrow[:], bu2sb[:, :]), waits=tok_k)
        t_sg = p.run("se", lambda e: e.activation(out=lrow[:], in_=lrow[:], func=AF.Sigmoid), waits=[t_l])
        t_s1 = p.run("vu", lambda e: e.reduce_sum(minv[:], lvp[:], axis=AX.X), waits=tok_k2)
        t_s2 = p.run("vu", lambda e: e.tensor_scalar(minv[:], minv[:], 1.0 / 16384.0, 1e-6,
                                                      op0=ALU.mult, op1=ALU.add))
        t_s3 = p.run("vu", lambda e: e.reciprocal(minv[:], minv[:]))
        t_nv = p.run("vu", lambda e: e.tensor_scalar_mul(trow[:], lvrow[:], minv[:, :]))
        t_u1 = p.run("vu", lambda e: e.tensor_scalar_mul(trow[:], trow[:], 0.5))
        t_u2 = p.run("vu", lambda e: e.scalar_tensor_tensor(urow[:], lrow[:], 2.5, trow[:],
                                                             op0=ALU.mult, op1=ALU.add), waits=[t_sg])
        t_u3 = p.run("vu", lambda e: e.tensor_scalar(urow[:], urow[:], 5.0, 0.0, op0=ALU.min, op1=ALU.max))
        t_u4 = p.run("vu", lambda e: e.tensor_scalar(urow[:], urow[:], 0.5, 1.0 / 1.5,
                                                      op0=ALU.subtract, op1=ALU.mult))
        t_u5 = p.run("vu", lambda e: e.tensor_scalar(urow[:], urow[:], 1.0, 0.0, op0=ALU.min, op1=ALU.max))
        t_g1 = p.run("se", lambda e: e.activation(out=urow[:], in_=urow[:], func=AF.Sigmoid,
                                                  scale=gwsb[:, :], bias=gbsb[:, :]), waits=[t_u5] + tok_k)
        t_g2 = p.run("vu", lambda e: e.tensor_single_scalar(trow[:], urow[:], 0.05, op=ALU.is_ge),
                     waits=[t_g1])
        t_g3 = p.run("vu", lambda e: e.tensor_mul(urow[:], urow[:], trow[:]))
        tok_gw = p.dma("sy", gscr, urow[:], waits=[t_g3], sem_name="gwq")

    es_outer.close()
    p.phase_barrier()
    with p.scoped() as s:
        grep = s.sb("grep", [128, T], F32)
        tok_gr = p.dma("sy", grep[:], gscr.to_broadcast([128, T]), waits=[("gwq", 16)], sem_name="grq")
        auxsb = s.sb("auxsb", [128, DCH * T], BF16)
        psA = [s.ps(f"fxpA{i}", [128, 512]) for i in range(4)]
        psB = [s.ps(f"fxpB{i}", [128, 512]) for i in range(4)]

        def evac_aux(o, pset, tok_mm):
            return p.runs("ve", [(lambda e, tb=tb, o=o: e.tensor_copy(
                out=auxsb[:, o * T + tb * 512:o * T + (tb + 1) * 512], in_=pset[tb][:, :]))
                for tb in range(4)], waits=[tok_mm])

        emit_stream_mm(p, "aux", vals, DCH, 8, wtsb, ("vsc2", p.c("vsc2")), evac_aux, (psA, psB), alloc=s)

        avdsb = s.sb("avdsb", [128, 2 * T], BF16)

        def evac_avd(o, pset, tok_mm):
            return p.runs("ve", [(lambda e, tb=tb, o=o: e.tensor_copy(
                out=avdsb[:, o * T + tb * 512:o * T + (tb + 1) * 512], in_=pset[tb][:, :]))
                for tb in range(4)], waits=[tok_mm])

        emit_stream_mm(p, "avd", wvdT, 2, DCH, auxsb, ("ve", p.c("ve")), evac_avd, (psA, psB), alloc=s)

        xnt = [s.sb(f"xnt{j}", [128, T], BF16) for j in range(2)]
        g1f = s.sb("g1f", [128, 512], F32)
        xl_tok = {}

        sil_prev = [None]

        def evac_g(o, pset, tok_mm):
            b = o % 2
            w = [xl_tok[o]] if o in xl_tok else []
            last = None
            for tb in range(4):
                wv = ([tok_mm, tok_gr] + w if tb == 0 else [tok_mm])
                if sil_prev[0] is not None:
                    wv = wv + [sil_prev[0]]
                t1 = p.run("ve", lambda e, tb=tb, b=b: e.tensor_mul(
                    g1f[:], pset[tb][:, :], grep[:, tb * 512:(tb + 1) * 512]),
                    waits=wv)
                t2 = p.run("ve", lambda e, tb=tb, b=b: e.tensor_add(
                    g1f[:], g1f[:], xnt[b][:, tb * 512:(tb + 1) * 512]))
                last = p.run("se", lambda e, tb=tb, o=o: e.activation(
                    out=auxsb[:, o * T + tb * 512:o * T + (tb + 1) * 512],
                    in_=g1f[:], func=AF.Silu), waits=[t2])
                sil_prev[0] = last
            tok_o = p.dma("sy", g0fm_o[o], auxsb[:, o * T:(o + 1) * T], waits=[last], sem_name=f"goq{o % 2}")
            return last

        def pre_mm_g(o):
            b = o % 2
            w = [("goq" + str(b), p.cnt.get("goq" + str(b), 0))]
            xl_tok[o] = p.dma("sy", xnt[b][:], xnfm[o], waits=w, sem_name=f"xlt{b}")

        wvsl = [s.sb(f"wvsl{j}", [128, 2 * 128], BF16) for j in range(2)]
        wtok, mmtok, evtok = {}, {}, {}
        for o in range(DCH):
            wd = [mmtok[o - 2]] if o >= 2 else []
            wtok[o] = p.dma("gp", wvsl[o % 2][:], wvuT[o].rearrange("p d q -> p (d q)"),
                            waits=wd, sem_name=f"wvq{o % 2}")
            pre_mm_g(o)
            pset = (psA, psB)[o % 2]
            w = [wtok[o], ("ve", p.c("ve"))] if o == 0 else [wtok[o]]
            if o >= 2:
                w.append(evtok[o - 2])
            fns = []
            for dc in range(2):
                for tb in range(4):
                    fns.append(lambda e, dc=dc, tb=tb, o=o: e.matmul(
                        pset[tb][:, :], wvsl[o % 2][:, dc * 128:(dc + 1) * 128],
                        avdsb[:, dc * T + tb * 512:dc * T + (tb + 1) * 512],
                        start=(dc == 0), stop=(dc == 1)))
            mmtok[o] = p.run_group("te", fns, waits=w)
            evtok[o] = evac_g(o, pset, mmtok[o])

    return [(p._phys("goq0"), p.c("goq0")), (p._phys("goq1"), p.c("goq1"))]


def build_ffn0():
    p = Prog()
    NSLOT = 1024
    io = {"xnfm": p.din("xnfm", [DCH, 128, T], BF16),
          "lvrow": p.din("lvrow", [1, T], F32),
          "lvparts": p.din("lvparts", [1, 8], F32),
          "wqT": p.din("wqT", [128, DCH, 64], F32),
          "wrT": p.din("wrT", [64, 48], F32),
          "keysT": p.din("keysT", [64, NSLOT], F32),
          "logrel": p.din("logrel", [1, NSLOT], F32),
          "wu1": p.din("wu1", [8, 128, DCH, 128], F32),
          "bu1": p.din("bu1", [128, 8], F32),
          "wu2T": p.din("wu2T", [128, 8], F32),
          "bu2": p.din("bu2", [1, 1], F32),
          "gw1": p.din("gw1", [1, 1], F32),
          "gb": p.din("gb", [1, 1], F32),
          "vals": p.din("vals", [DCH, 128, 8, 128], F32),
          "wvdT": p.din("wvdT", [2, 128, DCH, 128], F32),
          "wvuT": p.din("wvuT", [DCH, 128, 2, 128], F32),
          "identb": p.din("identb", [128, 128], BF16),
          "g0fm": p.dout("g0fm", [DCH, 128, T], BF16)}
    p.build()
    toks = emit_ffn0(p, p, io)
    p.finish(toks)
    return p


def host_ffn0_inputs(inputs, xnfm_list, lvrow_list, lvparts):
    wqT = np.ascontiguousarray(inputs["L0_ffn_Wq"].T.reshape(DCH, 128, 64).transpose(1, 0, 2)).astype(np.float32)
    wrT = np.ascontiguousarray(inputs["L0_ffn_Wr"].T).astype(np.float32)
    keysT = np.ascontiguousarray(inputs["L0_ffn_aux_keys"].T).astype(np.float32)
    logrel = np.ascontiguousarray(inputs["L0_ffn_log_rel"].reshape(1, 1024)).astype(np.float32)
    wu1 = np.ascontiguousarray(inputs["L0_ffn_Wu1"].T.reshape(DCH, 128, 8, 128).transpose(2, 1, 0, 3)).astype(np.float32)
    bu1 = np.ascontiguousarray(inputs["L0_ffn_bu1"].reshape(8, 128).T).astype(np.float32)
    wu2T = np.ascontiguousarray(inputs["L0_ffn_Wu2"].reshape(8, 128).T).astype(np.float32)
    bu2 = np.asarray(inputs["L0_ffn_bu2"]).reshape(1, 1).astype(np.float32)
    gw1 = np.asarray(inputs["L0_ffn_gate_w1"]).reshape(1, 1).astype(np.float32)
    gb = np.asarray(inputs["L0_ffn_gate_bias"]).reshape(1, 1).astype(np.float32)
    vals = np.ascontiguousarray(np.asarray(inputs["L0_ffn_aux_values"]).reshape(8, 128, DCH, 128)
                                .transpose(2, 1, 0, 3)).astype(np.float32)
    wvdT = np.ascontiguousarray(inputs["L0_ffn_Wvd"].T.reshape(DCH, 128, 2, 128).transpose(2, 1, 0, 3)).astype(np.float32)
    wvuT = np.ascontiguousarray(inputs["L0_ffn_Wvu"].T.reshape(2, 128, DCH, 128).transpose(2, 1, 0, 3)).astype(np.float32)
    identB = np.eye(128, dtype=np.float32).astype(BF)
    maps = []
    for c, xnfm in enumerate(xnfm_list):
        maps.append({"xnfm": xnfm, "lvrow": lvrow_list[c], "lvparts": lvparts,
                     "wqT": wqT, "wrT": wrT, "keysT": keysT, "logrel": logrel, "wu1": wu1,
                     "bu1": bu1, "wu2T": wu2T, "bu2": bu2, "gw1": gw1, "gb": gb, "vals": vals,
                     "wvdT": wvdT, "wvuT": wvuT, "identb": identB})
    return maps



def emit_res0(p, al, io):
    x = io["x"]; s0fm_i = io["s0fm"]; g0fm_i = io["g0fm"]; wqkv = io["wqkv"]
    routT = io["routT"]; acat = io["acat"]; ball = io["ball"]; identb = io["identb"]
    x1_o = io["x1"]; xn1fm_o = io["xn1fm"]; xn1lofm_o = io["xn1lofm"]
    g1fm_o = io["g1fm"]; qkvfm_o = io["qkvfm"]
    identS = al.sb("identS", [128, 128], BF16)
    gscr2 = p.dint("gscr2", [4, T], F32)

    p.dma("sy", identS[:], identb, sem_name="kq")
    tok_k = ("kq", p.c("kq"))

    with p.scoped() as s:
        ssl = [s.sb(f"ssl{j}", [128, DM], BF16) for j in range(2)]
        gsl = [s.sb(f"gsl{j}", [128, DM], BF16) for j in range(2)]
        xt = s.sb("xt", [128, DM], F32)
        sgt = s.sb("sgt", [128, 512], BF16)
        sgtok = s.sb("sgtok", [128, DM], F32)
        xnb = s.sb("xnb", [128, DM], BF16)
        xnf = s.sb("xnf", [128, DM], F32)
        xlob = s.sb("xlob", [128, DM], BF16)
        bst = s.sb("bst", [128, 6 * 8], F32)
        mv = s.sb("mv", [128, 2], F32)
        t1c = s.sb("t1c", [128, 1], F32)
        rstd = s.sb("rstd", [128, 1], F32)
        xev = [s.sb(f"xev{j}", [128, 512], BF16) for j in range(2)]
        pstA = s.ps("pstA", [128, 512], BF16)
        pstB = s.ps("pstB", [128, 512], BF16)

        prev = {}
        x1b = p.cg("x1q")
        for i in range(TT):
            b = i % 2
            wl = [prev[("sgp2", b)]] if ("sgp2", b) in prev else []
            p.dma("gp", ssl[b][:].rearrange("p (d t) -> p d t", d=DCH),
                  s0fm_i[:, :, i * 128:(i + 1) * 128].rearrange("d p t -> p d t"),
                  waits=wl, sem_name=f"ldq{b}")
            p.dma("gp", gsl[b][:].rearrange("p (d t) -> p d t", d=DCH),
                  g0fm_i[:, :, i * 128:(i + 1) * 128].rearrange("d p t -> p d t"), sem_name=f"ldq{b}")
            tok_ld = (f"ldq{b}", p.c(f"ldq{b}"))
            wx = [prev[("xn", i - 1)], ("x1q", x1b + 16 * i)] if i >= 1 else []
            tok_x = p.dma("sy", xt[:], x[i * 128:(i + 1) * 128, :], waits=wx, sem_name="xq")
            for dcg in range(8):
                wv = [tok_ld] + ([prev[("sgp",)]] if ("sgp",) in prev else [])
                t_sg = p.run("ve", lambda e, dcg=dcg, b=b: e.tensor_tensor(
                    sgt[:].rearrange("p (a k) -> p a k", a=4),
                    ssl[b][:].rearrange("p (d t) -> p d t", d=DCH)[:, dcg * 4:(dcg + 1) * 4, :],
                    gsl[b][:].rearrange("p (d t) -> p d t", d=DCH)[:, dcg * 4:(dcg + 1) * 4, :],
                    op=ALU.mult), waits=wv)
                q = dcg % 2
                ps = (pstA, pstB)[q]
                wv = [t_sg, tok_k]
                if ("sgev", q) in prev:
                    wv.append(prev[("sgev", q)])
                fns = [(lambda e, j=j, ps=ps: e.transpose(
                    ps[:, j * 128:(j + 1) * 128], sgt[:, j * 128:(j + 1) * 128], identS[:]))
                    for j in range(4)]
                t_tp = p.runs("te", fns, waits=wv, self_sync=False)
                prev[("sgp",)] = t_tp
                prev[("sgev", q)] = p.run("ve", lambda e, dcg=dcg, ps=ps: e.tensor_copy(
                    out=sgtok[:, dcg * 512:(dcg + 1) * 512], in_=ps[:, :]), waits=[t_tp])
            prev[("sgp2", b)] = prev[("sgp",)]
            t_x1 = p.run("ve", lambda e: e.tensor_add(xt[:], xt[:], sgtok[:]),
                         waits=[tok_x, prev[("sgev", 0)], prev[("sgev", 1)]])
            tok_x1 = p.dma("sy", x1_o[i * 128:(i + 1) * 128, :], xt[:], waits=[t_x1], sem_name="x1q")
            fns = [(lambda e, g=g: e.bn_stats(out=bst[:, 6 * g:6 * g + 6],
                                              in_=xt[:, 512 * g:512 * (g + 1)])) for g in range(8)]
            fns.append(lambda e: e.bn_aggr(out=mv[:], in_=bst[:]))
            fns.append(lambda e: e.tensor_scalar_add(t1c[:], mv[:, 1:2], EPS_LN))
            t_v = p.runs("ve", fns, waits=[t_x1])
            t_l1 = p.run("se", lambda e: e.activation(out=rstd[:], in_=t1c[:], func=AF.Ln), waits=[t_v])
            t_l2 = p.run("se", lambda e: e.activation(out=rstd[:], in_=rstd[:], func=AF.Exp, scale=-0.5))
            t_xnf = p.run("ve", lambda e: e.tensor_scalar(
                xnf[:], xt[:], mv[:, 0:1], rstd[:], op0=ALU.subtract, op1=ALU.mult), waits=[t_l2])
            t_xn = p.run("ve", lambda e: e.tensor_copy(out=xnb[:], in_=xnf[:]))
            t_lo = p.run("ve", lambda e: e.tensor_sub(xlob[:], xnf[:], xnb[:]))
            prev[("xn", i)] = t_xnf
            for dcg in range(8):
                q = dcg % 2
                ps = (pstA, pstB)[q]
                wv = [t_xn, tok_k]
                if ("sgev", q) in prev:
                    wv.append(prev[("sgev", q)])
                if ("xnev", q) in prev:
                    wv.append(prev[("xnev", q)])
                fns = [(lambda e, dc=dcg * 4 + j, ps=ps: e.transpose(
                    ps[:, (dc % 4) * 128:(dc % 4 + 1) * 128], xnb[:, dc * 128:(dc + 1) * 128], identS[:]))
                    for j in range(4)]
                t_tp = p.runs("te", fns, waits=wv, self_sync=False)
                prev[("sgp",)] = t_tp
                wv = [t_tp]
                if ("xnw", q) in prev:
                    wv.append(prev[("xnw", q)])
                t_e = p.run("ve2", lambda e, ps=ps, q=q: e.tensor_copy(out=xev[q][:], in_=ps[:, :]), waits=wv)
                dst = xn1fm_o[dcg * 4:(dcg + 1) * 4, :, i * 128:(i + 1) * 128].rearrange("a b c -> b a c")
                prev[("xnw", q)] = p.dma("gp", dst, xev[q][:].rearrange("p (a k) -> p a k", a=4),
                                         waits=[t_e], sem_name=f"xnw{q}")
                prev[("xnev", q)] = t_e
            for dcg in range(8):
                q = dcg % 2
                ps = (pstA, pstB)[q]
                wv = [t_lo, tok_k, prev[("xnev", q)]]
                fns = [(lambda e, dc=dcg * 4 + j, ps=ps: e.transpose(
                    ps[:, (dc % 4) * 128:(dc % 4 + 1) * 128], xlob[:, dc * 128:(dc + 1) * 128], identS[:]))
                    for j in range(4)]
                t_tp = p.runs("te", fns, waits=wv, self_sync=False)
                prev[("sgp",)] = t_tp
                wv = [t_tp]
                if ("xnw", q) in prev:
                    wv.append(prev[("xnw", q)])
                t_e = p.run("ve2", lambda e, ps=ps, q=q: e.tensor_copy(out=xev[q][:], in_=ps[:, :]), waits=wv)
                dst = xn1lofm_o[dcg * 4:(dcg + 1) * 4, :, i * 128:(i + 1) * 128].rearrange("a b c -> b a c")
                prev[("xnw", q)] = p.dma("gp", dst, xev[q][:].rearrange("p (a k) -> p a k", a=4),
                                         waits=[t_e], sem_name=f"xnw{q}")
                prev[("xnev", q)] = t_e
        tok_xnw = [("xnw0", p.c("xnw0")), ("xnw1", p.c("xnw1")), ("x1q", p.c("x1q"))]

    p.phase_barrier()
    with p.scoped() as so:
        xnsb = so.sb("xnsb2", [128, DCH * T], BF16)
        grep64 = so.sb("grep64", [64, T], BF16)
        midsb = so.sb("midsb", [64, T], BF16)
        xntok = load_xnsb(p, xnsb, xn1fm_o, DCH)

        with p.scoped() as s:
            rsb = s.sb("rsb", [128, DCH * 4], BF16)
            losl = [s.sb("losl0", [128, 8 * T], BF16)] * 2
            lgt = s.sb("lgt", [128, 8], F32)
            top2 = s.sb("top2", [128, 8], F32)
            nv1 = s.sb("nv1", [128, 1], F32)
            egt = s.sb("egt", [128, 4], F32)
            msk = s.sb("msk", [128, 4], F32)
            zc = s.sb("zc", [128, 1], F32)
            gts = s.sb("gts", [128, 4], BF16)
            gtr = s.sb("gtr", [4, T], F32)
            psl = s.ps("psl", [128, 64])
            pslb = s.ps("pslb", [4, 128], BF16)
            tok_r = p.dma("gp", rsb[:].rearrange("p (d q) -> p d q", d=DCH), routT, sem_name="rq")
            lotoks = {}
            mmtoks = {}
            for grp in range(4):
                b = grp % 2
                wl = [mmtoks[grp - 1]] if grp >= 1 else []
                src_ = xn1lofm_o[grp * 8:(grp + 1) * 8].rearrange("d p t -> p d t")
                dst_ = losl[b][:].rearrange("p (d t) -> p d t", d=8)
                lotoks[grp] = p.dma("sy", dst_, src_, waits=wl, sem_name=f"loq{b}")
                fns = []
                for dcg in range(8):
                    dc = grp * 8 + dcg
                    for i in range(TT):
                        fns.append(lambda e, dc=dc, i=i: e.matmul(
                            psl[:, i * 4:(i + 1) * 4], xnsb[:, dc * T + i * 128:dc * T + (i + 1) * 128],
                            rsb[:, dc * 4:(dc + 1) * 4], start=(dc == 0 and i == 0), stop=False))
                    for i in range(TT):
                        fns.append(lambda e, dc=dc, dcg=dcg, grp=grp, b=b, i=i: e.matmul(
                            psl[:, i * 4:(i + 1) * 4], losl[b][:, dcg * T + i * 128:dcg * T + (i + 1) * 128],
                            rsb[:, dc * 4:(dc + 1) * 4], start=False,
                            stop=(dc == DCH - 1 and i == TT - 1)))
                mmtoks[grp] = p.run_group("te", fns, waits=[xntok, lotoks[grp], tok_r])
            t_lmm = mmtoks[3]
            prev = {}
            for i in range(TT):
                fns = [lambda e: e.memset(lgt[:], NEG_BIG),
                       (lambda e, i=i: e.tensor_copy(out=lgt[:, 0:4], in_=psl[:, i * 4:(i + 1) * 4])),
                       lambda e: e.max(out=top2[:], in_=lgt[:]),
                       lambda e: e.tensor_scalar_mul(nv1[:], top2[:, 0:1], -1.0)]
                t_l = p.runs("ve", fns, waits=[t_lmm])
                t_e = p.run("se", lambda e: e.activation(out=egt[:], in_=lgt[:, 0:4], func=AF.Exp,
                                                         bias=nv1[:]), waits=[t_l])
                fns = [lambda e: e.tensor_single_scalar(msk[:], lgt[:, 0:4], top2[:, 1:2], op=ALU.is_ge),
                       lambda e: e.tensor_mul(egt[:], egt[:], msk[:]),
                       lambda e: e.reduce_sum(zc[:], egt[:], axis=AX.X),
                       lambda e: e.reciprocal(zc[:], zc[:]),
                       lambda e: e.tensor_scalar(gts[:], egt[:], zc[:], 4.0, op0=ALU.mult, op1=ALU.mult)]
                t_g = p.runs("ve", fns, waits=[t_e])
                wv = [t_g, tok_k] + ([prev[("gtp",)]] if ("gtp",) in prev else [])
                t_tp = p.run("te", lambda e: e.transpose(pslb[:, :], gts[:], identS[:]), waits=wv,
                             self_sync=False)
                prev[("gtp",)] = p.run("ve", lambda e, i=i: e.tensor_copy(
                    out=gtr[:, i * 128:(i + 1) * 128], in_=pslb[:, :]), waits=[t_tp])
            tok_gw = p.dma("sy", gscr2, gtr[:], waits=[prev[("gtp",)]], sem_name="gwq")
            for ee in range(4):
                p.dma("gp", grep64[16 * ee:16 * (ee + 1), :],
                      gscr2[ee:ee + 1, :].to_broadcast([16, T]), waits=[tok_gw] if ee == 0 else [],
                      sem_name="grq")
        tok_gr = ("grq", 64)

        p.phase_barrier()
        with p.scoped() as s:
            asb = s.sb("asb", [128, DCH * 64], BF16)
            bsl = [s.sb(f"bsl{j}", [64, 128], BF16) for j in range(2)]
            g1sc = [s.sb(f"g1sc{j}", [128, T], BF16) for j in range(2)]
            psm4 = [s.ps(f"psm4{i}", [64, 512]) for i in range(4)]
            psD = [s.ps(f"psD{i}", [128, 512]) for i in range(4)]
            tok_a = p.dma("gp", asb[:].rearrange("p (d q) -> p d q", d=DCH), acat, sem_name="rq")
            for tb in range(4):
                fns = [(lambda e, dc=dc, tb=tb: e.matmul(
                    psm4[tb][:, :], asb[:, dc * 64:(dc + 1) * 64],
                    xnsb[:, dc * T + tb * 512:dc * T + (tb + 1) * 512],
                    start=(dc == 0), stop=(dc == DCH - 1))) for dc in range(DCH)]
                t_mm = p.run_group("te", fns, waits=[xntok, tok_a])
                p.run("vm", lambda e, tb=tb: e.tensor_copy(out=midsb[:, tb * 512:(tb + 1) * 512],
                                                           in_=psm4[tb][:, :]), waits=[t_mm])
            t_ms = p.run("vm", lambda e: e.tensor_mul(midsb[:], midsb[:], grep64[:]), waits=[tok_gr])
            prev = {}
            for o in range(DCH):
                b = o % 2
                tok_b = p.dma("gp", bsl[b][:], ball[o],
                              waits=[prev[("dmm", b)]] if ("dmm", b) in prev else [], sem_name=f"bwq{b}")
                wv = [tok_b, t_ms] + ([prev[("dev",)]] if ("dev",) in prev else [])
                fns = [(lambda e, tb=tb, b=b: e.matmul(
                    psD[tb][:, :], bsl[b][:], midsb[:, tb * 512:(tb + 1) * 512],
                    start=True, stop=True)) for tb in range(4)]
                t_mm = p.run_group("te", fns, waits=wv)
                prev[("dmm", b)] = t_mm
                wv = [t_mm] + ([prev[("g1w", b)]] if ("g1w", b) in prev else [])
                t_ev = p.runs("se", [(lambda e, tb=tb, b=b: e.activation(
                    out=g1sc[b][:, tb * 512:(tb + 1) * 512], in_=psD[tb][:, :], func=AF.Silu))
                    for tb in range(4)], waits=wv)
                prev[("dev",)] = t_ev
                prev[("g1w", b)] = p.dma("sy", g1fm_o[o], g1sc[b][:], waits=[t_ev], sem_name=f"g1q{b}")
            tok_g1 = [("g1q0", p.c("g1q0")), ("g1q1", p.c("g1q1"))]

        p.phase_barrier()
        with p.scoped() as s:
            psA = [s.ps(f"qvA{i}", [128, 512]) for i in range(4)]
            psB = [s.ps(f"qvB{i}", [128, 512]) for i in range(4)]
            qsl = [s.sb(f"qsl{j}", [128, DCH * 128], BF16) for j in range(2)]
            qout = [s.sb(f"qout{j}", [128, T], BF16) for j in range(2)]

            def evac_qkv(o, pset, tok_mm):
                b = o % 2
                wv = [tok_mm] + ([otok[o - 2]] if o >= 2 else [])
                t_ev = p.runs("ve", [(lambda e, tb=tb, b=b: e.tensor_copy(
                    out=qout[b][:, tb * 512:(tb + 1) * 512], in_=pset[tb][:, :])) for tb in range(4)],
                    waits=wv)
                otok[o] = p.dma("sy", qkvfm_o[o], qout[b][:], waits=[t_ev], sem_name=f"qoq{b}")
                return t_ev

            otok = {}
            emit_stream_mm(p, "qkv", wqkv, 34, DCH, xnsb, xntok, evac_qkv, (psA, psB), alloc=s)
            tok_qo = [("qoq0", p.c("qoq0")), ("qoq1", p.c("qoq1"))]

    return tok_qo + tok_g1 + tok_xnw


def build_res0():
    p = Prog()
    io = {"x": p.din("x", [T, DM], F32),
          "s0fm": p.din("s0fm", [DCH, 128, T], BF16),
          "g0fm": p.din("g0fm", [DCH, 128, T], BF16),
          "wqkv": p.din("wqkv", [34, 128, DCH, 128], F32),
          "routT": p.din("routT", [128, DCH, 4], F32),
          "acat": p.din("acat", [128, DCH, 64], F32),
          "ball": p.din("ball", [DCH, 64, 128], F32),
          "identb": p.din("identb", [128, 128], BF16),
          "x1": p.dout("x1", [T, DM], F32),
          "xn1fm": p.dout("xn1fm", [DCH, 128, T], BF16),
          "xn1lofm": p.dout("xn1lofm", [DCH, 128, T], BF16),
          "g1fm": p.dout("g1fm", [DCH, 128, T], BF16),
          "qkvfm": p.dout("qkvfm", [34, 128, T], BF16)}
    p.build()
    toks = emit_res0(p, p, io)
    p.finish(toks)
    return p


def host_res0_inputs(inputs, x_full, s0_list, g0_list):
    wqkv = np.ascontiguousarray(np.asarray(inputs["L1_attn_Wqkv"]).reshape(DCH, 128, 34, 128)
                                .transpose(2, 1, 0, 3)).astype(np.float32)
    routT = np.ascontiguousarray(np.asarray(inputs["L1_ffn_router_w"]).T.reshape(DCH, 128, 4)
                                 .transpose(1, 0, 2)).astype(np.float32)
    A = np.asarray(inputs["L1_ffn_A"])
    acat = np.ascontiguousarray(A.transpose(2, 0, 1).reshape(4096, 64).reshape(DCH, 128, 64)
                                .transpose(1, 0, 2)).astype(np.float32)
    B = np.asarray(inputs["L1_ffn_B"])
    ball = np.ascontiguousarray(B.transpose(0, 2, 1).reshape(64, 4096).reshape(64, DCH, 128)
                                .transpose(1, 0, 2)).astype(np.float32)
    identB = np.eye(128, dtype=np.float32).astype(BF)
    maps = []
    for c in range(len(s0_list)):
        maps.append({"x": np.ascontiguousarray(x_full[c * T:(c + 1) * T]).astype(np.float32),
                     "s0fm": s0_list[c], "g0fm": g0_list[c], "wqkv": wqkv, "routT": routT,
                     "acat": acat, "ball": ball, "identb": identB})
    return maps



ATT_W = [64, 256, 512, 512]
ATT_WIN = [536, 2164, 999999, 999999]

def emit_attn(p, al, io):
    S = 4096
    acol_i = io["acol"]; mconst_i = io["mconst"]; mb64_i = io["mb64"]
    mb256_i = io["mb256"]; mb512_i = io["mb512"]; identb = io["identb"]
    attnfm_o = io["attnfm"]
    SCALE = float(128 ** -0.5)
    identS = al.sb("identS", [128, 128], BF16)
    acol = al.sb("acol_sb", [128, 4 * 32], F32)
    mconst = al.sb("mconst_sb", [1, 4 * 64], F32)
    mb64 = al.sb("mb64_sb", [128, 2 * 64], F32)
    mb256 = al.sb("mb256_sb", [128, 2 * 256], F32)
    mb512 = al.sb("mb512_sb", [128, 4 * 512], F32)
    onesb = al.sb("onesb", [128, 1], BF16)
    onesf = al.sb("onesf", [128, 1], F32)

    mscr = p.dint("mscr", [1, 64], F32)
    zscr = p.dint("zscr", [1, 512], F32)

    p.dma("sy", identS[:], identb, sem_name="kq")
    p.dma("sy", acol[:].rearrange("p (s t) -> p s t", s=4), acol_i, sem_name="kq")
    p.dma("sy", mconst[:].rearrange("a (s q) -> a s q", s=4), mconst_i.unsqueeze(0), sem_name="kq")
    p.dma("sy", mb64[:].rearrange("p (a w) -> p a w", a=2), mb64_i.rearrange("a p w -> p a w"), sem_name="kq")
    p.dma("sy", mb256[:].rearrange("p (a w) -> p a w", a=2), mb256_i.rearrange("a p w -> p a w"), sem_name="kq")
    p.dma("sy", mb512[:].rearrange("p (a w) -> p a w", a=4), mb512_i.rearrange("a p w -> p a w"), sem_name="kq")
    tok_k = ("kq", p.c("kq"))
    t_ones = p.run("ve", lambda e: e.memset(onesb[:], 1.0))
    t_onesf = p.run("ve", lambda e: e.memset(onesf[:], 1.0))

    with p.scoped() as s:
        qb_sb = s.sb("qb_sb", [128, 4 * S], BF16)
        kb_sb = s.sb("kb_sb", [128, S], BF16)
        vb_sb = s.sb("vb_sb", [128, S], BF16)
        vtok = s.sb("vtok", [128, 32 * 128], BF16)
        attnb = s.sb("attnb", [128, 4 * S], BF16)
        esb = [s.sb(f"esb{j}", [128, 512], BF16) for j in range(2)]
        ssb = s.sb("ssb", [128, 512], F32)
        qsq = s.sb("qsq", [128, S], F32)
        qn2 = s.sb("qn2", [1, S], F32)
        kn2 = s.sb("kn2", [1, 1], F32)
        mrow = s.sb("mrow", [1, 64], F32)
        mrep = s.sb("mrep", [128, 64], F32)
        zrow = s.sb("zrow", [1, 512], F32)
        zrep = [s.sb(f"zrep{j}", [128, 512], F32) for j in range(2)]
        bcol = [s.sb(f"bcol{j}", [128, 1], F32) for j in range(2)]

        pssc = [s.ps(f"pssc{j}", [128, 512]) for j in range(2)]
        psz = [s.ps(f"psz{j}", [1, 512]) for j in range(2)]
        pso = [s.ps(f"pso{j}", [128, 512]) for j in range(2)]
        psq2 = s.ps("psq2", [1, 512])
        pst = s.ps("pstv", [128, 512], BF16)

        prev = {}
        for b in range(4):
            bs = slice(b * S, (b + 1) * S)
            wb = [prev[("bdone",)]] if ("bdone",) in prev else []
            tok_q = io["load_q"](p, qb_sb, b, wb)
            io["load_kv"](p, kb_sb, vb_sb, b, wb)
            tok_kv = (p._phys("kld"), p.c("kld"))
            for g4 in range(8):
                wv = [tok_kv, tok_k] + ([prev[("vtp",)]] if ("vtp",) in prev else [])
                fns = [(lambda e, kt=g4 * 4 + j: e.transpose(
                    pst[:, (kt % 4) * 128:(kt % 4 + 1) * 128], vb_sb[:, kt * 128:(kt + 1) * 128],
                    identS[:])) for j in range(4)]
                t_tp = p.runs("te", fns, waits=wv, self_sync=False)
                prev[("vtp",)] = p.run("ve", lambda e, g4=g4: e.tensor_copy(
                    out=vtok[:, g4 * 512:(g4 + 1) * 512], in_=pst[:, :]), waits=[t_tp])
            t_vtok = prev[("vtp",)]
            t_ks = p.run("se", lambda e: e.activation(out=qsq[:], in_=kb_sb[:], func=AF.Square),
                         waits=[tok_kv])
            wv = [t_ks, t_onesf] + ([prev[("kn2mm",)]] if ("kn2mm",) in prev else [])
            fns = [(lambda e, tb=tb: e.matmul(psq2[:, 0:512], onesf[:].bitcast(F32) if False else onesf[:],
                                              qsq[:, tb * 512:(tb + 1) * 512],
                                              start=(tb == 0), stop=False)) for tb in range(8)]
            fns = [(lambda e, tb=tb: e.matmul(psq2[:, 0:512], onesf[:],
                                              qsq[:, tb * 512:(tb + 1) * 512],
                                              start=True, stop=True)) for tb in range(8)]
            t_kn = None
            for tb in range(8):
                t_mm = p.run("te", fns[tb], waits=wv if tb == 0 else [prev[("knev",)]], self_sync=False)
                prev[("knev",)] = p.run("vn", lambda e, tb=tb: e.tensor_copy(out=zrow[:], in_=psq2[:, 0:512]),
                                        waits=[t_mm])
                if tb == 0:
                    t_kn = p.run("vn", lambda e: e.reduce_max(kn2[:], zrow[:], axis=AX.X))
                else:
                    t_kn = p.run("vn", lambda e: e.reduce_max(zrow[:, 0:1], zrow[:], axis=AX.X))
                    t_kn = p.run("vn", lambda e: e.tensor_max(kn2[:], kn2[:], zrow[:, 0:1]))
            prev[("kn2mm",)] = t_kn
            t_kn2 = p.run("se", lambda e: e.activation(out=kn2[:], in_=kn2[:], func=AF.Sqrt), waits=[t_kn])

            for slot in range(4):
                W = ATT_W[slot]
                NQB = S // W
                qoff = slot * S
                t_qs = p.run("se", lambda e, slot=slot: e.activation(
                    out=qsq[:], in_=qb_sb[:, slot * S:(slot + 1) * S], func=AF.Square),
                    waits=[tok_q, prev[("qn2used",)]] if ("qn2used",) in prev else [tok_q])
                for tb in range(8):
                    wv = [t_qs, t_onesf] + ([prev[("qnev",)]] if ("qnev",) in prev else [])
                    t_mm = p.run("te", lambda e, tb=tb: e.matmul(
                        psq2[:, 0:512], onesf[:], qsq[:, tb * 512:(tb + 1) * 512],
                        start=True, stop=True), waits=wv, self_sync=False)
                    prev[("qnev",)] = p.run("vn", lambda e, tb=tb: e.tensor_copy(
                        out=qn2[:, tb * 512:(tb + 1) * 512], in_=psq2[:, 0:512]), waits=[t_mm])
                t_qn = p.run("vn", lambda e, W=W, NQB=NQB: e.reduce_max(
                    mrow[:, 0:NQB], qn2[:].rearrange("a (q w) -> a q w", w=W), axis=AX.X),
                    waits=[prev[("qnev",)]])
                t_m1 = p.run("se", lambda e, NQB=NQB: e.activation(out=mrow[:, 0:NQB], in_=mrow[:, 0:NQB],
                                                                   func=AF.Sqrt), waits=[t_qn])
                t_m2 = p.run("vn", lambda e, NQB=NQB: e.tensor_scalar_mul(mrow[:, 0:NQB], mrow[:, 0:NQB],
                                                                          kn2[:, :]), waits=[t_m1, t_kn2])
                t_m3 = p.run("vn", lambda e, NQB=NQB, slot=slot: e.scalar_tensor_tensor(
                    mrow[:, 0:NQB], mrow[:, 0:NQB], SCALE, mconst[:, slot * 64:slot * 64 + NQB],
                    op0=ALU.mult, op1=ALU.add), waits=[tok_k])
                wv = [t_m3] + ([prev[("mrd",)]] if ("mrd",) in prev else [])
                tok_mw = p.dma("sy", mscr[:, 0:NQB], mrow[:, 0:NQB], waits=wv, sem_name="mwq")
                wv = [tok_mw] + ([prev[("mused",)]] if ("mused",) in prev else [])
                tok_mr = p.dma("sy", mrep[:, 0:NQB], mscr[:, 0:NQB].to_broadcast([128, NQB]),
                               waits=wv, sem_name="mrq")
                prev[("mrd",)] = tok_mr
                prev[("qn2used",)] = t_qn

                alpha_min = 0.84 ** (slot * 8 + 8)
                for qb in range(NQB):
                    q_lo, q_end = qb * W, qb * W + W - 1
                    kt_hi = q_end // 128
                    kt_lo = 0
                    while (q_lo - (kt_lo * 128 + 127)) * alpha_min > 133:
                        kt_lo += 1
                    kts = list(range(kt_lo, kt_hi + 1))
                    pq = qb % 2
                    first = True
                    for kt in kts:
                        wv = [tok_mr, tok_k] + ([prev[("bused", pq)]] if ("bused", pq) in prev else [])
                        t_b = p.run("vb", lambda e, kt=kt, qb=qb, slot=slot, pq=pq: e.tensor_scalar(
                            bcol[pq][:], acol[:, slot * 32 + kt:slot * 32 + kt + 1],
                            mrep[:, qb:qb + 1], None, op0=ALU.subtract), waits=wv)
                        wv = [tok_q, tok_kv]
                        if ("spq", pq) in prev:
                            wv.append(prev[("spq", pq)])
                        t_sc = p.run("te", lambda e, kt=kt, qb=qb, slot=slot, pq=pq, W=W: e.matmul(
                            pssc[pq][:, 0:W], kb_sb[:, kt * 128:(kt + 1) * 128],
                            qb_sb[:, slot * S + qb * W:slot * S + (qb + 1) * W],
                            start=True, stop=True), waits=wv, self_sync=False)
                        partial = (kt * 128 + 127 > q_end - W + 0) and (kt * 128 + 127 > q_end) or (kt * 128 + 127 > q_lo + 0 and kt * 128 + 127 > q_end)
                        partial = kt * 128 + 127 > q_end
                        partial = (kt * 128 + 127) > q_lo and (kt == kt_hi or (kt * 128 + 127) > q_lo + W - 1) and ((kt * 128 + 127) > q_end or kt == kt_hi)
                        partial = (kt * 128 + 127) > q_lo
                        if partial:
                            off = qb * W - kt * 128
                            if W == 64:
                                mv_ap = mb64[:, (0 if off == 0 else 1) * 64:(0 if off == 0 else 1) * 64 + 64]
                                assert off in (0, 64), (W, off)
                                mv_ap = mb64[:, (off // 64) * 64:(off // 64) * 64 + 64]
                            elif W == 256:
                                assert off in (0, -128), (W, off)
                                mv_ap = mb256[:, (0 if off == 0 else 1) * 256:(0 if off == 0 else 1) * 256 + 256]
                            else:
                                assert off in (0, -128, -256, -384), (W, off)
                                mv_ap = mb512[:, (-off // 128) * 512:(-off // 128) * 512 + 512]
                            t_msk = p.run("vb", lambda e, pq=pq, W=W, mv_ap=mv_ap: e.tensor_add(
                                ssb[:, 0:W], pssc[pq][:, 0:W], mv_ap), waits=[t_sc])
                            exp_in = ssb[:, 0:W]
                            exp_wait = [t_msk, t_b]
                            prev[("spq", pq)] = t_msk
                        else:
                            exp_in = pssc[pq][:, 0:W]
                            exp_wait = [t_sc, t_b]
                        wv = exp_wait + ([prev[("eused", pq)]] if ("eused", pq) in prev else [])
                        t_exp = p.run("se", lambda e, pq=pq, W=W, exp_in=exp_in: e.activation(
                            out=esb[pq][:, 0:W], in_=exp_in, func=AF.Exp, scale=SCALE,
                            bias=bcol[pq][:]), waits=wv)
                        if not partial:
                            prev[("spq", pq)] = t_exp
                        prev[("bused", pq)] = t_exp
                        wz = [t_exp, t_ones]
                        if first and ("zev", pq) in prev:
                            wz.append(prev[("zev", pq)])
                        lastkt = (kt == kts[-1])
                        t_z = p.run("te", lambda e, pq=pq, W=W, first=first, lastkt=lastkt: e.matmul(
                            psz[pq][:, 0:W], onesb[:], esb[pq][:, 0:W],
                            start=first, stop=lastkt), waits=wz, self_sync=False)
                        wo = [t_exp, t_vtok]
                        if first and ("oev", pq) in prev:
                            wo.append(prev[("oev", pq)])
                        t_o = p.run("te", lambda e, pq=pq, W=W, kt=kt, first=first, lastkt=lastkt: e.matmul(
                            pso[pq][:, 0:W], vtok[:, kt * 128:(kt + 1) * 128], esb[pq][:, 0:W],
                            start=first, stop=lastkt), waits=wo, self_sync=False)
                        prev[("eused", pq)] = t_o
                        first = False
                    t_zev = p.run("vz", lambda e, pq=pq, W=W: e.tensor_copy(out=zrow[:, 0:W],
                                                                            in_=psz[pq][:, 0:W]),
                                  waits=[t_z] + ([prev[("zrow",)]] if ("zrow",) in prev else []))
                    prev[("zev", pq)] = t_zev
                    t_zc = p.run("vz", lambda e, W=W: e.tensor_scalar_max(zrow[:, 0:W], zrow[:, 0:W], 1e-35))
                    t_zr = p.run("vz", lambda e, W=W: e.reciprocal(zrow[:, 0:W], zrow[:, 0:W]))
                    prev[("zrow",)] = t_zr
                    wv = [t_zr] + ([prev[("zrd", pq)]] if ("zrd", pq) in prev else [])
                    tok_zw = p.dma("sy", zscr[:, 0:W], zrow[:, 0:W], waits=wv, sem_name="zwq")
                    wv = [tok_zw] + ([prev[("zused", pq)]] if ("zused", pq) in prev else [])
                    tok_zr = p.dma("sy", zrep[pq][:, 0:W], zscr[:, 0:W].to_broadcast([128, W]),
                                   waits=wv, sem_name=f"zrq{pq}")
                    prev[("zrd", pq)] = tok_zr
                    woev = [t_o, tok_zr]
                    if qb == 0 and ("adma", slot) in prev:
                        woev.append(prev[("adma", slot)])
                    t_oev = p.run("vz", lambda e, pq=pq, W=W, slot=slot, qb=qb: e.tensor_mul(
                        attnb[:, slot * S + qb * W:slot * S + (qb + 1) * W],
                        pso[pq][:, 0:W], zrep[pq][:, 0:W]), waits=woev)
                    prev[("oev", pq)] = t_oev
                    prev[("zused", pq)] = t_oev
                wv = [prev[("oev", 0)]]
                if NQB > 1:
                    wv.append(prev[("oev", 1)])
                prev[("adma", slot)] = p.dma("gp", attnfm_o[slot, :, bs],
                                             attnb[:, slot * S:(slot + 1) * S], waits=wv,
                                             sem_name=f"adq{slot}")
            prev[("bdone",)] = prev[("adma", 3)]
    return [(p._phys(f"adq{s2}"), p.c(f"adq{s2}")) for s2 in range(4)]


def build_attn():
    p = Prog()
    S = 4096
    qsl_i = p.din("qsl", [4, 128, 4 * S], BF16)
    ksl_i = p.din("ksl", [128, 4 * S], BF16)
    vsl_i = p.din("vsl", [128, 4 * S], BF16)

    def load_q(p2, qb_sb, b, wb):
        bs = slice(b * S, (b + 1) * S)
        return p2.dma("sy", qb_sb[:].rearrange("p (s t) -> p s t", s=4),
                      qsl_i[:, :, bs].rearrange("s p t -> p s t"), waits=wb, sem_name="qld")

    def load_kv(p2, kb_sb, vb_sb, b, wb):
        bs = slice(b * S, (b + 1) * S)
        p2.dma("gp", kb_sb[:], ksl_i[:, bs], waits=wb, sem_name="kld")
        p2.dma("gp", vb_sb[:], vsl_i[:, bs], sem_name="kld")

    io = {"acol": p.din("acol", [128, 4, 32], F32),
          "mconst": p.din("mconst", [4, 64], F32),
          "mb64": p.din("mb64", [2, 128, 64], F32),
          "mb256": p.din("mb256", [2, 128, 256], F32),
          "mb512": p.din("mb512", [4, 128, 512], F32),
          "identb": p.din("identb", [128, 128], BF16),
          "attnfm": p.dout("attnfm", [4, 128, 4 * S], BF16),
          "load_q": load_q, "load_kv": load_kv}
    p.build()
    toks = emit_attn(p, p, io)
    p.finish(toks)
    return p


def host_attn_inputs(qkvfm_all):
    import math
    S = 4096
    def _alibi_slopes(n):
        cp2 = 2 ** int(math.floor(math.log2(n)))
        base = 2.0 ** (-2.0 ** (-(math.log2(cp2) - 3)))
        slopes = base ** np.arange(1, cp2 + 1, dtype=np.float64)
        if cp2 != n:
            extra_base = 2.0 ** (-2.0 ** (-(math.log2(2 * cp2) - 3)))
            rem = min(cp2, n - cp2)
            extra = extra_base ** np.arange(1, 1 + 2 * rem, 2, dtype=np.float64)
            slopes = np.concatenate([slopes, extra])
        return slopes
    slopes = _alibi_slopes(32)
    ksl = np.ascontiguousarray(qkvfm_all[32])
    vsl = np.ascontiguousarray(qkvfm_all[33])
    identB = np.eye(128, dtype=np.float32).astype(BF)
    kabs = np.arange(S).reshape(32, 128)
    maps = []
    for c in range(NCORE):
        heads = [c, c + 8, c + 16, c + 24]
        qsl = np.ascontiguousarray(np.stack([qkvfm_all[h] for h in heads]))
        acol = np.ascontiguousarray(np.stack([slopes[h] * kabs for h in heads]).transpose(2, 0, 1)).astype(np.float32)
        mconst = np.zeros((4, 64), dtype=np.float32)
        for s2, h in enumerate(heads):
            W = ATT_W[s2]
            nqb = S // W
            mconst[s2, :nqb] = slopes[h] * (np.arange(nqb) * W + W - 1) - 40.0
        mb64 = np.stack([_maskbig(64, off) for off in (0, 64)]).astype(np.float32)
        mb256 = np.stack([_maskbig(256, off) for off in (0, -128)]).astype(np.float32)
        mb512 = np.stack([_maskbig(512, off) for off in (0, -128, -256, -384)]).astype(np.float32)
        maps.append({"qsl": qsl, "ksl": ksl, "vsl": vsl, "acol": acol, "mconst": mconst,
                     "mb64": mb64, "mb256": mb256, "mb512": mb512, "identb": identB})
    return maps


def _maskbig(W, off):
    i = np.arange(128)[:, None]
    j = np.arange(W)[None, :]
    return np.where(i <= off + j, 0.0, NEG_BIG).astype(np.float32)



def emit_wo(p, al, io):
    wo = io["wo"]; g1fm_i = io["g1fm"]; x1_i = io["x1"]; identb = io["identb"]
    out_o = io["out"]
    identS = al.sb("identS", [128, 128], BF16)
    sgscr = p.dint("sgscr", [DCH, 128, T], BF16)

    p.dma("sy", identS[:], identb, sem_name="kq")
    tok_k = ("kq", p.c("kq"))

    with p.scoped() as s:
        attnsb = s.sb("attnsb", [128, DCH * T], BF16)
        g1t = [s.sb(f"g1t{j}", [128, T], BF16) for j in range(2)]
        sgo = [s.sb(f"sgo{j}", [128, T], BF16) for j in range(2)]
        psA = [s.ps(f"woA{i}", [128, 512]) for i in range(4)]
        psB = [s.ps(f"woB{i}", [128, 512]) for i in range(4)]
        atok = io["load_attn"](p, attnsb)
        gl_tok = {}

        def evac_wo(o, pset, tok_mm):
            b = o % 2
            wv = [tok_mm, gl_tok[o]] + ([otok[o - 2]] if o >= 2 else [])
            t_ev = p.runs("ve", [(lambda e, tb=tb, b=b: e.tensor_mul(
                sgo[b][:, tb * 512:(tb + 1) * 512], pset[tb][:, :],
                g1t[b][:, tb * 512:(tb + 1) * 512])) for tb in range(4)], waits=wv)
            otok[o] = p.dma("sy", sgscr[o], sgo[b][:], waits=[t_ev], sem_name=f"soq{b}")
            return t_ev

        otok = {}
        wtok, mmtok, evtok = {}, {}, {}
        wsl = [s.sb(f"wosl{j}", [128, DCH * 128], BF16) for j in range(2)]
        for o in range(DCH):
            b = o % 2
            wd = [mmtok[o - 2]] if o >= 2 else []
            wtok[o] = p.dma("gp", wsl[b][:], wo[o].rearrange("p d q -> p (d q)"),
                            waits=wd, sem_name=f"wwq{b}")
            gl_tok[o] = p.dma("gp", g1t[b][:], g1fm_i[o],
                              waits=[otok[o - 2]] if o >= 2 else [], sem_name=f"glq{b}")
            pset = (psA, psB)[b]
            w = [wtok[o], atok]
            if o >= 2:
                w.append(evtok[o - 2])
            fns = []
            for dc in range(DCH):
                for tb in range(4):
                    fns.append(lambda e, dc=dc, tb=tb, o=o, b=b: e.matmul(
                        pset[tb][:, :], wsl[b][:, dc * 128:(dc + 1) * 128],
                        attnsb[:, dc * T + tb * 512:dc * T + (tb + 1) * 512],
                        start=(dc == 0), stop=(dc == DCH - 1)))
            mmtok[o] = p.run_group("te", fns, waits=w)
            evtok[o] = evac_wo(o, pset, mmtok[o])
        tok_sg = [("soq0", p.c("soq0")), ("soq1", p.c("soq1"))]

    p.phase_barrier()
    with p.scoped() as s:
        sgl = [s.sb(f"sgl{j}", [128, DM], BF16) for j in range(2)]
        xt = s.sb("xt", [128, DM], F32)
        sgt = s.sb("sgtk", [128, DM], F32)
        pstA = s.ps("pstA", [128, 512], BF16)
        pstB = s.ps("pstB", [128, 512], BF16)
        prev = {}
        owb = p.cg("owq")
        for i in range(TT):
            b = i % 2
            wl = [prev[("tp2", b)]] if ("tp2", b) in prev else []
            tok_sgl = p.dma("gp", sgl[b][:].rearrange("p (d t) -> p d t", d=DCH),
                            sgscr[:, :, i * 128:(i + 1) * 128].rearrange("d p t -> p d t"),
                            waits=wl, sem_name=f"sgq{b}")
            wx = ([prev[("add",)], ("owq", owb + 16 * i)] if i >= 1 else [])
            tok_x = p.dma("sy", xt[:], x1_i[i * 128:(i + 1) * 128, :], waits=wx, sem_name="xq")
            for dcg in range(8):
                q = dcg % 2
                ps = (pstA, pstB)[q]
                wv = [tok_sgl, tok_k]
                if ("ev", q) in prev:
                    wv.append(prev[("ev", q)])
                fns = [(lambda e, dc=dcg * 4 + j, ps=ps, b=b: e.transpose(
                    ps[:, (dc % 4) * 128:(dc % 4 + 1) * 128], sgl[b][:, dc * 128:(dc + 1) * 128],
                    identS[:])) for j in range(4)]
                t_tp = p.runs("te", fns, waits=wv, self_sync=False)
                prev[("tp2", b)] = t_tp
                prev[("ev", q)] = p.run("ve", lambda e, dcg=dcg, ps=ps: e.tensor_copy(
                    out=sgt[:, dcg * 512:(dcg + 1) * 512], in_=ps[:, :]), waits=[t_tp])
            t_add = p.run("ve", lambda e: e.tensor_add(xt[:], xt[:], sgt[:]),
                          waits=[tok_x, prev[("ev", 0)], prev[("ev", 1)]])
            prev[("add",)] = t_add
            p.dma("sy", out_o[i * 128:(i + 1) * 128, :], xt[:], waits=[t_add], sem_name="owq")

    return [(p._phys("owq"), p.c("owq"))]


def build_wo():
    p = Prog()
    attn_i = p.din("attn", [DCH, 128, T], BF16)

    def load_attn(p2, attnsb):
        return load_xnsb(p2, attnsb, attn_i, DCH)

    io = {"wo": p.din("wo", [DCH, 128, DCH, 128], F32),
          "g1fm": p.din("g1fm", [DCH, 128, T], BF16),
          "x1": p.din("x1", [T, DM], F32),
          "identb": p.din("identb", [128, 128], BF16),
          "out": p.dout("out", [T, DM], F32),
          "load_attn": load_attn}
    p.build()
    toks = emit_wo(p, p, io)
    p.finish(toks)
    return p


def host_wo_inputs(inputs, attn_list, g1_list, x1_list):
    woT = np.ascontiguousarray(np.asarray(inputs["L1_attn_Wo"]).reshape(DCH, 128, DCH, 128)
                               .transpose(2, 1, 0, 3)).astype(np.float32)
    identB = np.eye(128, dtype=np.float32).astype(BF)
    maps = []
    for c in range(len(attn_list)):
        maps.append({"attn": attn_list[c], "wo": woT, "g1fm": g1_list[c], "x1": x1_list[c],
                     "identb": identB})
    return maps



RG8 = [list(range(8))]
RGP = [[0, 1], [2, 3], [4, 5], [6, 7]]


def build_mega(upto=None, no_cc=False):
    p = Prog()
    S = 4096
    NCT = CONVD // 128
    NSLOT = 1024
    x = p.din("x", [T, DM], F32)
    identb = p.din("identb", [128, 128], BF16)
    identf = p.din("identf", [128, 128], F32)
    rolemask = p.din("rolemask", [128, 1], F32)
    wproj = p.din("wproj", [PROJ // 128, 128, DCH, 128], F32)
    dtb = p.din("dtb", [128, 1], F32)
    cw = p.din("cw", [NCT, 128, 4], F32)
    cb = p.din("cb", [NCT, 128], F32)
    alog = p.din("alog", [128, 1], F32)
    normw = p.din("normw", [128, DIN // 128], F32)
    dvec = p.din("dvec", [1, 128], F32)
    masklow_in = p.din("masklow_in", [128, 128], F32)
    opw = p.din("opw", [DCH, 128, DIN // 128, 128], F32)
    wqT = p.din("wqT", [128, DCH, 64], F32)
    wrT = p.din("wrT", [64, 48], F32)
    keysT = p.din("keysT", [64, NSLOT], F32)
    logrel = p.din("logrel", [1, NSLOT], F32)
    wu1 = p.din("wu1", [8, 128, DCH, 128], F32)
    bu1 = p.din("bu1", [128, 8], F32)
    wu2T = p.din("wu2T", [128, 8], F32)
    bu2 = p.din("bu2", [1, 1], F32)
    gw1 = p.din("gw1", [1, 1], F32)
    gb = p.din("gb", [1, 1], F32)
    vals = p.din("vals", [DCH, 128, 8, 128], F32)
    wvdT = p.din("wvdT", [2, 128, DCH, 128], F32)
    wvuT = p.din("wvuT", [DCH, 128, 2, 128], F32)
    wqkv = p.din("wqkv", [34, 128, DCH, 128], F32)
    routT = p.din("routT", [128, DCH, 4], F32)
    acat = p.din("acat", [128, DCH, 64], F32)
    ball = p.din("ball", [DCH, 64, 128], F32)
    acol = p.din("acol", [128, 4, 32], F32)
    mconst = p.din("mconst", [4, 64], F32)
    mb64 = p.din("mb64", [2, 128, 64], F32)
    mb256 = p.din("mb256", [2, 128, 256], F32)
    mb512 = p.din("mb512", [4, 128, 512], F32)
    wo_w = p.din("wo", [DCH, 128, DCH, 128], F32)
    out = p.dout("out", [T, DM], F32)
    xnfm0 = p.dint("i_xnfm0", [DCH, 128, T], BF16)
    lvflat = p.dint("i_lvflat", [1, T], F32)
    lvpart = p.dint("i_lvpart", [1, 1], F32)
    lvsum = p.dint("i_lvsum", [1, 1], F32)
    gsilu = p.dint("i_gsilu", [DIN // 128, 128, T], BF16)
    xbc = p.dint("i_xbc", [NCT, 128, T], BF16)
    dtfm = p.dint("i_dtfm", [128, T], F32)
    bdry = p.dint("i_bdry", [NCT, 128, 3], BF16)
    bdryg = p.dint("i_bdryg", [2 * NCT * 128, 3], BF16)
    xch = p.dint("i_xch", [NC, 128, DIN], BF16)
    xs = p.dint("i_xs", [NC, 128, DIN], BF16)
    bfm = p.dint("i_bfm", [G, 128, T], BF16)
    cfm = p.dint("i_cfm", [G, 128, T], BF16)
    cst = p.dint("i_cst", [NC, 128, 128], F32)
    ncst = p.dint("i_ncst", [NC, 128, 128], F32)
    ecst = p.dint("i_ecst", [NC, 128, 128], F32)
    pfac = p.dint("i_pfac", [128, NC], F32)
    dA_t = p.dint("i_dA", [128, NC], F32)
    rawst = p.dint("i_rawst", [NC, NH, NST, HD], F32)
    owncum = p.dint("i_owncum", [NC, NH, NST * HD], F32)
    shand = p.dint("i_shand", [NH, NST * HD], F32)
    shandg = p.dint("i_shandg", [2 * NH, NST * HD], F32)
    ynfm = p.dint("i_ynfm", [DIN // 128, 128, T], BF16)
    s0fm = p.dint("i_s0fm", [DCH, 128, T], BF16)
    g0fm = p.dint("i_g0fm", [DCH, 128, T], BF16)
    x1 = p.dint("i_x1", [T, DM], F32)
    xn1fm = p.dint("i_xn1fm", [DCH, 128, T], BF16)
    xn1lofm = p.dint("i_xn1lofm", [DCH, 128, T], BF16)
    g1fm = p.dint("i_g1fm", [DCH, 128, T], BF16)
    qkvfm = p.dint("i_qkvfm", [34, 128, T], BF16)
    qprep = p.dint("i_qprep", [8, 6, 128, T], BF16)
    qkvall2 = p.dint("i_qkvall2", [8 * 8 * 6 * 128, T], BF16)
    qloc = p.dint("i_qloc", [8, 6, 128, T], BF16)
    attnfm = p.dint("i_attnfm", [4, 128, 4 * S], BF16)
    aprep = p.dint("i_aprep", [8, 6 * 128 * T], BF16)
    attnall2 = p.dint("i_attnall2", [8 * 8, 6 * 128 * T], BF16)
    attnloc = p.dint("i_attnloc", [8, 4, 128, T], BF16)

    p.build()

    def collective(kind, op, groups, src, dst):
        if no_cc:
            return
        p.run("gp", lambda e: e.collective_compute(kind, op, groups, [src.opt()], [dst.opt()]))

    p.phase_ns("A")
    with p.scoped() as al:
        emit_ln0(p, al, {"x": x, "identb": identb, "identf": identf, "xnfm": xnfm0,
                         "lvflat": lvflat, "lvpart": lvpart})
    p.phase_barrier()
    collective("AllReduce", ALU.add, RG8, lvpart, lvsum)
    if upto == "ARlv":
        p.phase_barrier()
        p.finish([])
        return p

    if upto == "A":
        p.phase_barrier()
        p.finish([])
        return p

    p.phase_ns("B")
    p.phase_barrier()
    with p.scoped() as al:
        emit_inproj(p, al, {"xnfm": xnfm0, "wproj": wproj, "dtb": dtb, "gsilu": gsilu,
                            "xbc": xbc, "dtfm": dtfm, "bdry": bdry})
    if upto == "B":
        p.phase_barrier()
        p.finish([])
        return p

    p.phase_barrier()
    collective("AllGather", ALU.bypass, RGP, bdry.rearrange("n p w -> (n p) w"), bdryg)
    if upto == "AGbd":
        p.phase_barrier()
        p.finish([])
        return p

    p.phase_ns("C")
    p.phase_barrier()
    with p.scoped() as al:
        emit_convssm(p, al, {
            "xbc": xbc, "bdry_src": (bdry if upto == "Cx" else
                                     bdryg.rearrange("(a n p) w -> a n p w", a=2, n=NCT)[0]),
            "rolemask": rolemask, "cw": cw, "cb": cb, "dtfm": dtfm, "alog": alog,
            "identb": identb, "identf": identf, "xch": xch, "xs": xs, "bfm": bfm,
            "cfm": cfm, "cst": cst, "ncst": ncst, "ecst": ecst, "pfac": pfac,
            "dA": dA_t, "rawst": rawst, "owncum": owncum, "shand": shand})
    if upto in ("C", "Cx"):
        p.phase_barrier()
        p.finish([])
        return p

    p.phase_barrier()
    collective("AllGather", ALU.bypass, RGP, shand, shandg)
    if upto == "AGsh":
        p.phase_barrier()
        p.finish([])
        return p

    p.phase_ns("D")
    p.phase_barrier()
    with p.scoped() as al:
        emit_ssm2(p, al, {
            "xs": xs, "xch": xch, "bfm": bfm, "cfm": cfm, "cst": cst, "ncst": ncst,
            "ecst": ecst, "pfac": pfac, "owncum": owncum,
            "sinit": shandg[0:NH], "rolemask": rolemask, "gsilu": gsilu,
            "normw": normw, "dvec": dvec, "identb": identb, "identf": identf,
            "masklow_in": masklow_in, "opw": opw, "ynfm": ynfm, "s0fm": s0fm})
    if upto == "D":
        p.phase_barrier()
        p.finish([])
        return p

    p.phase_ns("E")
    p.phase_barrier()
    with p.scoped() as al:
        emit_ffn0(p, al, {
            "xnfm": xnfm0, "lvrow": lvflat, "lvparts": lvsum, "wqT": wqT, "wrT": wrT,
            "keysT": keysT, "logrel": logrel, "wu1": wu1, "bu1": bu1, "wu2T": wu2T,
            "bu2": bu2, "gw1": gw1, "gb": gb, "vals": vals, "wvdT": wvdT, "wvuT": wvuT,
            "identb": identb, "g0fm": g0fm}, npart=1)
    if upto == "E":
        p.phase_barrier()
        p.finish([])
        return p

    p.phase_ns("F")
    p.phase_barrier()
    with p.scoped() as al:
        emit_res0(p, al, {
            "x": x, "s0fm": s0fm, "g0fm": g0fm, "wqkv": wqkv, "routT": routT,
            "acat": acat, "ball": ball, "identb": identb, "x1": x1, "xn1fm": xn1fm,
            "xn1lofm": xn1lofm, "g1fm": g1fm, "qkvfm": qkvfm})
    if upto == "F":
        p.phase_barrier()
        p.finish([])
        return p

    p.phase_barrier()
    for d in range(8):
        for j in range(4):
            p.dma("sy", qprep[d, j], qkvfm[d + 8 * j], sem_name="rlq")
        p.dma("sy", qprep[d, 4], qkvfm[32], sem_name="rlq")
        p.dma("sy", qprep[d, 5], qkvfm[33], sem_name="rlq")
    p.phase_barrier()
    collective("AllGather", ALU.bypass, RG8, qprep.rearrange("d j p t -> (d j p) t"), qkvall2)
    if upto == "AGq":
        p.phase_barrier()
        p.finish([])
        return p

    p.phase_ns("G")
    p.phase_barrier()

    BLKE = 6 * 128 * T
    rankoff = {}

    def pre_gather(dst_ap, src_tensor, nelems, sem_name, waits):
        sem = p.sem(sem_name)
        pk = p._phys(sem_name)

        def body(e):
            p._waits(e, waits)
            if "sv" not in rankoff:
                r = e.cc_rank(RG8)
                rankoff["sv"] = e.compute_val(r * BLKE)
            sv = rankoff["sv"]
            src = AP(tensor=src_tensor, offset=sv, ap=[[8 * BLKE, 8], [1, nelems]])
            e.dma_start(out=dst_ap, in_=src).then_inc(sem, 16)

        p._section("sy", body)
        p.cnt[pk] += 16
        return (pk, p.cnt[pk])

    tok_pg = pre_gather(qloc.rearrange("r j p t -> r (j p t)"), qkvall2.tensor, BLKE, "pgq", [])
    if upto == "Gpre":
        p.phase_barrier()
        p.finish([])
        return p


    qloc_v = qloc

    def load_q(p2, qb_sb, b, wb):
        tok = None
        for s2 in range(4):
            for ri, r in enumerate((2 * b, 2 * b + 1)):
                tok = p2.dma("sy", qb_sb[:, s2 * S + ri * T:s2 * S + (ri + 1) * T],
                             qloc_v[r, s2], waits=(wb + [tok_pg]) if (s2 == 0 and ri == 0) else [],
                             sem_name="qld")
        return tok

    def load_kv(p2, kb_sb, vb_sb, b, wb):
        for ri, r in enumerate((2 * b, 2 * b + 1)):
            p2.dma("gp", kb_sb[:, ri * T:(ri + 1) * T], qloc_v[r, 4],
                   waits=(wb + [tok_pg]) if ri == 0 else [], sem_name="kld")
            p2.dma("gp", vb_sb[:, ri * T:(ri + 1) * T], qloc_v[r, 5], sem_name="kld")

    with p.scoped() as al:
        emit_attn(p, al, {
            "acol": acol, "mconst": mconst, "mb64": mb64, "mb256": mb256, "mb512": mb512,
            "identb": identb, "attnfm": attnfm, "load_q": load_q, "load_kv": load_kv})
    if upto == "G":
        p.phase_barrier()
        p.finish([])
        return p

    p.phase_barrier()
    for d in range(8):
        for s2 in range(4):
            p.dma("sy", aprep.rearrange("d e -> d e")[d, s2 * 128 * T:(s2 + 1) * 128 * T]
                  .rearrange("(p t) -> p t", p=128),
                  attnfm[s2, :, d * T:(d + 1) * T], sem_name="rla")
    p.phase_barrier()
    collective("AllGather", ALU.bypass, RG8, aprep, attnall2)
    if upto == "AGa":
        p.phase_barrier()
        p.finish([])
        return p

    p.phase_ns("H")
    p.phase_barrier()
    tok_pa = pre_gather(attnloc.rearrange("r j p t -> r (j p t)"), attnall2.tensor,
                        4 * 128 * T, "pga", [])

    def load_attn(p2, attnsb):
        tok = None
        for h in range(DCH):
            tok = p2.dma("sy", attnsb[:, h * T:(h + 1) * T], attnloc[h % 8, h // 8],
                         waits=[tok_pa] if h == 0 else [], sem_name="xnq")
        return tok

    with p.scoped() as al:
        toks = emit_wo(p, al, {
            "wo": wo_w, "g1fm": g1fm, "x1": x1, "identb": identb, "out": out,
            "load_attn": load_attn})
    p.finish(toks)
    return p


def host_mega_inputs(inputs):
    x_full = np.ascontiguousarray(np.asarray(inputs["L0_attn_hidden_states"])
                                  .astype(np.float32, copy=False).reshape(16384, DM))
    dz = np.zeros((1, 1), dtype=np.float32)
    m_ln = host_ln0_inputs(x_full)
    m_ip = host_inproj_inputs(inputs, [dz] * NCORE)[0]
    m_cv = host_convssm_inputs(inputs, [dz], [dz])[0]
    m_s2 = host_ssm2_inputs(inputs, [{k: dz for k in ("xs", "xch", "bfm", "cfm", "cst",
                                                      "ncst", "ecst", "pfac", "owncum",
                                                      "gsilu")}], [dz])[0]
    m_f0 = host_ffn0_inputs(inputs, [dz], [dz], dz)[0]
    m_r0 = host_res0_inputs(inputs, np.zeros((T, 1), dtype=np.float32), [dz], [dz])[0]
    m_at = host_attn_inputs(np.zeros((34, 128, 16), dtype=BF))[0]
    m_wo = host_wo_inputs(inputs, [dz], [dz], [dz])[0]
    maps = []
    for c in range(NCORE):
        m = {"x": m_ln[c]["x"], "identb": m_ln[c]["identb"], "identf": m_ln[c]["identf"],
             "rolemask": np.full((128, 1), 1.0 if c % 2 else 0.0, dtype=np.float32),
             "wproj": m_ip["wproj"], "dtb": m_ip["dtb"],
             "cw": m_cv["cw"], "cb": m_cv["cb"], "alog": m_cv["alog"],
             "normw": m_s2["normw"], "dvec": m_s2["dvec"], "masklow_in": m_s2["masklow_in"],
             "opw": m_s2["opw"],
             "wqT": m_f0["wqT"], "wrT": m_f0["wrT"], "keysT": m_f0["keysT"],
             "logrel": m_f0["logrel"], "wu1": m_f0["wu1"], "bu1": m_f0["bu1"],
             "wu2T": m_f0["wu2T"], "bu2": m_f0["bu2"], "gw1": m_f0["gw1"], "gb": m_f0["gb"],
             "vals": m_f0["vals"], "wvdT": m_f0["wvdT"], "wvuT": m_f0["wvuT"],
             "wqkv": m_r0["wqkv"], "routT": m_r0["routT"], "acat": m_r0["acat"],
             "ball": m_r0["ball"],
             "mb64": m_at["mb64"], "mb256": m_at["mb256"],
             "mb512": m_at["mb512"], "wo": m_wo["wo"]}
        maps.append(m)
    m_at_all = host_attn_inputs(np.zeros((34, 128, 16), dtype=BF))
    for c in range(NCORE):
        maps[c]["acol"] = m_at_all[c]["acol"]
        maps[c]["mconst"] = m_at_all[c]["mconst"]
    return maps


def kernel_mega(**inputs):
    inputs = {k: np.asarray(v) for k, v in inputs.items()}
    maps = host_mega_inputs(inputs)
    p = build_mega()
    res = run_bass_kernel_spmd(p.nc, maps, core_ids=list(range(NCORE)))
    p.close()
    out = np.concatenate([np.asarray(res.results[c]["out"]) for c in range(NCORE)], axis=0)
    return np.ascontiguousarray(out).reshape(4, 4096, DM).astype(np.float32, copy=False)



def host_ln0_inputs(x_full):
    ident = np.eye(128, dtype=np.float32).astype(BF)
    identf = np.eye(128, dtype=np.float32)
    maps = []
    for c in range(NCORE):
        maps.append({"x": np.ascontiguousarray(x_full[c * T:(c + 1) * T]),
                     "identb": ident, "identf": identf})
    return maps


def _run8(p, maps):
    res = run_bass_kernel_spmd(p.nc, maps, core_ids=list(range(len(maps))))
    p.close()
    return res.results


def kernel_multi(**inputs):
    inputs = {k: np.asarray(v) for k, v in inputs.items()}
    x_full = np.ascontiguousarray(inputs["L0_attn_hidden_states"].astype(np.float32, copy=False)
                                  .reshape(16384, DM))
    r1 = _run8(build_ln0(), host_ln0_inputs(x_full))
    xnfm_list = [r["xnfm"] for r in r1]
    lvrow_list = [np.ascontiguousarray(r["lvflat"]) for r in r1]
    lvparts = np.ascontiguousarray(np.concatenate([r["lvpart"] for r in r1], axis=1)).astype(np.float32)
    r2 = _run8(build_inproj(), host_inproj_inputs(inputs, xnfm_list))
    r3 = _run8(build_convssm(), host_convssm_inputs(inputs, [r["xbc"] for r in r2],
                                                    [r["dtfm"] for r in r2]))
    l3o = [{**r3[c], "gsilu": r2[c]["gsilu"]} for c in range(NCORE)]
    r4 = _run8(build_ssm2(), host_ssm2_inputs(inputs, l3o, [r["shand"] for r in r3]))
    r5 = _run8(build_ffn0(), host_ffn0_inputs(inputs, xnfm_list, lvrow_list, lvparts))
    r6 = _run8(build_res0(), host_res0_inputs(inputs, x_full,
                                              [r4[c]["s0fm"] for c in range(NCORE)],
                                              [r5[c]["g0fm"] for c in range(NCORE)]))
    qkv_all = np.ascontiguousarray(np.concatenate([r6[c]["qkvfm"] for c in range(NCORE)], axis=2))
    r7 = _run8(build_attn(), host_attn_inputs(qkv_all))
    attn_full = np.stack([r7[h % 8]["attnfm"][h // 8] for h in range(32)])
    attn_list = [np.ascontiguousarray(attn_full[:, :, c * T:(c + 1) * T]) for c in range(NCORE)]
    r8 = _run8(build_wo(), host_wo_inputs(inputs, attn_list,
                                          [r6[c]["g1fm"] for c in range(NCORE)],
                                          [r6[c]["x1"] for c in range(NCORE)]))
    out = np.concatenate([np.asarray(r8[c]["out"]) for c in range(NCORE)], axis=0)
    return np.ascontiguousarray(out).reshape(4, 4096, DM).astype(np.float32, copy=False)


def kernel(**inputs):
    return kernel_mega(**inputs)
